# version 15
# baseline (speedup 1.0000x reference)
import numpy as np
from contextlib import ExitStack

import concourse.bass as bass
import concourse.bass_utils as _bass_utils
import concourse.mybir as mybir
from concourse.bass_utils import run_bass_kernel_spmd

if not getattr(_bass_utils, "_ldw_opt_patched", False):
    _orig_run_command = _bass_utils.run_command

    def _run_command_ldw(cmd, **kw):
        if isinstance(cmd, list):
            cmd = ["--enable-ldw-opt=true" if c == "--enable-ldw-opt=false" else c
                   for c in cmd]
        return _orig_run_command(cmd, **kw)

    _bass_utils.run_command = _run_command_ldw
    _bass_utils._ldw_opt_patched = True

F32 = mybir.dt.float32
BF16 = mybir.dt.bfloat16
F8 = mybir.dt.float8e4
AF = mybir.ActivationFunctionType
ALU = mybir.AluOpType
AX = mybir.AxisListType

NCORE = 8
T = 2048
D = 2048
I = 6144
NIC = I // 128
H = 16
HD = 128
CH = 128
TS = 512
NSC = T // TS
AS = 512
NAS = T // AS
NCHS = AS // CH
IBW = 256
NIB = I // IBW
EBW = 512
NEB = D // EBW
QBW = 512
NQB = D // QBW
EPS = 1e-5

ENGINES = ("sync", "gpsimd", "scalar", "vector", "tensor")
STAGES = ("ffn1", "attn", "ffn2")


class Sem:
    def __init__(self, nc, es, name):
        self.h = es.enter_context(nc.semaphore(name))
        self.v = 0

    def bump(self, n=1):
        self.v += n
        return self.v


class Plan:
    def __init__(self):
        self.ev = {e: [] for e in ENGINES}
        self.last_wait = {}
        self.sems = []

    def on(self, eng, fn):
        self.ev[eng].append(fn)

    def wait(self, eng, sem, val=None):
        v = sem.v if val is None else val
        if v <= 0:
            return
        key = (eng, id(sem))
        if self.last_wait.get(key, -1) >= v:
            return
        self.last_wait[key] = v
        self.on(eng, lambda e, s=sem.h, v=v: e.wait_ge(s, v))

    def join(self, engines=ENGINES):
        for eng in engines:
            for sem in self.sems:
                self.wait(eng, sem)

    def dma(self, eng, sem, out, in_):
        self.wait(eng, sem)
        self.on(eng, lambda e, o=out, i=in_: e.dma_start(out=o, in_=i).then_inc(sem.h, 16))
        return sem.bump(16)

    def dma_slot(self, eng, sem, out, in_):
        self.on(eng, lambda e, o=out, i=in_: e.dma_start(out=o, in_=i).then_inc(sem.h, 16))
        return sem.bump(16)

    def wait_marker(self, eng, marker):
        for sem, val in marker:
            self.wait(eng, sem, val)

    def drain(self, eng):
        self.on(eng, lambda e: e.drain())

    def fence(self, eng, sem):
        self.on(eng, lambda e: e.drain())
        self.on(eng, lambda e, s=sem.h: e.nop().then_inc(s, 1))
        return sem.bump()

    def emit(self, block):
        for eng in ENGINES:
            evs = self.ev[eng]

            def body(e, evs=evs):
                for fn in evs:
                    fn(e)

            getattr(block, eng)(body)


def build_nc(stages=("ffn1", "attn", "ffn2")):
    nc = bass.Bass("TRN2", target_bir_lowering=False)

    x_in = nc.dram_tensor("x", [T, D], F32, kind="ExternalInput").ap()
    gw_in = nc.dram_tensor("gw", [D, I], F32, kind="ExternalInput").ap()
    uw_in = nc.dram_tensor("uw", [D, I], F32, kind="ExternalInput").ap()
    dw_in = nc.dram_tensor("dw", [I, D], F32, kind="ExternalInput").ap()
    wq_in = nc.dram_tensor("wq", [D, D], F32, kind="ExternalInput").ap()
    wk_in = nc.dram_tensor("wk", [D, D], F32, kind="ExternalInput").ap()
    wv_in = nc.dram_tensor("wv", [D, D], F32, kind="ExternalInput").ap()
    wo_in = nc.dram_tensor("wo", [D, D], F32, kind="ExternalInput").ap()
    wg1_in = nc.dram_tensor("wg1", [D, HD], F32, kind="ExternalInput").ap()
    wg2_in = nc.dram_tensor("wg2", [HD, D], F32, kind="ExternalInput").ap()
    ld_b_in = nc.dram_tensor("ld_b", [128, H], F32, kind="ExternalInput").ap()
    gs_in = nc.dram_tensor("gs_pp", [128, NIC], F32, kind="ExternalInput").ap()
    us_in = nc.dram_tensor("us_pp", [128, NIC], F32, kind="ExternalInput").ap()
    ds_in = nc.dram_tensor("ds_b", [128, D], F32, kind="ExternalInput").ap()
    nw_in = nc.dram_tensor("nw_pp", [128, H], F32, kind="ExternalInput").ap()
    pw_in = nc.dram_tensor("pw_b", [128, NCORE], F32, kind="ExternalInput").ap()
    mk_in = nc.dram_tensor("mk_b", [128, NCORE], F32, kind="ExternalInput").ap()
    y_out = nc.dram_tensor("y", [T, D], F32, kind="ExternalOutput").ap()

    x_cur = nc.dram_tensor("x_cur", [T, D], F32).ap()
    gw_s = nc.dram_tensor("gw_s", [NIB, 128, 16 * IBW], F8).ap()
    uw_s = nc.dram_tensor("uw_s", [NIB, 128, 16 * IBW], F8).ap()
    dw_s = nc.dram_tensor("dw_s", [NEB * 6, 128, 8 * EBW], F8).ap()
    wq_s = nc.dram_tensor("wq_s", [NQB, 128, 16 * QBW], BF16).ap()
    wk_s = nc.dram_tensor("wk_s", [NQB, 128, 16 * QBW], BF16).ap()
    wv_s = nc.dram_tensor("wv_s", [NQB, 128, 16 * QBW], BF16).ap()
    wo_s = nc.dram_tensor("wo_s", [NQB, 128, 16 * QBW], BF16).ap()
    sig_sp = nc.dram_tensor("sig_sp", [NAS, 128, H * AS], BF16).ap()
    q_sp = nc.dram_tensor("q_sp", [NAS, 128, H * AS], BF16).ap()
    o_sp = nc.dram_tensor("o_sp", [NAS, 128, H * AS], BF16).ap()
    kv_b_in = nc.dram_tensor("kv_b_in", [128, H * HD], F32)
    kv_b_out = nc.dram_tensor("kv_b_out", [NCORE * 128, H * HD], F32)

    es = ExitStack()
    with es:
        ident = es.enter_context(nc.sbuf_tensor("ident", [128, 128], BF16))

        P = Plan()
        mksem = lambda name: Sem(nc, es, name)
        sq_w = mksem("sq_w")
        gq_x = mksem("gq_x")
        gq_o = mksem("gq_o")
        gq_c = mksem("gq_c")
        s_cc = mksem("s_cc")
        t_mm = mksem("t_mm")
        s_sc = mksem("s_sc")
        s_v = mksem("s_v")
        s_g = mksem("s_g")
        cvs = [mksem(f"cvs{i}") for i in range(3)]
        cvq = mksem("cvq")
        wgs = [mksem(f"wgs{i}") for i in range(2)]
        dws = [mksem(f"dws{i}") for i in range(2)]
        sxl = [mksem(f"sxl{i}") for i in range(2)]
        xas = [mksem(f"xas{i}") for i in range(2)]
        xts = [mksem(f"xts{i}") for i in range(2)]
        aqw = [mksem(f"aqw{i}") for i in range(2)]
        P.sems = [sq_w, gq_x, gq_o, gq_c, s_cc, t_mm, s_sc, s_v, s_g, cvq] + cvs + wgs + dws + sxl + xas + xts + aqw

        conv_jobs = []
        for ib in range(NIB):
            conv_jobs.append((gw_s[ib].rearrange("p (a w) -> p a w", a=16),
                              gw_in[:, ib * IBW:(ib + 1) * IBW].rearrange("(a p) w -> p a w", p=128)))
            conv_jobs.append((uw_s[ib].rearrange("p (a w) -> p a w", a=16),
                              uw_in[:, ib * IBW:(ib + 1) * IBW].rearrange("(a p) w -> p a w", p=128)))
        for eb in range(NEB):
            for ig in range(6):
                conv_jobs.append((dw_s[eb * 6 + ig].rearrange("p (a w) -> p a w", a=8),
                                  dw_in[ig * 1024:(ig + 1) * 1024, eb * EBW:(eb + 1) * EBW]
                                  .rearrange("(a p) w -> p a w", p=128)))
        N_FFN_CONV = len(conv_jobs)
        for w_src, w_dst in ((wq_in, wq_s), (wk_in, wk_s), (wv_in, wv_s), (wo_in, wo_s)):
            for eb in range(NQB):
                conv_jobs.append((w_dst[eb].rearrange("p (a w) -> p a w", a=16),
                                  w_src[:, eb * QBW:(eb + 1) * QBW].rearrange("(a p) w -> p a w", p=128)))
        conv_pos = [0]

        def conv_issue(n):
            for _ in range(n):
                j = conv_pos[0]
                if j >= len(conv_jobs):
                    return
                sem = cvs[j % 3] if j < N_FFN_CONV else cvq
                dst, src = conv_jobs[j]
                P.dma_slot("gpsimd", sem, dst, src)
                conv_pos[0] += 1

        def conv_wait(eng):
            for s in cvs:
                P.wait(eng, s)

        def prologue():
            def gp(e):
                e.memset(ident[:, :], 0.0)
                e.drain()
                e.affine_select(out=ident[:, :], in_=ident[:, :], compare_op=ALU.not_equal,
                                fill=1.0, base=0, pattern=[[-1, 128]], channel_multiplier=1)
                e.drain()
            P.on("gpsimd", gp)
            P.fence("gpsimd", s_g)
            conv_issue(4)

        ln_ctr = [0]
        sv_hist = {}

        ffn_id = [0]

        def ffn(src, dst, join=True):
            if join:
                P.join()
            tag = f"f{ffn_id[0]}"
            ffn_id[0] += 1
            fes = ExitStack()
            sb = lambda n, s, d: fes.enter_context(nc.sbuf_tensor(tag + n, s, d))
            pt = lambda n, s, d: fes.enter_context(nc.psum_tensor(tag + n, s, d))
            gs_pp = sb("gs_pp_t", [128, NIC], F32)
            us_pp = sb("us_pp_t", [128, NIC], F32)
            ds_b = sb("ds_b_t", [128, D], F32)
            xln = [sb(f"xln{i}", [128, D], F32) for i in range(2)]
            ln_tm = [sb(f"ln_tm{i}", [128, D], BF16) for i in range(2)]
            st2 = [{n: sb(f"st{i}_" + n, [128, 1], F32)
                    for n in ("sum", "ssq", "mean", "nmean", "tv", "vpe", "sqv", "inv", "mi")}
                   for i in range(2)]
            ln_fm = sb("ln_fm", [128, 16 * TS], BF16)
            fuse = sb("fuse", [128, NIC * TS], BF16)
            gwb = [sb(f"gwb{i}", [128, 16 * IBW], BF16) for i in range(2)]
            uwb = [sb(f"uwb{i}", [128, 16 * IBW], BF16) for i in range(2)]
            dwb = [sb(f"dwb{i}", [128, 8 * EBW], BF16) for i in range(2)]
            gact = sb("gact", [128, TS], BF16)
            xadd = [sb(f"xadd{i}", [128, 4 * EBW], F32) for i in range(2)]
            xtmp = [sb(f"xtmp{i}", [128, 4 * EBW], F32) for i in range(2)]
            ptr = [pt(f"ptr{i}", [128, 8 * 128], BF16) for i in range(2)]
            pg = pt("pg", [128, 512], F32)
            pu = pt("pu", [128, 512], F32)
            pd = [pt(f"pd{t}", [128, 512], F32) for t in range(4)]

            for dstb, srca in ((gs_pp, gs_in), (us_pp, us_in), (ds_b, ds_in)):
                P.dma("sync", sq_w, dstb[:, :], srca)
            scales_ready = sq_w.v

            ln_fm3 = ln_fm[:, :].rearrange("p (a b) -> p a b", a=16)

            def ln_tile(row0, bi, out_tm):
                key = ln_ctr[0]
                ln_ctr[0] += 1
                st = st2[bi]
                xb = xln[bi]
                prev = sv_hist.get(key - 2)
                if prev:
                    P.wait("scalar", s_v, prev)
                P.dma_slot("scalar", sxl[bi], xb[:, :], src[row0:row0 + 128, :])
                P.wait("vector", sxl[bi])
                P.on("vector", lambda e, xb=xb: e.reduce_sum(
                    st["sum"][:, :], xb[:, :], axis=AX.X).then_inc(s_v.h, 1))
                s_v.bump()
                P.wait("scalar", sxl[bi])
                P.on("scalar", lambda e, xb=xb, o=out_tm: e.activation(
                    o[:, :], xb[:, :], AF.Square, accum_out=st["ssq"][:, :]))
                P.drain("scalar")
                P.wait("scalar", s_v)
                P.on("scalar", lambda e: e.activation(st["mean"][:, :], st["sum"][:, :], AF.Copy, scale=1.0 / D))
                P.on("scalar", lambda e: e.activation(st["nmean"][:, :], st["sum"][:, :], AF.Copy, scale=-1.0 / D))
                P.on("scalar", lambda e: e.activation(st["tv"][:, :], st["ssq"][:, :], AF.Copy,
                                                      scale=1.0 / D, bias=EPS))
                P.fence("scalar", s_sc)
                P.wait("vector", s_sc)
                P.on("vector", lambda e: e.scalar_tensor_tensor(
                    st["vpe"][:, :], st["nmean"][:, :], st["mean"][:, 0:1], st["tv"][:, :],
                    ALU.mult, ALU.add).then_inc(s_v.h, 1))
                s_v.bump()
                P.wait("scalar", s_v)
                P.on("scalar", lambda e: e.activation(
                    st["sqv"][:, :], st["vpe"][:, :], AF.Sqrt).then_inc(s_sc.h, 1))
                s_sc.bump()
                P.wait("vector", s_sc)
                P.on("vector", lambda e: e.reciprocal(st["inv"][:, :], st["sqv"][:, :]))
                P.drain("vector")
                P.on("vector", lambda e: e.tensor_scalar(
                    st["mi"][:, :], st["nmean"][:, :], st["inv"][:, 0:1], None, ALU.mult))
                P.drain("vector")
                P.on("vector", lambda e, xb=xb, o=out_tm: e.tensor_scalar(
                    o[:, :], xb[:, :], st["inv"][:, 0:1], st["mi"][:, 0:1],
                    ALU.mult, ALU.add).then_inc(s_v.h, 1))
                s_v.bump()
                sv_hist[key] = s_v.v

            def mm(out, lhsT, rhs, start, stop, inc=False):
                if inc:
                    P.on("tensor", lambda e, o=out, l=lhsT, r=rhs, a=start, b=stop:
                         e.matmul(o, l, r, start=a, stop=b).then_inc(t_mm.h, 1))
                    t_mm.bump()
                else:
                    P.on("tensor", lambda e, o=out, l=lhsT, r=rhs, a=start, b=stop:
                         e.matmul(o, l, r, start=a, stop=b))

            first_wload = [True]
            first_dload = [True]
            P.wait("tensor", s_g, 1)
            ln_ready_at = {}

            def ln_phase(sc, gate_done_val):
                for t in range(TS // 128):
                    ln_tile(sc * TS + t * 128, t % 2, ln_tm[t % 2])
                    P.wait("tensor", s_v)
                    for half in range(2):
                        ptb = ptr[half]
                        P.wait("tensor", s_sc)
                        for dci in range(8):
                            dc = half * 8 + dci
                            if dci < 7:
                                P.on("tensor", lambda e, ptb=ptb, b=ln_tm[t % 2], dc=dc, dci=dci:
                                     e.transpose(ptb[:, dci * 128:(dci + 1) * 128],
                                                 b[:, dc * 128:(dc + 1) * 128], ident[:, :]))
                            else:
                                P.on("tensor", lambda e, ptb=ptb, b=ln_tm[t % 2], dc=dc, dci=dci:
                                     e.transpose(ptb[:, dci * 128:(dci + 1) * 128],
                                                 b[:, dc * 128:(dc + 1) * 128],
                                                 ident[:, :]).then_inc(t_mm.h, 1))
                                t_mm.bump()
                        P.wait("scalar", t_mm)
                        if gate_done_val:
                            P.wait("scalar", t_mm, gate_done_val)
                        P.on("scalar", lambda e, ptb=ptb, half=half, t=t: e.copy(
                            ln_fm3[:, half * 8:(half + 1) * 8, t * 128:(t + 1) * 128],
                            ptb[:, :].rearrange("p (a b) -> p a b", a=8)).then_inc(s_sc.h, 1))
                        s_sc.bump()
                ln_ready_at[sc] = s_sc.v

            ln_phase(0, 0)
            hist = {}
            for sc in range(NSC):
                ln_ready = ln_ready_at[sc]

                for ib in range(NIB):
                    bi = ib % 2
                    gib = sc * NIB + ib
                    conv_wait("gpsimd")
                    if gib >= 2:
                        P.wait("gpsimd", t_mm, hist[("mm", gib - 2)])
                    P.dma_slot("gpsimd", wgs[bi], gwb[bi][:, :], gw_s[ib])
                    P.dma_slot("gpsimd", wgs[bi], uwb[bi][:, :], uw_s[ib])
                    conv_issue(3)
                    P.wait("tensor", wgs[bi])
                    P.wait("tensor", s_sc, ln_ready)
                    for i128 in range(IBW // 128):
                        ic = ib * (IBW // 128) + i128
                        gic = sc * NIC + ic
                        pga = pg if ic % 2 == 0 else pd[0]
                        pua = pu if ic % 2 == 0 else pd[1]
                        if gic >= 2:
                            P.wait("tensor", s_sc, hist[("g", gic - 2)])
                            P.wait("tensor", s_v, hist[("u", gic - 2)])
                        if ic == 1 and ("xcons", 1) in hist:
                            P.wait("tensor", s_v, hist[("xcons", 1)])
                        for dc in range(16):
                            mm(pga[:, :],
                               gwb[bi][:, dc * IBW + i128 * 128: dc * IBW + (i128 + 1) * 128],
                               ln_fm[:, dc * TS:(dc + 1) * TS], dc == 0, dc == 15)
                        for dc in range(16):
                            mm(pua[:, :],
                               uwb[bi][:, dc * IBW + i128 * 128: dc * IBW + (i128 + 1) * 128],
                               ln_fm[:, dc * TS:(dc + 1) * TS], dc == 0, dc == 15,
                               inc=(dc == 15))
                        hist[("mm", gib)] = t_mm.v
                        P.wait("scalar", t_mm)
                        P.wait("scalar", sq_w, scales_ready)
                        if gic >= 1:
                            P.wait("scalar", s_v, hist[("u", gic - 1)])
                        P.on("scalar", lambda e, ic=ic, pga=pga: e.activation(
                            gact[:, :], pga[:, :], AF.Silu,
                            scale=gs_pp[:, ic:ic + 1]).then_inc(s_sc.h, 1))
                        s_sc.bump()
                        hist[("g", gic)] = s_sc.v
                        P.wait("vector", s_sc)
                        P.on("vector", lambda e, ic=ic, pua=pua: e.scalar_tensor_tensor(
                            fuse[:, ic * TS:(ic + 1) * TS], pua[:, :], us_pp[:, ic:ic + 1],
                            gact[:, :], ALU.mult, ALU.mult).then_inc(s_v.h, 1))
                        s_v.bump()
                        hist[("u", gic)] = s_v.v
                fuse_ready = s_v.v
                gate_done = t_mm.v
                if sc + 1 < NSC:
                    ln_phase(sc + 1, gate_done)

                for eb in range(NEB):
                    if eb > 0:
                        P.wait("tensor", s_v)
                    abI = eb % 2
                    prev_cons = hist.get(("xcons", abI))
                    if prev_cons:
                        P.wait("sync", s_v, prev_cons)
                    P.dma_slot("sync", xas[abI], xadd[abI][:, :].rearrange("p (a b) -> p a b", a=4),
                               src[sc * TS:(sc + 1) * TS,
                                   eb * EBW:(eb + 1) * EBW].rearrange("(t p) w -> p t w", p=128))
                    for ig in range(6):
                        bi = ig % 2
                        gdc = sc * NEB * 6 + eb * 6 + ig
                        conv_wait("gpsimd")
                        if gdc >= 2:
                            P.wait("gpsimd", t_mm, hist[("dmm", gdc - 2)])
                        P.dma_slot("gpsimd", dws[bi], dwb[bi][:, :], dw_s[eb * 6 + ig])
                        conv_issue(1)
                        P.wait("tensor", dws[bi])
                        P.wait("tensor", s_v, fuse_ready)
                        for i8 in range(8):
                            ic = ig * 8 + i8
                            for t in range(4):
                                mm(pd[t][:, :],
                                   fuse[:, ic * TS + t * 128: ic * TS + (t + 1) * 128],
                                   dwb[bi][:, i8 * EBW:(i8 + 1) * EBW],
                                   ic == 0, ic == NIC - 1,
                                   inc=(i8 == 7 and t == 3))
                        hist[("dmm", gdc)] = t_mm.v
                    P.wait("vector", xts[abI])
                    P.wait("vector", xas[abI])
                    P.wait("vector", t_mm)
                    for t in range(4):
                        P.on("vector", lambda e, abI=abI, t=t, eb=eb: e.scalar_tensor_tensor(
                            xtmp[abI][:, t * EBW:(t + 1) * EBW], pd[t][:, :], 0.5,
                            ds_b[:, eb * EBW:(eb + 1) * EBW], ALU.mult, ALU.mult))
                    P.drain("vector")
                    for t in range(4):
                        P.on("vector", lambda e, abI=abI, t=t: e.tensor_add(
                            xtmp[abI][:, t * EBW:(t + 1) * EBW], xtmp[abI][:, t * EBW:(t + 1) * EBW],
                            xadd[abI][:, t * EBW:(t + 1) * EBW]))
                    P.fence("vector", s_v)
                    hist[("xcons", abI)] = s_v.v
                    P.wait("sync", s_v, hist[("xcons", abI)])
                    P.dma_slot("sync", xts[abI],
                               dst[sc * TS:(sc + 1) * TS,
                                   eb * EBW:(eb + 1) * EBW].rearrange("(t p) w -> p t w", p=128),
                               xtmp[abI][:, :].rearrange("p (a b) -> p a b", a=4))
            fes.close()

        def attn(src, dst):
            P.join()
            aes = ExitStack()
            asb = lambda n, s, d: aes.enter_context(nc.sbuf_tensor("a" + n, s, d))
            ldb = asb("ldb", [128, H], F32)
            kdt = asb("kdt", [128, H], F32)
            cdb = asb("cdb", [128, H], F32)
            pw_b = asb("pw_b", [128, NCORE], F32)
            mk_b = asb("mk_b", [128, NCORE], F32)
            Wb = asb("Wb", [128, H * NCORE], F32)
            qdt = asb("qdt", [128, H * 128], F32)
            S = asb("S", [128, H * HD], F32)
            S_bf = asb("S_bf", [128, H * HD], BF16)
            Bst = asb("Bst", [128, H * HD], F32)
            nw_pp = asb("nw_pp", [128, H], F32)
            wg1_bf = asb("wg1_bf", [128, H * 128], BF16)
            wg2_bf = asb("wg2_bf", [128, D], BF16)
            dsc1 = asb("dsc1", [128, 128], F32)
            dsc2 = asb("dsc2", [128, 128], F32)
            cdp = asb("cdp", [128, NAS * NCHS * H], F32)
            ones_bf = asb("ones_bf", [128, 128], BF16)

            def mm(out, lhsT, rhs, start, stop, inc=False):
                if inc:
                    P.on("tensor", lambda e, o=out, l=lhsT, r=rhs, a=start, b=stop:
                         e.matmul(o, l, r, start=a, stop=b).then_inc(t_mm.h, 1))
                    t_mm.bump()
                else:
                    P.on("tensor", lambda e, o=out, l=lhsT, r=rhs, a=start, b=stop:
                         e.matmul(o, l, r, start=a, stop=b))

            for dstb, srca in ((ldb, ld_b_in), (pw_b, pw_in), (mk_b, mk_in), (nw_pp, nw_in)):
                P.dma("sync", sq_w, dstb[:, :], srca)
            ses = ExitStack()
            wg1f = ses.enter_context(nc.sbuf_tensor("awg1f", [128, H * 128], F32))
            wg2f = ses.enter_context(nc.sbuf_tensor("awg2f", [128, D], F32))
            P.dma("sync", sq_w, wg1f[:, :].rearrange("p (a b) -> p a b", a=16),
                  wg1_in.rearrange("(a p) b -> p a b", p=128))
            P.dma("sync", sq_w, wg2f[:, :], wg2_in)

            def gp_masks(e):
                e.iota(dsc1[:, :], [[1, 128]], base=0, channel_multiplier=-1,
                       allow_small_or_imprecise_dtypes=True)
                e.drain()
                e.iota(dsc2[:, :], [[1, 128]], base=1, channel_multiplier=0,
                       allow_small_or_imprecise_dtypes=True)
                e.drain()
                e.memset(S[:, :], 0.0)
                e.memset(S_bf[:, :], 0.0)
                e.memset(Bst[:, :], 0.0)
                e.memset(cdp[:, 0:H], 1.0)
                e.memset(ones_bf[:, :], 1.0)
            P.on("gpsimd", gp_masks)
            P.fence("gpsimd", s_g)
            P.wait("scalar", s_g)
            P.wait("scalar", sq_w)
            P.on("scalar", lambda e: e.copy(wg1_bf[:, :], wg1f[:, :]))
            P.on("scalar", lambda e: e.copy(wg2_bf[:, :], wg2f[:, :]))
            P.on("scalar", lambda e: e.activation(cdb[:, :], ldb[:, :], AF.Exp, scale=128.0))
            P.on("scalar", lambda e: e.activation(dsc1[:, :], dsc1[:, :], AF.Relu))
            P.drain("scalar")
            for h in range(H):
                P.on("scalar", lambda e, h=h: e.activation(
                    qdt[:, h * 128:(h + 1) * 128], dsc2[:, :], AF.Exp, scale=ldb[:, h:h + 1]))
            for h in range(H):
                P.on("scalar", lambda e, h=h: e.activation(
                    Wb[:, h * NCORE:(h + 1) * NCORE], pw_b[:, :], AF.Exp, scale=ldb[:, h:h + 1]))
            P.drain("scalar")
            P.fence("scalar", s_sc)
            P.wait("vector", s_sc)
            for h in range(H):
                P.on("vector", lambda e, h=h: e.tensor_mul(
                    Wb[:, h * NCORE:(h + 1) * NCORE], Wb[:, h * NCORE:(h + 1) * NCORE], mk_b[:, :]))
            P.wait("vector", s_g)
            for g in range(1, NAS * NCHS):
                P.on("vector", lambda e, g=g: e.tensor_mul(
                    cdp[:, g * H:(g + 1) * H], cdp[:, (g - 1) * H:g * H], cdb[:, :]))
                P.drain("vector")
            P.fence("vector", s_v)
            P.wait("gpsimd", s_sc)
            P.on("gpsimd", lambda e: e.iota(dsc2[:, 0:1], [[0, 1]], base=127, channel_multiplier=-1,
                                            allow_small_or_imprecise_dtypes=True))
            P.fence("gpsimd", s_g)
            P.wait("scalar", s_g)
            for h in range(H):
                P.on("scalar", lambda e, h=h: e.activation(
                    kdt[:, h:h + 1], dsc2[:, 0:1], AF.Exp, scale=ldb[:, h:h + 1]))
            P.drain("scalar")
            P.fence("scalar", s_sc)
            ses.close()

            b1 = ExitStack()
            bsb = lambda n, s, d: b1.enter_context(nc.sbuf_tensor("b1" + n, s, d))
            bpt = lambda n, s, d: b1.enter_context(nc.psum_tensor("b1" + n, s, d))
            DT = bsb("DT", [128, H * 128], BF16)
            xln = [bsb(f"xln{i}", [128, D], F32) for i in range(2)]
            ln_tm = [bsb(f"ln_tm{i}", [128, D], BF16) for i in range(2)]
            st2 = [{n: bsb(f"st{i}_" + n, [128, 1], F32)
                    for n in ("sum", "ssq", "mean", "nmean", "tv", "vpe", "sqv", "inv", "mi")}
                   for i in range(2)]
            ln2_fm = bsb("ln2_fm", [128, 16 * AS], BF16)
            q_fm = bsb("q_fm", [128, H * AS], BF16)
            k_fm = bsb("k_fm", [128, H * AS], BF16)
            v_tm = bsb("v_tm", [128, NCHS * D], BF16)
            ktm = bsb("ktm", [128, NCHS * H * 128], BF16)
            o_fm = bsb("o_fm", [128, H * AS], BF16)
            sigh = [bsb(f"sigh{i}", [128, 4 * AS], BF16) for i in range(2)]
            g1b = bsb("g1b", [128, AS], BF16)
            q2b = [bsb(f"q2b{i}", [128, 128], BF16) for i in range(2)]
            scd = [bsb(f"scd{i}", [128, 128], BF16) for i in range(2)]
            wqb2 = [bsb(f"wqb{i}", [128, 16 * QBW], BF16) for i in range(2)]
            pban = [bpt(f"pban{i}", [128, 512], F32) for i in range(6)]
            ptrb = [bpt(f"ptrb{i}", [128, 8 * 128], BF16) for i in range(2)]

            ln2_fm3 = ln2_fm[:, :].rearrange("p (a b) -> p a b", a=16)

            def ln_tile(row0, bi, out_tm):
                key = ln_ctr[0]
                ln_ctr[0] += 1
                st = st2[bi]
                xb = xln[bi]
                prev = sv_hist.get(key - 2)
                if prev:
                    P.wait("scalar", s_v, prev)
                P.dma_slot("scalar", sxl[bi], xb[:, :], src[row0:row0 + 128, :])
                P.wait("vector", sxl[bi])
                P.on("vector", lambda e, xb=xb: e.reduce_sum(
                    st["sum"][:, :], xb[:, :], axis=AX.X).then_inc(s_v.h, 1))
                s_v.bump()
                P.wait("scalar", sxl[bi])
                P.on("scalar", lambda e, xb=xb, o=out_tm: e.activation(
                    o[:, :], xb[:, :], AF.Square, accum_out=st["ssq"][:, :]))
                P.drain("scalar")
                P.wait("scalar", s_v)
                P.on("scalar", lambda e: e.activation(st["mean"][:, :], st["sum"][:, :], AF.Copy, scale=1.0 / D))
                P.on("scalar", lambda e: e.activation(st["nmean"][:, :], st["sum"][:, :], AF.Copy, scale=-1.0 / D))
                P.on("scalar", lambda e: e.activation(st["tv"][:, :], st["ssq"][:, :], AF.Copy,
                                                      scale=1.0 / D, bias=EPS))
                P.drain("scalar")
                P.fence("scalar", s_sc)
                P.wait("vector", s_sc)
                P.on("vector", lambda e: e.scalar_tensor_tensor(
                    st["vpe"][:, :], st["nmean"][:, :], st["mean"][:, 0:1], st["tv"][:, :],
                    ALU.mult, ALU.add).then_inc(s_v.h, 1))
                s_v.bump()
                P.wait("scalar", s_v)
                P.on("scalar", lambda e: e.activation(
                    st["sqv"][:, :], st["vpe"][:, :], AF.Sqrt).then_inc(s_sc.h, 1))
                s_sc.bump()
                P.wait("vector", s_sc)
                P.on("vector", lambda e: e.reciprocal(st["inv"][:, :], st["sqv"][:, :]))
                P.drain("vector")
                P.on("vector", lambda e: e.tensor_scalar(
                    st["mi"][:, :], st["nmean"][:, :], st["inv"][:, 0:1], None, ALU.mult))
                P.drain("vector")
                P.on("vector", lambda e, xb=xb, o=out_tm: e.tensor_scalar(
                    o[:, :], xb[:, :], st["inv"][:, 0:1], st["mi"][:, 0:1],
                    ALU.mult, ALU.add).then_inc(s_v.h, 1))
                s_v.bump()
                sv_hist[key] = s_v.v

            P.wait("scalar", s_sc)
            for h in range(H):
                P.on("scalar", lambda e, h=h: e.activation(
                    DT[:, h * 128:(h + 1) * 128], dsc1[:, :], AF.Exp, scale=ldb[:, h:h + 1]))
            P.drain("scalar")
            P.fence("scalar", s_sc)
            P.wait("gpsimd", s_sc)

            def gp_dtmask(e):
                for h in range(H):
                    e.affine_select(out=DT[:, h * 128:(h + 1) * 128],
                                    in_=DT[:, h * 128:(h + 1) * 128],
                                    compare_op=ALU.is_ge, fill=0.0, base=0,
                                    pattern=[[1, 128]], channel_multiplier=-1)
                e.drain()
            P.on("gpsimd", gp_dtmask)
            P.fence("gpsimd", s_g)

            hist = {}
            pctr = [0]
            vctr = [0]
            wctr2 = [0]
            for sc in range(NAS):
                for t in range(NCHS):
                    ln_tile(sc * AS + t * 128, t % 2, ln_tm[t % 2])
                    P.wait("tensor", s_v)
                    for half in range(2):
                        ptb = ptrb[half]
                        P.wait("tensor", s_sc)
                        for dci in range(8):
                            dc = half * 8 + dci
                            mmfn = (lambda e, ptb=ptb, b=ln_tm[t % 2], dc=dc, dci=dci:
                                    e.transpose(ptb[:, dci * 128:(dci + 1) * 128],
                                                b[:, dc * 128:(dc + 1) * 128], ident[:, :]))
                            if dci < 7:
                                P.on("tensor", mmfn)
                            else:
                                P.on("tensor", lambda e, f=mmfn: f(e).then_inc(t_mm.h, 1))
                                t_mm.bump()
                        P.wait("scalar", t_mm)
                        P.on("scalar", lambda e, ptb=ptb, half=half, t=t: e.copy(
                            ln2_fm3[:, half * 8:(half + 1) * 8, t * 128:(t + 1) * 128],
                            ptb[:, :].rearrange("p (a b) -> p a b", a=8)).then_inc(s_sc.h, 1))
                        s_sc.bump()
                ln2_ready = s_sc.v

                if ("g1r", sc - 1) in hist:
                    P.wait("tensor", s_sc, hist[("g1r", sc - 1)])
                if ("qsp", sc - 1) in hist:
                    P.wait("scalar", gq_o, hist[("qsp", sc - 1)])
                def proj_fm(w_s, out_buf, act, nm):
                    for eb in range(NQB):
                        wb_i = (wctr2[0] % 2)
                        wctr2[0] += 1
                        wqb = wqb2[wb_i]
                        if ("wq", wctr2[0] - 3) in hist:
                            P.wait("sync", t_mm, hist[("wq", wctr2[0] - 3)])
                        else:
                            P.wait("sync", t_mm)
                        P.dma_slot("sync", aqw[wb_i], wqb[:, :], w_s[eb])
                        P.wait("tensor", aqw[wb_i])
                        P.wait("tensor", s_sc, ln2_ready)
                        for e128 in range(4):
                            ec = eb * 4 + e128
                            n = pctr[0]
                            pctr[0] += 1
                            pa = pban[n % 2]
                            if ("proj", n - 2) in hist:
                                P.wait("tensor", s_sc, hist[("proj", n - 2)])
                            for dc in range(16):
                                mm(pa[:, :], wqb[:, dc * QBW + e128 * 128: dc * QBW + (e128 + 1) * 128],
                                   ln2_fm[:, dc * AS:(dc + 1) * AS], dc == 0, dc == 15,
                                   inc=(dc == 15))
                            P.wait("scalar", t_mm)
                            P.on("scalar", lambda e, pa=pa, ec=ec, o=out_buf, act=act: e.activation(
                                o[:, ec * AS:(ec + 1) * AS], pa[:, :], act).then_inc(s_sc.h, 1))
                            s_sc.bump()
                            hist[("proj", n)] = s_sc.v
                        hist[("wq", wctr2[0] - 1)] = t_mm.v
                proj_fm(wq_s, q_fm, AF.Silu, "wq")
                proj_fm(wk_s, k_fm, AF.Silu, "wk")
                q_ready = s_sc.v
                P.wait("gpsimd", s_sc, q_ready)
                P.dma("gpsimd", gq_o, q_sp[sc], q_fm[:, :])
                hist[("qsp", sc)] = gq_o.v

                if ("sgev", sc * H - 1) in hist:
                    P.wait("tensor", s_sc, hist[("sgev", sc * H - 1)])
                for eb in range(NQB):
                    wb_i = (wctr2[0] % 2)
                    wctr2[0] += 1
                    wqb = wqb2[wb_i]
                    if ("wq", wctr2[0] - 3) in hist:
                        P.wait("sync", t_mm, hist[("wq", wctr2[0] - 3)])
                    else:
                        P.wait("sync", t_mm)
                    P.dma_slot("sync", aqw[wb_i], wqb[:, :], wv_s[eb])
                    P.wait("tensor", aqw[wb_i])
                    for t in range(NCHS):
                        nv = vctr[0]
                        vctr[0] += 1
                        pa = pban[2 + (nv % 2)]
                        if ("vp", nv - 2) in hist:
                            P.wait("tensor", s_sc, hist[("vp", nv - 2)])
                        for dc in range(16):
                            mm(pa[:, :], ln2_fm[:, dc * AS + t * 128: dc * AS + (t + 1) * 128],
                               wqb[:, dc * QBW:(dc + 1) * QBW], dc == 0, dc == 15, inc=(dc == 15))
                        P.wait("scalar", t_mm)
                        P.on("scalar", lambda e, pa=pa, t=t, eb=eb: e.activation(
                            v_tm[:, t * D + eb * QBW: t * D + (eb + 1) * QBW], pa[:, :],
                            AF.Silu).then_inc(s_sc.h, 1))
                        s_sc.bump()
                        hist[("vp", nv)] = s_sc.v
                    hist[("wq", wctr2[0] - 1)] = t_mm.v
                v_ready = s_sc.v

                kq_ready = s_sc.v
                for c in range(NCHS):
                    for half in range(2):
                        ptb = ptrb[half]
                        P.wait("tensor", s_sc)
                        for hh in range(8):
                            h = half * 8 + hh
                            f = (lambda e, ptb=ptb, hh=hh, h=h, c=c:
                                 e.transpose(ptb[:, hh * 128:(hh + 1) * 128],
                                             k_fm[:, h * AS + c * 128: h * AS + (c + 1) * 128],
                                             ident[:, :]))
                            if hh < 7:
                                P.on("tensor", f)
                            else:
                                P.on("tensor", lambda e, f=f: f(e).then_inc(t_mm.h, 1))
                                t_mm.bump()
                        P.wait("scalar", t_mm)
                        for hh in range(8):
                            h = half * 8 + hh
                            last = hh == 7
                            fn = (lambda e, ptb=ptb, hh=hh, h=h, c=c: e.activation(
                                ktm[:, (c * H + h) * 128:(c * H + h + 1) * 128],
                                ptb[:, hh * 128:(hh + 1) * 128], AF.Copy,
                                scale=kdt[:, h:h + 1]))
                            if last:
                                P.on("scalar", lambda e, f=fn: f(e).then_inc(s_sc.h, 1))
                                s_sc.bump()
                            else:
                                P.on("scalar", fn)
                ktm_ready = s_sc.v

                P.wait("tensor", s_g)
                P.wait("vector", s_g)
                sitems = [(c, h) for c in range(NCHS) for h in range(H)]
                base = sc * NCHS * H

                def emit_score(idx):
                    c, h = sitems[idx]
                    i = base + idx
                    sl = i % 2
                    if ("scd", i - 2) in hist:
                        P.wait("tensor", s_v, hist[("scd", i - 2)])
                    mm(pban[sl][:, 0:128], k_fm[:, h * AS + c * 128: h * AS + (c + 1) * 128],
                       q_fm[:, h * AS + c * 128: h * AS + (c + 1) * 128], True, True, inc=True)
                    hist[("sc", i)] = t_mm.v

                emit_score(0)
                for idx, (c, h) in enumerate(sitems):
                    i = base + idx
                    sl = i % 2
                    if idx == 0 and ("osp", sc - 1) in hist:
                        P.wait("scalar", gq_o, hist[("osp", sc - 1)])
                    P.wait("vector", t_mm, hist[("sc", i)])
                    if ("omm", i - 2) in hist:
                        P.wait("vector", t_mm, hist[("omm", i - 2)])
                    P.on("vector", lambda e, sl=sl, h=h: e.tensor_mul(
                        scd[sl][:, :], pban[sl][:, 0:128], DT[:, h * 128:(h + 1) * 128]))
                    P.on("vector", lambda e, sl=sl, h=h, c=c: e.tensor_mul(
                        q2b[sl][:, :], q_fm[:, h * AS + c * 128: h * AS + (c + 1) * 128],
                        qdt[:, h * 128:(h + 1) * 128]).then_inc(s_v.h, 1))
                    s_v.bump()
                    hist[("scd", i)] = s_v.v
                    if idx + 1 < len(sitems):
                        emit_score(idx + 1)
                    P.wait("tensor", s_v, hist[("scd", i)])
                    if ("oev", i - 2) in hist:
                        P.wait("tensor", s_sc, hist[("oev", i - 2)])
                    mm(pban[2 + sl][:, 0:128],
                       v_tm[:, c * D + h * 128: c * D + (h + 1) * 128], scd[sl][:, :],
                       True, False)
                    mm(pban[2 + sl][:, 0:128], S_bf[:, h * 128:(h + 1) * 128], q2b[sl][:, :],
                       False, True, inc=True)
                    hist[("omm", i)] = t_mm.v
                    P.wait("scalar", t_mm, hist[("omm", i)])
                    P.on("scalar", lambda e, sl=sl, h=h, c=c: e.activation(
                        o_fm[:, h * AS + c * 128: h * AS + (c + 1) * 128],
                        pban[2 + sl][:, 0:128], AF.Copy).then_inc(s_sc.h, 1))
                    s_sc.bump()
                    hist[("oev", i)] = s_sc.v
                    if ("sup", i - 2) in hist:
                        P.wait("tensor", s_v, hist[("sup", i - 2)])
                    mm(pban[4 + sl][:, 0:128], ktm[:, (c * H + h) * 128:(c * H + h + 1) * 128],
                       v_tm[:, c * D + h * 128: c * D + (h + 1) * 128], True, True, inc=True)
                    hist[("kv", i)] = t_mm.v
                    P.wait("vector", t_mm, hist[("kv", i)])
                    P.on("vector", lambda e, sl=sl, h=h: e.scalar_tensor_tensor(
                        S[:, h * 128:(h + 1) * 128], S[:, h * 128:(h + 1) * 128],
                        cdb[:, h:h + 1], pban[4 + sl][:, 0:128], ALU.mult, ALU.add))
                    P.drain("vector")
                    P.on("vector", lambda e, h=h: e.tensor_copy(
                        S_bf[:, h * 128:(h + 1) * 128],
                        S[:, h * 128:(h + 1) * 128]).then_inc(s_v.h, 1))
                    s_v.bump()
                    hist[("sup", i)] = s_v.v
                last_i = base + len(sitems) - 1
                P.wait("gpsimd", s_sc, hist[("oev", last_i)])
                P.dma("gpsimd", gq_o, o_sp[sc], o_fm[:, :])
                hist[("osp", sc)] = gq_o.v

                P.wait("tensor", s_v, hist[("scd", last_i)])
                for dc in range(16):
                    mm(pban[0][:, :], wg1_bf[:, dc * 128:(dc + 1) * 128],
                       ln2_fm[:, dc * AS:(dc + 1) * AS], dc == 0, dc == 15, inc=(dc == 15))
                g1v = t_mm.v
                P.wait("scalar", t_mm, g1v)
                P.on("scalar", lambda e: e.copy(g1b[:, :], pban[0][:, :]).then_inc(s_sc.h, 1))
                s_sc.bump()
                hist[("g1r", sc)] = s_sc.v
                P.wait("tensor", s_sc, hist[("g1r", sc)])
                P.wait("tensor", s_sc, hist[("oev", last_i)])
                for h in range(H):
                    shi = (h // 4) % 2
                    pa = pban[2 + (h % 2)]
                    if ("sgev", sc * H + h - 2) in hist:
                        P.wait("tensor", s_sc, hist[("sgev", sc * H + h - 2)])
                    mm(pa[:, :], wg2_bf[:, h * 128:(h + 1) * 128], g1b[:, :],
                       True, True, inc=True)
                    P.wait("scalar", t_mm)
                    if h % 4 == 0 and ("sigsp", sc * H + h - 5) in hist:
                        P.wait("scalar", gq_o, hist[("sigsp", sc * H + h - 5)])
                    P.on("scalar", lambda e, pa=pa, shi=shi, h=h: e.activation(
                        sigh[shi][:, (h % 4) * AS:(h % 4 + 1) * AS], pa[:, :],
                        AF.Sigmoid).then_inc(s_sc.h, 1))
                    s_sc.bump()
                    hist[("sgev", sc * H + h)] = s_sc.v
                    if h % 4 == 3:
                        P.wait("gpsimd", s_sc, hist[("sgev", sc * H + h)])
                        P.dma("gpsimd", gq_o, sig_sp[sc][:, (h - 3) * AS:(h + 1) * AS],
                              sigh[shi][:, :])
                        hist[("sigsp", sc * H + h)] = gq_o.v
            b1.close()

            P.wait("gpsimd", s_v)
            P.dma("gpsimd", gq_c, kv_b_in.ap(), S[:, :])
            P.wait("gpsimd", gq_c)
            P.on("gpsimd", lambda e: e.collective_compute(
                "AllGather", ALU.bypass, replica_groups=[list(range(NCORE))],
                ins=[kv_b_in.ap()], outs=[kv_b_out.ap()]).then_inc(s_cc.h, 1))
            s_cc.bump(1)

            def b2_reduce():
                P.wait("gpsimd", s_cc)
                P.wait("gpsimd", t_mm)
                b2 = ExitStack()
                kvg = b2.enter_context(nc.sbuf_tensor("kvg", [128, H * HD], F32))
                for cp in range(NCORE):
                    P.dma("gpsimd", gq_c, kvg[:, :], kv_b_out.ap()[cp * 128:(cp + 1) * 128, :])
                    P.wait("vector", gq_c)
                    for h in range(H):
                        P.on("vector", lambda e, h=h, cp=cp: e.scalar_tensor_tensor(
                            Bst[:, h * 128:(h + 1) * 128], kvg[:, h * 128:(h + 1) * 128],
                            Wb[:, h * NCORE + cp: h * NCORE + cp + 1],
                            Bst[:, h * 128:(h + 1) * 128], ALU.mult, ALU.add))
                    P.drain("vector")
                    P.fence("vector", s_v)
                    P.wait("gpsimd", s_v)
                b2.close()
                P.on("vector", lambda e: e.tensor_copy(S_bf[:, :], Bst[:, :]).then_inc(s_v.h, 1))
                s_v.bump()
                return s_v.v

            b3 = ExitStack()
            csb = lambda n, s, d: b3.enter_context(nc.sbuf_tensor("b3" + n, s, d))
            cpt = lambda n, s, d: b3.enter_context(nc.psum_tensor("b3" + n, s, d))
            qr = csb("qr", [128, H * AS], BF16)
            obr = csb("obr", [128, H * AS], BF16)
            sigr = csb("sigr", [128, H * AS], BF16)
            qdt4 = csb("qdt4", [128, H * AS], BF16)
            ogn_fm = csb("ogn", [128, H * AS], BF16)
            og_h = [csb(f"og{i}", [128, AS], F32) for i in range(4)]
            sqh = [csb(f"sq{i}", [128, AS], BF16) for i in range(2)]
            q2bu = [csb(f"q2u{i}", [128, AS], BF16) for i in range(2)]
            rrh = [csb(f"rr{i}", [1, AS], F32) for i in range(2)]
            rrb = [csb(f"rb{i}", [1, AS], BF16) for i in range(2)]
            rrf = [csb(f"rf{i}", [128, AS], BF16) for i in range(2)]
            wob2 = [csb("wob0", [128, 16 * QBW], BF16)]
            xadd = [csb(f"xadd{i}", [128, 4 * QBW], F32) for i in range(2)]
            xtmp = [csb(f"xtmp{i}", [128, 4 * QBW], F32) for i in range(2)]
            psC = [cpt(f"psC{i}", [128, AS], F32) for i in range(2)]
            psR = [cpt(f"psR{i}", [1, AS], F32) for i in range(2)]
            psB = [cpt(f"psB{i}", [128, AS], F32) for i in range(2)]
            pb3 = [cpt(f"pb3{i}", [128, 512], F32) for i in range(2)]

            hist = {}
            wctr = [0]
            P.wait("gpsimd", gq_o)

            for eb in range(NQB):
                P.wait("sync", gq_o)
                P.dma("sync", aqw[0], wob2[0][:, :], wo_s[eb])
                P.wait("scalar", aqw[0])
                for dc in range(16):
                    fn = (lambda e, dc=dc: e.activation(
                        wob2[0][:, dc * QBW:(dc + 1) * QBW],
                        wob2[0][:, dc * QBW:(dc + 1) * QBW], AF.Copy,
                        scale=nw_pp[:, dc:dc + 1]))
                    if dc == 15:
                        P.on("scalar", lambda e, f=fn: f(e).then_inc(s_sc.h, 1))
                        s_sc.bump()
                    else:
                        P.on("scalar", fn)
                P.wait("gpsimd", s_sc)
                P.dma("gpsimd", gq_o, wo_s[eb], wob2[0][:, :])
            wo_scaled = gq_o.v

            b0_ready = b2_reduce()
            for sc in range(NAS):
                pgd = hist.get(("rel", sc - 1))
                if pgd:
                    P.wait("gpsimd", t_mm, pgd[0])
                    P.wait("gpsimd", s_v, pgd[1])
                P.dma("gpsimd", gq_x, qr[:, :], q_sp[sc])
                P.dma("gpsimd", gq_x, obr[:, :], o_sp[sc])
                P.dma("gpsimd", gq_x, sigr[:, :], sig_sp[sc])
                rel_ready = gq_x.v

                if ("q2", sc * H - 1) in hist:
                    P.wait("scalar", s_g, hist[("q2", sc * H - 1)])
                for h in range(H):
                    for c in range(NCHS):
                        g = sc * NCHS + c
                        last = (h == H - 1 and c == NCHS - 1)
                        fn = (lambda e, h=h, c=c, g=g: e.activation(
                            qdt4[:, h * AS + c * 128: h * AS + (c + 1) * 128],
                            qdt[:, h * 128:(h + 1) * 128], AF.Copy,
                            scale=cdp[:, g * H + h: g * H + h + 1]))
                        if last:
                            P.on("scalar", lambda e, f=fn: f(e).then_inc(s_sc.h, 1))
                            s_sc.bump()
                        else:
                            P.on("scalar", fn)
                qdt4_ready = s_sc.v

                for h in range(H):
                    sl = h % 2
                    sl4 = h % 4
                    i = sc * H + h
                    P.wait("gpsimd", gq_x, rel_ready)
                    P.wait("gpsimd", s_sc, qdt4_ready)
                    if ("cmm", i - 2) in hist:
                        P.wait("gpsimd", t_mm, hist[("cmm", i - 2)])
                    P.on("gpsimd", lambda e, sl=sl, h=h: e.tensor_mul(
                        q2bu[sl][:, :], qr[:, h * AS:(h + 1) * AS],
                        qdt4[:, h * AS:(h + 1) * AS]).then_inc(s_g.h, 1))
                    s_g.bump()
                    hist[("q2", i)] = s_g.v
                    P.wait("tensor", s_g, hist[("q2", i)])
                    P.wait("tensor", s_v, b0_ready)
                    P.wait("tensor", gq_x, rel_ready)
                    if ("ogm", i - 2) in hist:
                        P.wait("tensor", s_v, hist[("ogm", i - 2)])
                    mm(psC[sl][:, :], S_bf[:, h * 128:(h + 1) * 128], q2bu[sl][:, :],
                       True, False)
                    mm(psC[sl][:, :], ident[:, :], obr[:, h * AS:(h + 1) * AS],
                       False, True, inc=True)
                    hist[("cmm", i)] = t_mm.v
                    P.wait("vector", t_mm, hist[("cmm", i)])
                    P.wait("vector", gq_x, rel_ready)
                    if ("ogn", i - 4) in hist:
                        P.wait("vector", s_g, hist[("ogn", i - 4)])
                    P.on("vector", lambda e, sl4=sl4, sl=sl, h=h: e.tensor_mul(
                        og_h[sl4][:, :], psC[sl][:, :],
                        sigr[:, h * AS:(h + 1) * AS]).then_inc(s_v.h, 1))
                    s_v.bump()
                    hist[("ogm", i)] = s_v.v
                    P.wait("gpsimd", s_v, hist[("ogm", i)])
                    if ("csum", i - 2) in hist:
                        P.wait("gpsimd", t_mm, hist[("csum", i - 2)])
                    P.on("gpsimd", lambda e, sl=sl, sl4=sl4: e.tensor_mul(
                        sqh[sl][:, :], og_h[sl4][:, :], og_h[sl4][:, :]).then_inc(s_g.h, 1))
                    s_g.bump()
                    hist[("sqv", i)] = s_g.v
                    P.wait("tensor", s_g, hist[("sqv", i)])
                    if ("rme", i - 2) in hist:
                        P.wait("tensor", s_v, hist[("rme", i - 2)])
                    mm(psR[sl][0:1, :], ones_bf[:, 0:1], sqh[sl][:, :], True, True, inc=True)
                    hist[("csum", i)] = t_mm.v
                    P.wait("vector", t_mm, hist[("csum", i)])
                    if ("rrc", i - 2) in hist:
                        P.wait("vector", s_sc, hist[("rrc", i - 2)])
                    P.on("vector", lambda e, sl=sl: e.tensor_scalar(
                        rrh[sl][0:1, :], psR[sl][0:1, :], 1.0 / HD, EPS,
                        ALU.mult, ALU.add).then_inc(s_v.h, 1))
                    s_v.bump()
                    hist[("rme", i)] = s_v.v
                    P.wait("scalar", s_v, hist[("rme", i)])
                    if ("bmm", i - 2) in hist:
                        P.wait("scalar", t_mm, hist[("bmm", i - 2)])
                    def _sq(e, sl=sl):
                        with nc.allow_low_precision(reason="rms as bf16 matmul operand"):
                            e.activation(rrb[sl][0:1, :], rrh[sl][0:1, :],
                                         AF.Sqrt).then_inc(s_sc.h, 1)
                    P.on("scalar", _sq)
                    s_sc.bump()
                    hist[("rrc", i)] = s_sc.v
                    P.wait("tensor", s_sc, hist[("rrc", i)])
                    if ("rcp", i - 2) in hist:
                        P.wait("tensor", s_v, hist[("rcp", i - 2)])
                    mm(psB[sl][:, :], ones_bf[0:1, 0:128], rrb[sl][0:1, :], True, True, inc=True)
                    hist[("bmm", i)] = t_mm.v
                    P.wait("vector", t_mm, hist[("bmm", i)])
                    if ("ogn", i - 2) in hist:
                        P.wait("vector", s_g, hist[("ogn", i - 2)])
                    def _rcp(e, sl=sl):
                        with nc.allow_low_precision(reason="1/rms factor in bf16"):
                            e.reciprocal(rrf[sl][:, :], psB[sl][:, :]).then_inc(s_v.h, 1)
                    P.on("vector", _rcp)
                    s_v.bump()
                    hist[("rcp", i)] = s_v.v
                    P.wait("gpsimd", s_v, hist[("rcp", i)])
                    if ("wolast", sc - 1) in hist and h == 0:
                        P.wait("gpsimd", t_mm, hist[("wolast", sc - 1)])
                    P.on("gpsimd", lambda e, sl4=sl4, sl=sl, h=h: e.tensor_mul(
                        ogn_fm[:, h * AS:(h + 1) * AS], og_h[sl4][:, :],
                        rrf[sl][:, :]).then_inc(s_g.h, 1))
                    s_g.bump()
                    hist[("ogn", i)] = s_g.v
                hist[("rel", sc)] = (hist[("cmm", sc * H + H - 1)],
                                     hist[("ogm", sc * H + H - 1)])
                ognT_ready = hist[("ogn", sc * H + H - 1)]

                for eb in range(NQB):
                    wb_i = 0
                    P.wait("sync", gq_o, wo_scaled)
                    P.wait("sync", t_mm)
                    P.dma_slot("sync", aqw[wb_i], wob2[wb_i][:, :], wo_s[eb])
                    abI = eb % 2
                    prev_cons = hist.get(("b3xcons", abI))
                    if prev_cons:
                        P.wait("sync", s_v, prev_cons)
                    P.dma_slot("sync", xas[abI], xadd[abI][:, :].rearrange("p (a b) -> p a b", a=4),
                               src[sc * AS:(sc + 1) * AS,
                                   eb * QBW:(eb + 1) * QBW].rearrange("(t p) w -> p t w", p=128))
                    P.wait("tensor", aqw[wb_i])
                    P.wait("tensor", s_g, ognT_ready)
                    wob = wob2[wb_i]
                    for c in range(NCHS):
                        nw_ = wctr[0]
                        wctr[0] += 1
                        pa = pb3[nw_ % 2]
                        if ("wo", nw_ - 2) in hist:
                            P.wait("tensor", s_v, hist[("wo", nw_ - 2)])
                        for dc in range(16):
                            mm(pa[:, :], ogn_fm[:, dc * AS + c * 128: dc * AS + (c + 1) * 128],
                               wob[:, dc * QBW:(dc + 1) * QBW], dc == 0, dc == 15, inc=(dc == 15))
                        womm = t_mm.v
                        P.wait("vector", t_mm, womm)
                        P.wait("vector", xas[abI])
                        P.wait("vector", xts[abI])
                        P.on("vector", lambda e, pa=pa, abI=abI, c=c: e.tensor_add(
                            xtmp[abI][:, c * QBW:(c + 1) * QBW], pa[:, :],
                            xadd[abI][:, c * QBW:(c + 1) * QBW]).then_inc(s_v.h, 1))
                        s_v.bump()
                        hist[("wo", nw_)] = s_v.v
                    hist[("b3xcons", abI)] = s_v.v
                    P.wait("sync", s_v, hist[("b3xcons", abI)])
                    P.dma_slot("sync", xts[abI],
                               dst[sc * AS:(sc + 1) * AS,
                                   eb * QBW:(eb + 1) * QBW].rearrange("(t p) w -> p t w", p=128),
                               xtmp[abI][:, :].rearrange("p (a b) -> p a b", a=4))
                hist[("wolast", sc)] = t_mm.v
            b3.close()
            aes.close()

        prologue()
        have_attn = "attn" in stages
        have_f2 = "ffn2" in stages
        if "ffn1" in stages:
            ffn(x_in, x_cur if (have_attn or have_f2) else y_out, join=False)
        conv_issue(len(conv_jobs))
        if have_attn:
            attn(x_cur if "ffn1" in stages else x_in, x_cur if have_f2 else y_out)
        if have_f2:
            ffn(x_cur if ("ffn1" in stages or have_attn) else x_in, y_out)

        for sem in P.sems:
            P.wait("sync", sem)

        with nc.Block() as block:
            P.emit(block)

    return nc


def kernel(L0_attn_x, L0_attn_log_decay, L0_attn_Wq, L0_attn_Wk, L0_attn_Wv,
           L0_attn_Wo, L0_attn_Wg1, L0_attn_Wg2, L0_attn_norm_w,
           L0_ffn_gate_w, L0_ffn_up_w, L0_ffn_down_w,
           L0_ffn_gate_scale, L0_ffn_up_scale, L0_ffn_down_scale,
           _stages=None):
    if _stages is None:
        _stages = STAGES
    x = np.ascontiguousarray(np.asarray(L0_attn_x, dtype=np.float32).reshape(NCORE * T, D))
    ld = np.asarray(L0_attn_log_decay, dtype=np.float32)
    asc = np.ascontiguousarray

    common = {
        "gw": asc(np.asarray(L0_ffn_gate_w, np.float32)),
        "uw": asc(np.asarray(L0_ffn_up_w, np.float32)),
        "dw": asc(np.asarray(L0_ffn_down_w, np.float32)),
        "wq": asc(np.asarray(L0_attn_Wq, np.float32)),
        "wk": asc(np.asarray(L0_attn_Wk, np.float32)),
        "wv": asc(np.asarray(L0_attn_Wv, np.float32)),
        "wo": asc(np.asarray(L0_attn_Wo, np.float32)),
        "wg1": asc(np.asarray(L0_attn_Wg1, np.float32)),
        "wg2": asc(np.asarray(L0_attn_Wg2, np.float32)),
        "ld_b": asc(np.broadcast_to(ld.reshape(1, H), (128, H))),
        "gs_pp": asc(np.asarray(L0_ffn_gate_scale, np.float32).reshape(NIC, 128).T),
        "us_pp": asc(np.asarray(L0_ffn_up_scale, np.float32).reshape(NIC, 128).T),
        "ds_b": asc(np.broadcast_to(np.asarray(L0_ffn_down_scale, np.float32).reshape(1, D), (128, D))),
        "nw_pp": asc(np.asarray(L0_attn_norm_w, np.float32).reshape(H, 128).T),
    }
    in_maps = []
    for c in range(NCORE):
        pw = np.zeros(NCORE, np.float32)
        mk = np.zeros(NCORE, np.float32)
        for cp in range(c):
            pw[cp] = 2048.0 * (c - 1 - cp)
            mk[cp] = 1.0
        m = dict(common)
        m["x"] = asc(x[c * T:(c + 1) * T, :])
        m["pw_b"] = asc(np.broadcast_to(pw.reshape(1, NCORE), (128, NCORE)))
        m["mk_b"] = asc(np.broadcast_to(mk.reshape(1, NCORE), (128, NCORE)))
        in_maps.append(m)

    nc = build_nc(_stages)
    res = run_bass_kernel_spmd(nc, in_maps, core_ids=list(range(NCORE)))
    out = np.concatenate([res.results[c]["y"] for c in range(NCORE)], axis=0)
    return out.reshape(1, NCORE * T, D).astype(np.float32)


# version 16
# speedup vs baseline: 1.0617x; 1.0617x over previous
import numpy as np
from contextlib import ExitStack

import concourse.bass as bass
import concourse.bass_utils as _bass_utils
import concourse.mybir as mybir
from concourse.bass_utils import run_bass_kernel_spmd

if not getattr(_bass_utils, "_ldw_opt_patched", False):
    _orig_run_command = _bass_utils.run_command

    def _run_command_ldw(cmd, **kw):
        if isinstance(cmd, list):
            cmd = ["--enable-ldw-opt=true" if c == "--enable-ldw-opt=false" else c
                   for c in cmd]
        return _orig_run_command(cmd, **kw)

    _bass_utils.run_command = _run_command_ldw
    _bass_utils._ldw_opt_patched = True

F32 = mybir.dt.float32
BF16 = mybir.dt.bfloat16
F8 = mybir.dt.float8e4
AF = mybir.ActivationFunctionType
ALU = mybir.AluOpType
AX = mybir.AxisListType

NCORE = 8
T = 2048
D = 2048
I = 6144
NIC = I // 128
H = 16
HD = 128
CH = 128
TS = 512
NSC = T // TS
AS = 512
NAS = T // AS
NCHS = AS // CH
IBW = 256
NIB = I // IBW
EBW = 512
NEB = D // EBW
QBW = 512
NQB = D // QBW
EPS = 1e-5

ENGINES = ("sync", "gpsimd", "scalar", "vector", "tensor")
STAGES = ("ffn1", "attn", "ffn2")


class Sem:
    def __init__(self, nc, es, name):
        self.h = es.enter_context(nc.semaphore(name))
        self.v = 0

    def bump(self, n=1):
        self.v += n
        return self.v


class Plan:
    def __init__(self):
        self.ev = {e: [] for e in ENGINES}
        self.last_wait = {}
        self.sems = []

    def on(self, eng, fn):
        self.ev[eng].append(fn)

    def wait(self, eng, sem, val=None):
        v = sem.v if val is None else val
        if v <= 0:
            return
        key = (eng, id(sem))
        if self.last_wait.get(key, -1) >= v:
            return
        self.last_wait[key] = v
        self.on(eng, lambda e, s=sem.h, v=v: e.wait_ge(s, v))

    def join(self, engines=ENGINES):
        for eng in engines:
            for sem in self.sems:
                self.wait(eng, sem)

    def dma(self, eng, sem, out, in_):
        self.wait(eng, sem)
        self.on(eng, lambda e, o=out, i=in_: e.dma_start(out=o, in_=i).then_inc(sem.h, 16))
        return sem.bump(16)

    def dma_slot(self, eng, sem, out, in_):
        self.on(eng, lambda e, o=out, i=in_: e.dma_start(out=o, in_=i).then_inc(sem.h, 16))
        return sem.bump(16)

    def wait_marker(self, eng, marker):
        for sem, val in marker:
            self.wait(eng, sem, val)

    def drain(self, eng):
        self.on(eng, lambda e: e.drain())

    def fence(self, eng, sem):
        self.on(eng, lambda e: e.drain())
        self.on(eng, lambda e, s=sem.h: e.nop().then_inc(s, 1))
        return sem.bump()

    def emit(self, block):
        for eng in ENGINES:
            evs = self.ev[eng]

            def body(e, evs=evs):
                for fn in evs:
                    fn(e)

            getattr(block, eng)(body)


def build_nc(stages=("ffn1", "attn", "ffn2")):
    nc = bass.Bass("TRN2", target_bir_lowering=False)

    x_in = nc.dram_tensor("x", [T, D], F32, kind="ExternalInput").ap()
    gw_in = nc.dram_tensor("gw", [D, I], F32, kind="ExternalInput").ap()
    uw_in = nc.dram_tensor("uw", [D, I], F32, kind="ExternalInput").ap()
    dw_in = nc.dram_tensor("dw", [I, D], F32, kind="ExternalInput").ap()
    wq_in = nc.dram_tensor("wq", [D, D], F32, kind="ExternalInput").ap()
    wk_in = nc.dram_tensor("wk", [D, D], F32, kind="ExternalInput").ap()
    wv_in = nc.dram_tensor("wv", [D, D], F32, kind="ExternalInput").ap()
    wo_in = nc.dram_tensor("wo", [D, D], F32, kind="ExternalInput").ap()
    wg1_in = nc.dram_tensor("wg1", [D, HD], F32, kind="ExternalInput").ap()
    wg2_in = nc.dram_tensor("wg2", [HD, D], F32, kind="ExternalInput").ap()
    ld_b_in = nc.dram_tensor("ld_b", [128, H], F32, kind="ExternalInput").ap()
    gs_in = nc.dram_tensor("gs_pp", [128, NIC], F32, kind="ExternalInput").ap()
    us_in = nc.dram_tensor("us_pp", [128, NIC], F32, kind="ExternalInput").ap()
    ds_in = nc.dram_tensor("ds_b", [128, D], F32, kind="ExternalInput").ap()
    nw_in = nc.dram_tensor("nw_pp", [128, H], F32, kind="ExternalInput").ap()
    pw_in = nc.dram_tensor("pw_b", [128, NCORE], F32, kind="ExternalInput").ap()
    mk_in = nc.dram_tensor("mk_b", [128, NCORE], F32, kind="ExternalInput").ap()
    y_out = nc.dram_tensor("y", [T, D], F32, kind="ExternalOutput").ap()

    x_cur = nc.dram_tensor("x_cur", [T, D], F32).ap()
    gw_s = nc.dram_tensor("gw_s", [NIB, 128, 16 * IBW], F8).ap()
    uw_s = nc.dram_tensor("uw_s", [NIB, 128, 16 * IBW], F8).ap()
    dw_s = nc.dram_tensor("dw_s", [NEB * 6, 128, 8 * EBW], F8).ap()
    wq_s = nc.dram_tensor("wq_s", [NQB, 128, 16 * QBW], BF16).ap()
    wk_s = nc.dram_tensor("wk_s", [NQB, 128, 16 * QBW], BF16).ap()
    wv_s = nc.dram_tensor("wv_s", [NQB, 128, 16 * QBW], BF16).ap()
    wo_s = nc.dram_tensor("wo_s", [NQB, 128, 16 * QBW], BF16).ap()
    sig_sp = nc.dram_tensor("sig_sp", [NAS, 128, H * AS], BF16).ap()
    q_sp = nc.dram_tensor("q_sp", [NAS, 128, H * AS], BF16).ap()
    o_sp = nc.dram_tensor("o_sp", [NAS, 128, H * AS], BF16).ap()
    kv_b_in = nc.dram_tensor("kv_b_in", [128, H * HD], F32)
    kv_b_out = nc.dram_tensor("kv_b_out", [NCORE * 128, H * HD], F32)

    es = ExitStack()
    with es:
        ident = es.enter_context(nc.sbuf_tensor("ident", [128, 128], BF16))

        P = Plan()
        mksem = lambda name: Sem(nc, es, name)
        sq_w = mksem("sq_w")
        gq_x = mksem("gq_x")
        gq_o = mksem("gq_o")
        gq_c = mksem("gq_c")
        s_cc = mksem("s_cc")
        t_mm = mksem("t_mm")
        s_sc = mksem("s_sc")
        s_v = mksem("s_v")
        s_g = mksem("s_g")
        cvs = [mksem(f"cvs{i}") for i in range(3)]
        cvq = mksem("cvq")
        wgs = [mksem(f"wgs{i}") for i in range(2)]
        dws = [mksem(f"dws{i}") for i in range(2)]
        sxl = [mksem(f"sxl{i}") for i in range(2)]
        xas = [mksem(f"xas{i}") for i in range(2)]
        xts = [mksem(f"xts{i}") for i in range(2)]
        aqw = [mksem(f"aqw{i}") for i in range(2)]
        P.sems = [sq_w, gq_x, gq_o, gq_c, s_cc, t_mm, s_sc, s_v, s_g, cvq] + cvs + wgs + dws + sxl + xas + xts + aqw

        conv_jobs = []
        for ib in range(NIB):
            conv_jobs.append((gw_s[ib].rearrange("p (a w) -> p a w", a=16),
                              gw_in[:, ib * IBW:(ib + 1) * IBW].rearrange("(a p) w -> p a w", p=128)))
            conv_jobs.append((uw_s[ib].rearrange("p (a w) -> p a w", a=16),
                              uw_in[:, ib * IBW:(ib + 1) * IBW].rearrange("(a p) w -> p a w", p=128)))
        for eb in range(NEB):
            for ig in range(6):
                conv_jobs.append((dw_s[eb * 6 + ig].rearrange("p (a w) -> p a w", a=8),
                                  dw_in[ig * 1024:(ig + 1) * 1024, eb * EBW:(eb + 1) * EBW]
                                  .rearrange("(a p) w -> p a w", p=128)))
        N_FFN_CONV = len(conv_jobs)
        for w_src, w_dst in ((wq_in, wq_s), (wk_in, wk_s), (wv_in, wv_s), (wo_in, wo_s)):
            for eb in range(NQB):
                conv_jobs.append((w_dst[eb].rearrange("p (a w) -> p a w", a=16),
                                  w_src[:, eb * QBW:(eb + 1) * QBW].rearrange("(a p) w -> p a w", p=128)))
        conv_pos = [0]

        def conv_issue(n):
            for _ in range(n):
                j = conv_pos[0]
                if j >= len(conv_jobs):
                    return
                sem = cvs[j % 3] if j < N_FFN_CONV else cvq
                dst, src = conv_jobs[j]
                P.dma_slot("gpsimd", sem, dst, src)
                conv_pos[0] += 1

        def conv_wait(eng):
            for s in cvs:
                P.wait(eng, s)

        def prologue():
            def gp(e):
                e.memset(ident[:, :], 0.0)
                e.drain()
                e.affine_select(out=ident[:, :], in_=ident[:, :], compare_op=ALU.not_equal,
                                fill=1.0, base=0, pattern=[[-1, 128]], channel_multiplier=1)
                e.drain()
            P.on("gpsimd", gp)
            P.fence("gpsimd", s_g)
            conv_issue(4)

        ln_ctr = [0]
        sv_hist = {}

        ffn_id = [0]

        def ffn(src, dst, join=True):
            if join:
                P.join()
            tag = f"f{ffn_id[0]}"
            ffn_id[0] += 1
            fes = ExitStack()
            sb = lambda n, s, d: fes.enter_context(nc.sbuf_tensor(tag + n, s, d))
            pt = lambda n, s, d: fes.enter_context(nc.psum_tensor(tag + n, s, d))
            gs_pp = sb("gs_pp_t", [128, NIC], F32)
            us_pp = sb("us_pp_t", [128, NIC], F32)
            ds_b = sb("ds_b_t", [128, D], F32)
            xln = [sb(f"xln{i}", [128, D], F32) for i in range(2)]
            ln_tm = [sb(f"ln_tm{i}", [128, D], BF16) for i in range(2)]
            st2 = [{n: sb(f"st{i}_" + n, [128, 1], F32)
                    for n in ("sum", "ssq", "mean", "nmean", "tv", "vpe", "sqv", "inv", "mi")}
                   for i in range(2)]
            ln_fm = sb("ln_fm", [128, 16 * TS], BF16)
            fuse = sb("fuse", [128, NIC * TS], BF16)
            gwb = [sb(f"gwb{i}", [128, 16 * IBW], BF16) for i in range(2)]
            uwb = [sb(f"uwb{i}", [128, 16 * IBW], BF16) for i in range(2)]
            dwb = [sb(f"dwb{i}", [128, 8 * EBW], BF16) for i in range(2)]
            gact = sb("gact", [128, TS], BF16)
            xadd = [sb(f"xadd{i}", [128, 4 * EBW], F32) for i in range(2)]
            xtmp = [sb(f"xtmp{i}", [128, 4 * EBW], F32) for i in range(2)]
            ptr = [pt(f"ptr{i}", [128, 8 * 128], BF16) for i in range(2)]
            pg = pt("pg", [128, 512], F32)
            pu = pt("pu", [128, 512], F32)
            pd = [pt(f"pd{t}", [128, 512], F32) for t in range(4)]

            for dstb, srca in ((gs_pp, gs_in), (us_pp, us_in), (ds_b, ds_in)):
                P.dma("sync", sq_w, dstb[:, :], srca)
            scales_ready = sq_w.v

            ln_fm3 = ln_fm[:, :].rearrange("p (a b) -> p a b", a=16)

            def ln_tile(row0, bi, out_tm):
                key = ln_ctr[0]
                ln_ctr[0] += 1
                st = st2[bi]
                xb = xln[bi]
                prev = sv_hist.get(key - 2)
                if prev:
                    P.wait("scalar", s_v, prev)
                P.dma_slot("scalar", sxl[bi], xb[:, :], src[row0:row0 + 128, :])
                P.wait("vector", sxl[bi])
                P.on("vector", lambda e, xb=xb: e.reduce_sum(
                    st["sum"][:, :], xb[:, :], axis=AX.X).then_inc(s_v.h, 1))
                s_v.bump()
                P.wait("scalar", sxl[bi])
                P.on("scalar", lambda e, xb=xb, o=out_tm: e.activation(
                    o[:, :], xb[:, :], AF.Square, accum_out=st["ssq"][:, :]))
                P.drain("scalar")
                P.wait("scalar", s_v)
                P.on("scalar", lambda e: e.activation(st["mean"][:, :], st["sum"][:, :], AF.Copy, scale=1.0 / D))
                P.on("scalar", lambda e: e.activation(st["nmean"][:, :], st["sum"][:, :], AF.Copy, scale=-1.0 / D))
                P.on("scalar", lambda e: e.activation(st["tv"][:, :], st["ssq"][:, :], AF.Copy,
                                                      scale=1.0 / D, bias=EPS))
                P.fence("scalar", s_sc)
                P.wait("vector", s_sc)
                P.on("vector", lambda e: e.scalar_tensor_tensor(
                    st["vpe"][:, :], st["nmean"][:, :], st["mean"][:, 0:1], st["tv"][:, :],
                    ALU.mult, ALU.add).then_inc(s_v.h, 1))
                s_v.bump()
                P.wait("scalar", s_v)
                P.on("scalar", lambda e: e.activation(
                    st["sqv"][:, :], st["vpe"][:, :], AF.Sqrt).then_inc(s_sc.h, 1))
                s_sc.bump()
                P.wait("vector", s_sc)
                P.on("vector", lambda e: e.reciprocal(st["inv"][:, :], st["sqv"][:, :]))
                P.drain("vector")
                P.on("vector", lambda e: e.tensor_scalar(
                    st["mi"][:, :], st["nmean"][:, :], st["inv"][:, 0:1], None, ALU.mult))
                P.drain("vector")
                P.on("vector", lambda e, xb=xb, o=out_tm: e.tensor_scalar(
                    o[:, :], xb[:, :], st["inv"][:, 0:1], st["mi"][:, 0:1],
                    ALU.mult, ALU.add).then_inc(s_v.h, 1))
                s_v.bump()
                sv_hist[key] = s_v.v

            def mm(out, lhsT, rhs, start, stop, inc=False):
                if inc:
                    P.on("tensor", lambda e, o=out, l=lhsT, r=rhs, a=start, b=stop:
                         e.matmul(o, l, r, start=a, stop=b).then_inc(t_mm.h, 1))
                    t_mm.bump()
                else:
                    P.on("tensor", lambda e, o=out, l=lhsT, r=rhs, a=start, b=stop:
                         e.matmul(o, l, r, start=a, stop=b))

            first_wload = [True]
            first_dload = [True]
            P.wait("tensor", s_g, 1)
            ln_ready_at = {}

            def ln_phase(sc, gate_done_val):
                for t in range(TS // 128):
                    ln_tile(sc * TS + t * 128, t % 2, ln_tm[t % 2])
                    P.wait("tensor", s_v)
                    for half in range(2):
                        ptb = ptr[half]
                        P.wait("tensor", s_sc)
                        for dci in range(8):
                            dc = half * 8 + dci
                            if dci < 7:
                                P.on("tensor", lambda e, ptb=ptb, b=ln_tm[t % 2], dc=dc, dci=dci:
                                     e.transpose(ptb[:, dci * 128:(dci + 1) * 128],
                                                 b[:, dc * 128:(dc + 1) * 128], ident[:, :]))
                            else:
                                P.on("tensor", lambda e, ptb=ptb, b=ln_tm[t % 2], dc=dc, dci=dci:
                                     e.transpose(ptb[:, dci * 128:(dci + 1) * 128],
                                                 b[:, dc * 128:(dc + 1) * 128],
                                                 ident[:, :]).then_inc(t_mm.h, 1))
                                t_mm.bump()
                        P.wait("scalar", t_mm)
                        if gate_done_val:
                            P.wait("scalar", t_mm, gate_done_val)
                        P.on("scalar", lambda e, ptb=ptb, half=half, t=t: e.copy(
                            ln_fm3[:, half * 8:(half + 1) * 8, t * 128:(t + 1) * 128],
                            ptb[:, :].rearrange("p (a b) -> p a b", a=8)).then_inc(s_sc.h, 1))
                        s_sc.bump()
                ln_ready_at[sc] = s_sc.v

            ln_phase(0, 0)
            hist = {}
            for sc in range(NSC):
                ln_ready = ln_ready_at[sc]

                for ib in range(NIB):
                    bi = ib % 2
                    gib = sc * NIB + ib
                    conv_wait("gpsimd")
                    if gib >= 2:
                        P.wait("gpsimd", t_mm, hist[("mm", gib - 2)])
                    P.dma_slot("gpsimd", wgs[bi], gwb[bi][:, :], gw_s[ib])
                    P.dma_slot("gpsimd", wgs[bi], uwb[bi][:, :], uw_s[ib])
                    conv_issue(3)
                    P.wait("tensor", wgs[bi])
                    P.wait("tensor", s_sc, ln_ready)
                    for i128 in range(IBW // 128):
                        ic = ib * (IBW // 128) + i128
                        gic = sc * NIC + ic
                        pga = pg if ic % 2 == 0 else pd[0]
                        pua = pu if ic % 2 == 0 else pd[1]
                        if gic >= 2:
                            P.wait("tensor", s_sc, hist[("g", gic - 2)])
                            P.wait("tensor", s_v, hist[("u", gic - 2)])
                        if ic == 1 and ("xcons", 1) in hist:
                            P.wait("tensor", s_v, hist[("xcons", 1)])
                        for dc in range(16):
                            mm(pga[:, :],
                               gwb[bi][:, dc * IBW + i128 * 128: dc * IBW + (i128 + 1) * 128],
                               ln_fm[:, dc * TS:(dc + 1) * TS], dc == 0, dc == 15)
                        for dc in range(16):
                            mm(pua[:, :],
                               uwb[bi][:, dc * IBW + i128 * 128: dc * IBW + (i128 + 1) * 128],
                               ln_fm[:, dc * TS:(dc + 1) * TS], dc == 0, dc == 15,
                               inc=(dc == 15))
                        hist[("mm", gib)] = t_mm.v
                        P.wait("scalar", t_mm)
                        P.wait("scalar", sq_w, scales_ready)
                        if gic >= 1:
                            P.wait("scalar", s_v, hist[("u", gic - 1)])
                        P.on("scalar", lambda e, ic=ic, pga=pga: e.activation(
                            gact[:, :], pga[:, :], AF.Silu,
                            scale=gs_pp[:, ic:ic + 1]).then_inc(s_sc.h, 1))
                        s_sc.bump()
                        hist[("g", gic)] = s_sc.v
                        P.wait("vector", s_sc)
                        P.on("vector", lambda e, ic=ic, pua=pua: e.scalar_tensor_tensor(
                            fuse[:, ic * TS:(ic + 1) * TS], pua[:, :], us_pp[:, ic:ic + 1],
                            gact[:, :], ALU.mult, ALU.mult).then_inc(s_v.h, 1))
                        s_v.bump()
                        hist[("u", gic)] = s_v.v
                fuse_ready = s_v.v
                gate_done = t_mm.v
                if sc + 1 < NSC:
                    ln_phase(sc + 1, gate_done)

                for eb in range(NEB):
                    if eb > 0:
                        P.wait("tensor", s_v)
                    abI = eb % 2
                    prev_cons = hist.get(("xcons", abI))
                    if prev_cons:
                        P.wait("sync", s_v, prev_cons)
                    P.dma_slot("sync", xas[abI], xadd[abI][:, :].rearrange("p (a b) -> p a b", a=4),
                               src[sc * TS:(sc + 1) * TS,
                                   eb * EBW:(eb + 1) * EBW].rearrange("(t p) w -> p t w", p=128))
                    for ig in range(6):
                        bi = ig % 2
                        gdc = sc * NEB * 6 + eb * 6 + ig
                        conv_wait("gpsimd")
                        if gdc >= 2:
                            P.wait("gpsimd", t_mm, hist[("dmm", gdc - 2)])
                        P.dma_slot("gpsimd", dws[bi], dwb[bi][:, :], dw_s[eb * 6 + ig])
                        conv_issue(1)
                        P.wait("tensor", dws[bi])
                        P.wait("tensor", s_v, fuse_ready)
                        for i8 in range(8):
                            ic = ig * 8 + i8
                            for t in range(4):
                                mm(pd[t][:, :],
                                   fuse[:, ic * TS + t * 128: ic * TS + (t + 1) * 128],
                                   dwb[bi][:, i8 * EBW:(i8 + 1) * EBW],
                                   ic == 0, ic == NIC - 1,
                                   inc=(i8 == 7 and t == 3))
                        hist[("dmm", gdc)] = t_mm.v
                    P.wait("vector", xts[abI])
                    P.wait("vector", xas[abI])
                    P.wait("vector", t_mm)
                    for t in range(4):
                        P.on("vector", lambda e, abI=abI, t=t, eb=eb: e.scalar_tensor_tensor(
                            xtmp[abI][:, t * EBW:(t + 1) * EBW], pd[t][:, :], 0.5,
                            ds_b[:, eb * EBW:(eb + 1) * EBW], ALU.mult, ALU.mult))
                    P.drain("vector")
                    for t in range(4):
                        P.on("vector", lambda e, abI=abI, t=t: e.tensor_add(
                            xtmp[abI][:, t * EBW:(t + 1) * EBW], xtmp[abI][:, t * EBW:(t + 1) * EBW],
                            xadd[abI][:, t * EBW:(t + 1) * EBW]))
                    P.fence("vector", s_v)
                    hist[("xcons", abI)] = s_v.v
                    P.wait("sync", s_v, hist[("xcons", abI)])
                    P.dma_slot("sync", xts[abI],
                               dst[sc * TS:(sc + 1) * TS,
                                   eb * EBW:(eb + 1) * EBW].rearrange("(t p) w -> p t w", p=128),
                               xtmp[abI][:, :].rearrange("p (a b) -> p a b", a=4))
            fes.close()

        def attn(src, dst):
            P.join()
            aes = ExitStack()
            asb = lambda n, s, d: aes.enter_context(nc.sbuf_tensor("a" + n, s, d))
            ldb = asb("ldb", [128, H], F32)
            kdt = asb("kdt", [128, H], F32)
            cdb = asb("cdb", [128, H], F32)
            pw_b = asb("pw_b", [128, NCORE], F32)
            mk_b = asb("mk_b", [128, NCORE], F32)
            Wb = asb("Wb", [128, H * NCORE], F32)
            qdt = asb("qdt", [128, H * 128], F32)
            S = asb("S", [128, H * HD], F32)
            S_bf = asb("S_bf", [128, H * HD], BF16)
            Bst = asb("Bst", [128, H * HD], F32)
            nw_pp = asb("nw_pp", [128, H], F32)
            wg1_bf = asb("wg1_bf", [128, H * 128], BF16)
            wg2_bf = asb("wg2_bf", [128, D], BF16)
            dsc1 = asb("dsc1", [128, 128], F32)
            dsc2 = asb("dsc2", [128, 128], F32)
            cdp = asb("cdp", [128, NAS * NCHS * H], F32)
            ones_bf = asb("ones_bf", [128, 128], BF16)

            def mm(out, lhsT, rhs, start, stop, inc=False):
                if inc:
                    P.on("tensor", lambda e, o=out, l=lhsT, r=rhs, a=start, b=stop:
                         e.matmul(o, l, r, start=a, stop=b).then_inc(t_mm.h, 1))
                    t_mm.bump()
                else:
                    P.on("tensor", lambda e, o=out, l=lhsT, r=rhs, a=start, b=stop:
                         e.matmul(o, l, r, start=a, stop=b))

            for dstb, srca in ((ldb, ld_b_in), (pw_b, pw_in), (mk_b, mk_in), (nw_pp, nw_in)):
                P.dma("sync", sq_w, dstb[:, :], srca)
            ses = ExitStack()
            wg1f = ses.enter_context(nc.sbuf_tensor("awg1f", [128, H * 128], F32))
            wg2f = ses.enter_context(nc.sbuf_tensor("awg2f", [128, D], F32))
            P.dma("sync", sq_w, wg1f[:, :].rearrange("p (a b) -> p a b", a=16),
                  wg1_in.rearrange("(a p) b -> p a b", p=128))
            P.dma("sync", sq_w, wg2f[:, :], wg2_in)

            def gp_masks(e):
                e.iota(dsc1[:, :], [[1, 128]], base=0, channel_multiplier=-1,
                       allow_small_or_imprecise_dtypes=True)
                e.drain()
                e.iota(dsc2[:, :], [[1, 128]], base=1, channel_multiplier=0,
                       allow_small_or_imprecise_dtypes=True)
                e.drain()
                e.memset(S[:, :], 0.0)
                e.memset(S_bf[:, :], 0.0)
                e.memset(Bst[:, :], 0.0)
                e.memset(cdp[:, 0:H], 1.0)
                e.memset(ones_bf[:, :], 1.0)
            P.on("gpsimd", gp_masks)
            P.fence("gpsimd", s_g)
            P.wait("scalar", s_g)
            P.wait("scalar", sq_w)
            P.on("scalar", lambda e: e.copy(wg1_bf[:, :], wg1f[:, :]))
            P.on("scalar", lambda e: e.copy(wg2_bf[:, :], wg2f[:, :]))
            P.on("scalar", lambda e: e.activation(cdb[:, :], ldb[:, :], AF.Exp, scale=128.0))
            P.on("scalar", lambda e: e.activation(dsc1[:, :], dsc1[:, :], AF.Relu))
            P.drain("scalar")
            for h in range(H):
                P.on("scalar", lambda e, h=h: e.activation(
                    qdt[:, h * 128:(h + 1) * 128], dsc2[:, :], AF.Exp, scale=ldb[:, h:h + 1]))
            for h in range(H):
                P.on("scalar", lambda e, h=h: e.activation(
                    Wb[:, h * NCORE:(h + 1) * NCORE], pw_b[:, :], AF.Exp, scale=ldb[:, h:h + 1]))
            P.drain("scalar")
            P.fence("scalar", s_sc)
            P.wait("vector", s_sc)
            for h in range(H):
                P.on("vector", lambda e, h=h: e.tensor_mul(
                    Wb[:, h * NCORE:(h + 1) * NCORE], Wb[:, h * NCORE:(h + 1) * NCORE], mk_b[:, :]))
            P.wait("vector", s_g)
            for g in range(1, NAS * NCHS):
                P.on("vector", lambda e, g=g: e.tensor_mul(
                    cdp[:, g * H:(g + 1) * H], cdp[:, (g - 1) * H:g * H], cdb[:, :]))
                P.drain("vector")
            P.fence("vector", s_v)
            P.wait("gpsimd", s_sc)
            P.on("gpsimd", lambda e: e.iota(dsc2[:, 0:1], [[0, 1]], base=127, channel_multiplier=-1,
                                            allow_small_or_imprecise_dtypes=True))
            P.fence("gpsimd", s_g)
            P.wait("scalar", s_g)
            for h in range(H):
                P.on("scalar", lambda e, h=h: e.activation(
                    kdt[:, h:h + 1], dsc2[:, 0:1], AF.Exp, scale=ldb[:, h:h + 1]))
            P.drain("scalar")
            P.fence("scalar", s_sc)
            ses.close()

            b1 = ExitStack()
            bsb = lambda n, s, d: b1.enter_context(nc.sbuf_tensor("b1" + n, s, d))
            bpt = lambda n, s, d: b1.enter_context(nc.psum_tensor("b1" + n, s, d))
            DT = bsb("DT", [128, H * 128], BF16)
            xln = [bsb(f"xln{i}", [128, D], F32) for i in range(2)]
            ln_tm = [bsb(f"ln_tm{i}", [128, D], BF16) for i in range(2)]
            st2 = [{n: bsb(f"st{i}_" + n, [128, 1], F32)
                    for n in ("sum", "ssq", "mean", "nmean", "tv", "vpe", "sqv", "inv", "mi")}
                   for i in range(2)]
            ln2_fm = bsb("ln2_fm", [128, 16 * AS], BF16)
            q_fm = bsb("q_fm", [128, H * AS], BF16)
            k_fm = bsb("k_fm", [128, H * AS], BF16)
            v_tm = bsb("v_tm", [128, NCHS * D], BF16)
            ktm = bsb("ktm", [128, NCHS * H * 128], BF16)
            o_fm = bsb("o_fm", [128, H * AS], BF16)
            sigh = [bsb(f"sigh{i}", [128, 4 * AS], BF16) for i in range(2)]
            g1b = bsb("g1b", [128, AS], BF16)
            q2b = [bsb(f"q2b{i}", [128, 128], BF16) for i in range(2)]
            scd = [bsb(f"scd{i}", [128, 128], BF16) for i in range(2)]
            wqb2 = [bsb(f"wqb{i}", [128, 16 * QBW], BF16) for i in range(2)]
            pban = [bpt(f"pban{i}", [128, 512], F32) for i in range(6)]
            ptrb = [bpt(f"ptrb{i}", [128, 8 * 128], BF16) for i in range(2)]

            ln2_fm3 = ln2_fm[:, :].rearrange("p (a b) -> p a b", a=16)

            def ln_tile(row0, bi, out_tm):
                key = ln_ctr[0]
                ln_ctr[0] += 1
                st = st2[bi]
                xb = xln[bi]
                prev = sv_hist.get(key - 2)
                if prev:
                    P.wait("scalar", s_v, prev)
                P.dma_slot("scalar", sxl[bi], xb[:, :], src[row0:row0 + 128, :])
                P.wait("vector", sxl[bi])
                P.on("vector", lambda e, xb=xb: e.reduce_sum(
                    st["sum"][:, :], xb[:, :], axis=AX.X).then_inc(s_v.h, 1))
                s_v.bump()
                P.wait("scalar", sxl[bi])
                P.on("scalar", lambda e, xb=xb, o=out_tm: e.activation(
                    o[:, :], xb[:, :], AF.Square, accum_out=st["ssq"][:, :]))
                P.drain("scalar")
                P.wait("scalar", s_v)
                P.on("scalar", lambda e: e.activation(st["mean"][:, :], st["sum"][:, :], AF.Copy, scale=1.0 / D))
                P.on("scalar", lambda e: e.activation(st["nmean"][:, :], st["sum"][:, :], AF.Copy, scale=-1.0 / D))
                P.on("scalar", lambda e: e.activation(st["tv"][:, :], st["ssq"][:, :], AF.Copy,
                                                      scale=1.0 / D, bias=EPS))
                P.drain("scalar")
                P.fence("scalar", s_sc)
                P.wait("vector", s_sc)
                P.on("vector", lambda e: e.scalar_tensor_tensor(
                    st["vpe"][:, :], st["nmean"][:, :], st["mean"][:, 0:1], st["tv"][:, :],
                    ALU.mult, ALU.add).then_inc(s_v.h, 1))
                s_v.bump()
                P.wait("scalar", s_v)
                P.on("scalar", lambda e: e.activation(
                    st["sqv"][:, :], st["vpe"][:, :], AF.Sqrt).then_inc(s_sc.h, 1))
                s_sc.bump()
                P.wait("vector", s_sc)
                P.on("vector", lambda e: e.reciprocal(st["inv"][:, :], st["sqv"][:, :]))
                P.drain("vector")
                P.on("vector", lambda e: e.tensor_scalar(
                    st["mi"][:, :], st["nmean"][:, :], st["inv"][:, 0:1], None, ALU.mult))
                P.drain("vector")
                P.on("vector", lambda e, xb=xb, o=out_tm: e.tensor_scalar(
                    o[:, :], xb[:, :], st["inv"][:, 0:1], st["mi"][:, 0:1],
                    ALU.mult, ALU.add).then_inc(s_v.h, 1))
                s_v.bump()
                sv_hist[key] = s_v.v

            P.wait("scalar", s_sc)
            for h in range(H):
                P.on("scalar", lambda e, h=h: e.activation(
                    DT[:, h * 128:(h + 1) * 128], dsc1[:, :], AF.Exp, scale=ldb[:, h:h + 1]))
            P.drain("scalar")
            P.fence("scalar", s_sc)
            P.wait("gpsimd", s_sc)

            def gp_dtmask(e):
                for h in range(H):
                    e.affine_select(out=DT[:, h * 128:(h + 1) * 128],
                                    in_=DT[:, h * 128:(h + 1) * 128],
                                    compare_op=ALU.is_ge, fill=0.0, base=0,
                                    pattern=[[1, 128]], channel_multiplier=-1)
                e.drain()
            P.on("gpsimd", gp_dtmask)
            P.fence("gpsimd", s_g)

            hist = {}
            pctr = [0]
            vctr = [0]
            wctr2 = [0]
            for sc in range(NAS):
                for t in range(NCHS):
                    ln_tile(sc * AS + t * 128, t % 2, ln_tm[t % 2])
                    P.wait("tensor", s_v)
                    for half in range(2):
                        ptb = ptrb[half]
                        P.wait("tensor", s_sc)
                        for dci in range(8):
                            dc = half * 8 + dci
                            mmfn = (lambda e, ptb=ptb, b=ln_tm[t % 2], dc=dc, dci=dci:
                                    e.transpose(ptb[:, dci * 128:(dci + 1) * 128],
                                                b[:, dc * 128:(dc + 1) * 128], ident[:, :]))
                            if dci < 7:
                                P.on("tensor", mmfn)
                            else:
                                P.on("tensor", lambda e, f=mmfn: f(e).then_inc(t_mm.h, 1))
                                t_mm.bump()
                        P.wait("scalar", t_mm)
                        P.on("scalar", lambda e, ptb=ptb, half=half, t=t: e.copy(
                            ln2_fm3[:, half * 8:(half + 1) * 8, t * 128:(t + 1) * 128],
                            ptb[:, :].rearrange("p (a b) -> p a b", a=8)).then_inc(s_sc.h, 1))
                        s_sc.bump()
                ln2_ready = s_sc.v

                if ("g1r", sc - 1) in hist:
                    P.wait("tensor", s_sc, hist[("g1r", sc - 1)])
                if ("qsp", sc - 1) in hist:
                    P.wait("scalar", gq_o, hist[("qsp", sc - 1)])
                def proj_fm(w_s, out_buf, act, nm):
                    for eb in range(NQB):
                        wb_i = (wctr2[0] % 2)
                        wctr2[0] += 1
                        wqb = wqb2[wb_i]
                        if ("wq", wctr2[0] - 3) in hist:
                            P.wait("sync", t_mm, hist[("wq", wctr2[0] - 3)])
                        else:
                            P.wait("sync", t_mm)
                        P.dma_slot("sync", aqw[wb_i], wqb[:, :], w_s[eb])
                        P.wait("tensor", aqw[wb_i])
                        P.wait("tensor", s_sc, ln2_ready)
                        for e128 in range(4):
                            ec = eb * 4 + e128
                            n = pctr[0]
                            pctr[0] += 1
                            pa = pban[n % 2]
                            if ("proj", n - 2) in hist:
                                P.wait("tensor", s_sc, hist[("proj", n - 2)])
                            for dc in range(16):
                                mm(pa[:, :], wqb[:, dc * QBW + e128 * 128: dc * QBW + (e128 + 1) * 128],
                                   ln2_fm[:, dc * AS:(dc + 1) * AS], dc == 0, dc == 15,
                                   inc=(dc == 15))
                            P.wait("scalar", t_mm)
                            P.on("scalar", lambda e, pa=pa, ec=ec, o=out_buf, act=act: e.activation(
                                o[:, ec * AS:(ec + 1) * AS], pa[:, :], act).then_inc(s_sc.h, 1))
                            s_sc.bump()
                            hist[("proj", n)] = s_sc.v
                        hist[("wq", wctr2[0] - 1)] = t_mm.v
                proj_fm(wq_s, q_fm, AF.Silu, "wq")
                proj_fm(wk_s, k_fm, AF.Silu, "wk")
                q_ready = s_sc.v
                P.wait("gpsimd", s_sc, q_ready)
                P.dma("gpsimd", gq_o, q_sp[sc], q_fm[:, :])
                hist[("qsp", sc)] = gq_o.v

                if ("sgev", sc * H - 1) in hist:
                    P.wait("tensor", s_sc, hist[("sgev", sc * H - 1)])
                for eb in range(NQB):
                    wb_i = (wctr2[0] % 2)
                    wctr2[0] += 1
                    wqb = wqb2[wb_i]
                    if ("wq", wctr2[0] - 3) in hist:
                        P.wait("sync", t_mm, hist[("wq", wctr2[0] - 3)])
                    else:
                        P.wait("sync", t_mm)
                    P.dma_slot("sync", aqw[wb_i], wqb[:, :], wv_s[eb])
                    P.wait("tensor", aqw[wb_i])
                    for t in range(NCHS):
                        nv = vctr[0]
                        vctr[0] += 1
                        pa = pban[2 + (nv % 2)]
                        if ("vp", nv - 2) in hist:
                            P.wait("tensor", s_sc, hist[("vp", nv - 2)])
                        for dc in range(16):
                            mm(pa[:, :], ln2_fm[:, dc * AS + t * 128: dc * AS + (t + 1) * 128],
                               wqb[:, dc * QBW:(dc + 1) * QBW], dc == 0, dc == 15, inc=(dc == 15))
                        P.wait("scalar", t_mm)
                        P.on("scalar", lambda e, pa=pa, t=t, eb=eb: e.activation(
                            v_tm[:, t * D + eb * QBW: t * D + (eb + 1) * QBW], pa[:, :],
                            AF.Silu).then_inc(s_sc.h, 1))
                        s_sc.bump()
                        hist[("vp", nv)] = s_sc.v
                    hist[("wq", wctr2[0] - 1)] = t_mm.v
                v_ready = s_sc.v

                kq_ready = s_sc.v
                for c in range(NCHS):
                    for half in range(2):
                        ptb = ptrb[half]
                        P.wait("tensor", s_sc)
                        for hh in range(8):
                            h = half * 8 + hh
                            f = (lambda e, ptb=ptb, hh=hh, h=h, c=c:
                                 e.transpose(ptb[:, hh * 128:(hh + 1) * 128],
                                             k_fm[:, h * AS + c * 128: h * AS + (c + 1) * 128],
                                             ident[:, :]))
                            if hh < 7:
                                P.on("tensor", f)
                            else:
                                P.on("tensor", lambda e, f=f: f(e).then_inc(t_mm.h, 1))
                                t_mm.bump()
                        P.wait("scalar", t_mm)
                        for hh in range(8):
                            h = half * 8 + hh
                            last = hh == 7
                            fn = (lambda e, ptb=ptb, hh=hh, h=h, c=c: e.activation(
                                ktm[:, (c * H + h) * 128:(c * H + h + 1) * 128],
                                ptb[:, hh * 128:(hh + 1) * 128], AF.Copy,
                                scale=kdt[:, h:h + 1]))
                            if last:
                                P.on("scalar", lambda e, f=fn: f(e).then_inc(s_sc.h, 1))
                                s_sc.bump()
                            else:
                                P.on("scalar", fn)
                ktm_ready = s_sc.v

                P.wait("tensor", s_g)
                P.wait("vector", s_g)
                sitems = [(c, h) for c in range(NCHS) for h in range(H)]
                base = sc * NCHS * H

                def emit_score(idx):
                    c, h = sitems[idx]
                    i = base + idx
                    sl = i % 2
                    if ("scd", i - 2) in hist:
                        P.wait("tensor", s_v, hist[("scd", i - 2)])
                    mm(pban[sl][:, 0:128], k_fm[:, h * AS + c * 128: h * AS + (c + 1) * 128],
                       q_fm[:, h * AS + c * 128: h * AS + (c + 1) * 128], True, True, inc=True)
                    hist[("sc", i)] = t_mm.v

                emit_score(0)
                for idx, (c, h) in enumerate(sitems):
                    i = base + idx
                    sl = i % 2
                    if idx == 0 and ("osp", sc - 1) in hist:
                        P.wait("scalar", gq_o, hist[("osp", sc - 1)])
                    P.wait("vector", t_mm, hist[("sc", i)])
                    if ("omm", i - 2) in hist:
                        P.wait("vector", t_mm, hist[("omm", i - 2)])
                    P.on("vector", lambda e, sl=sl, h=h: e.tensor_mul(
                        scd[sl][:, :], pban[sl][:, 0:128], DT[:, h * 128:(h + 1) * 128]))
                    P.on("vector", lambda e, sl=sl, h=h, c=c: e.tensor_mul(
                        q2b[sl][:, :], q_fm[:, h * AS + c * 128: h * AS + (c + 1) * 128],
                        qdt[:, h * 128:(h + 1) * 128]).then_inc(s_v.h, 1))
                    s_v.bump()
                    hist[("scd", i)] = s_v.v
                    if idx + 1 < len(sitems):
                        emit_score(idx + 1)
                    P.wait("tensor", s_v, hist[("scd", i)])
                    if ("oev", i - 2) in hist:
                        P.wait("tensor", s_sc, hist[("oev", i - 2)])
                    mm(pban[2 + sl][:, 0:128],
                       v_tm[:, c * D + h * 128: c * D + (h + 1) * 128], scd[sl][:, :],
                       True, False)
                    mm(pban[2 + sl][:, 0:128], S_bf[:, h * 128:(h + 1) * 128], q2b[sl][:, :],
                       False, True, inc=True)
                    hist[("omm", i)] = t_mm.v
                    P.wait("scalar", t_mm, hist[("omm", i)])
                    P.on("scalar", lambda e, sl=sl, h=h, c=c: e.activation(
                        o_fm[:, h * AS + c * 128: h * AS + (c + 1) * 128],
                        pban[2 + sl][:, 0:128], AF.Copy).then_inc(s_sc.h, 1))
                    s_sc.bump()
                    hist[("oev", i)] = s_sc.v
                    if ("sup", i - 2) in hist:
                        P.wait("tensor", s_v, hist[("sup", i - 2)])
                    mm(pban[4 + sl][:, 0:128], ktm[:, (c * H + h) * 128:(c * H + h + 1) * 128],
                       v_tm[:, c * D + h * 128: c * D + (h + 1) * 128], True, True, inc=True)
                    hist[("kv", i)] = t_mm.v
                    P.wait("vector", t_mm, hist[("kv", i)])
                    P.on("vector", lambda e, sl=sl, h=h: e.scalar_tensor_tensor(
                        S[:, h * 128:(h + 1) * 128], S[:, h * 128:(h + 1) * 128],
                        cdb[:, h:h + 1], pban[4 + sl][:, 0:128], ALU.mult, ALU.add))
                    P.drain("vector")
                    P.on("vector", lambda e, h=h: e.tensor_copy(
                        S_bf[:, h * 128:(h + 1) * 128],
                        S[:, h * 128:(h + 1) * 128]).then_inc(s_v.h, 1))
                    s_v.bump()
                    hist[("sup", i)] = s_v.v
                last_i = base + len(sitems) - 1
                P.wait("gpsimd", s_sc, hist[("oev", last_i)])
                P.dma("gpsimd", gq_o, o_sp[sc], o_fm[:, :])
                hist[("osp", sc)] = gq_o.v

                P.wait("tensor", s_v, hist[("scd", last_i)])
                for dc in range(16):
                    mm(pban[0][:, :], wg1_bf[:, dc * 128:(dc + 1) * 128],
                       ln2_fm[:, dc * AS:(dc + 1) * AS], dc == 0, dc == 15, inc=(dc == 15))
                g1v = t_mm.v
                P.wait("scalar", t_mm, g1v)
                P.on("scalar", lambda e: e.copy(g1b[:, :], pban[0][:, :]).then_inc(s_sc.h, 1))
                s_sc.bump()
                hist[("g1r", sc)] = s_sc.v
                P.wait("tensor", s_sc, hist[("g1r", sc)])
                P.wait("tensor", s_sc, hist[("oev", last_i)])
                for h in range(H):
                    shi = (h // 4) % 2
                    pa = pban[2 + (h % 2)]
                    if ("sgev", sc * H + h - 2) in hist:
                        P.wait("tensor", s_sc, hist[("sgev", sc * H + h - 2)])
                    mm(pa[:, :], wg2_bf[:, h * 128:(h + 1) * 128], g1b[:, :],
                       True, True, inc=True)
                    P.wait("scalar", t_mm)
                    if h % 4 == 0 and ("sigsp", sc * H + h - 5) in hist:
                        P.wait("scalar", gq_o, hist[("sigsp", sc * H + h - 5)])
                    P.on("scalar", lambda e, pa=pa, shi=shi, h=h: e.activation(
                        sigh[shi][:, (h % 4) * AS:(h % 4 + 1) * AS], pa[:, :],
                        AF.Sigmoid).then_inc(s_sc.h, 1))
                    s_sc.bump()
                    hist[("sgev", sc * H + h)] = s_sc.v
                    if h % 4 == 3:
                        P.wait("gpsimd", s_sc, hist[("sgev", sc * H + h)])
                        P.dma("gpsimd", gq_o, sig_sp[sc][:, (h - 3) * AS:(h + 1) * AS],
                              sigh[shi][:, :])
                        hist[("sigsp", sc * H + h)] = gq_o.v
            b1.close()

            P.wait("gpsimd", s_v)
            P.dma("gpsimd", gq_c, kv_b_in.ap(), S[:, :])
            P.wait("gpsimd", gq_c)
            P.on("gpsimd", lambda e: e.collective_compute(
                "AllGather", ALU.bypass, replica_groups=[list(range(NCORE))],
                ins=[kv_b_in.ap()], outs=[kv_b_out.ap()]).then_inc(s_cc.h, 1))
            s_cc.bump(1)

            def b2_reduce():
                P.wait("gpsimd", s_cc)
                P.wait("gpsimd", t_mm)
                b2 = ExitStack()
                kvg = b2.enter_context(nc.sbuf_tensor("kvg", [128, H * HD], F32))
                for cp in range(NCORE):
                    P.dma("gpsimd", gq_c, kvg[:, :], kv_b_out.ap()[cp * 128:(cp + 1) * 128, :])
                    P.wait("vector", gq_c)
                    for h in range(H):
                        P.on("vector", lambda e, h=h, cp=cp: e.scalar_tensor_tensor(
                            Bst[:, h * 128:(h + 1) * 128], kvg[:, h * 128:(h + 1) * 128],
                            Wb[:, h * NCORE + cp: h * NCORE + cp + 1],
                            Bst[:, h * 128:(h + 1) * 128], ALU.mult, ALU.add))
                    P.drain("vector")
                    P.fence("vector", s_v)
                    P.wait("gpsimd", s_v)
                b2.close()
                P.on("vector", lambda e: e.tensor_copy(S_bf[:, :], Bst[:, :]).then_inc(s_v.h, 1))
                s_v.bump()
                return s_v.v

            b3 = ExitStack()
            csb = lambda n, s, d: b3.enter_context(nc.sbuf_tensor("b3" + n, s, d))
            cpt = lambda n, s, d: b3.enter_context(nc.psum_tensor("b3" + n, s, d))
            qr = csb("qr", [128, H * AS], BF16)
            obr = csb("obr", [128, H * AS], BF16)
            sigr = csb("sigr", [128, H * AS], BF16)
            qdt4 = csb("qdt4", [128, H * AS], BF16)
            og_all = csb("oga", [128, H * AS], BF16)
            ogn_fm = og_all
            sqh = [csb(f"sq{i}", [128, AS], BF16) for i in range(2)]
            q2bu = [csb(f"q2u{i}", [128, AS], BF16) for i in range(2)]
            rrh_all = csb("rra", [1, H * AS], BF16)
            rrf = [csb(f"rf{i}", [128, AS], BF16) for i in range(2)]
            wob2 = [csb(f"wob{i}", [128, 16 * QBW], BF16) for i in range(2)]
            xadd = [csb("xadd0", [128, 4 * QBW], F32)]
            xtmp = [csb("xtmp0", [128, 4 * QBW], F32)]
            psC = [cpt(f"psC{i}", [128, AS], F32) for i in range(2)]
            psR = [cpt(f"psR{i}", [1, AS], F32) for i in range(2)]
            psB = [cpt(f"psB{i}", [128, AS], F32) for i in range(2)]
            pb3 = [cpt(f"pb3{i}", [128, 512], F32) for i in range(2)]

            hist = {}
            wctr = [0]
            P.wait("gpsimd", gq_o)

            for eb in range(NQB):
                P.wait("sync", gq_o)
                P.dma("sync", aqw[0], wob2[0][:, :], wo_s[eb])
                P.wait("scalar", aqw[0])
                for dc in range(16):
                    fn = (lambda e, dc=dc: e.activation(
                        wob2[0][:, dc * QBW:(dc + 1) * QBW],
                        wob2[0][:, dc * QBW:(dc + 1) * QBW], AF.Copy,
                        scale=nw_pp[:, dc:dc + 1]))
                    if dc == 15:
                        P.on("scalar", lambda e, f=fn: f(e).then_inc(s_sc.h, 1))
                        s_sc.bump()
                    else:
                        P.on("scalar", fn)
                P.wait("gpsimd", s_sc)
                P.dma("gpsimd", gq_o, wo_s[eb], wob2[0][:, :])
            wo_scaled = gq_o.v

            b0_ready = b2_reduce()
            for sc in range(NAS):
                pgd = hist.get(("rel", sc - 1))
                if pgd:
                    P.wait("gpsimd", t_mm, pgd[0])
                    P.wait("gpsimd", s_v, pgd[1])
                P.dma("gpsimd", gq_x, qr[:, :], q_sp[sc])
                P.dma("gpsimd", gq_x, obr[:, :], o_sp[sc])
                P.dma("gpsimd", gq_x, sigr[:, :], sig_sp[sc])
                rel_ready = gq_x.v

                if ("q2", sc * H - 1) in hist:
                    P.wait("scalar", s_g, hist[("q2", sc * H - 1)])
                for h in range(H):
                    for c in range(NCHS):
                        g = sc * NCHS + c
                        last = (h == H - 1 and c == NCHS - 1)
                        fn = (lambda e, h=h, c=c, g=g: e.activation(
                            qdt4[:, h * AS + c * 128: h * AS + (c + 1) * 128],
                            qdt[:, h * 128:(h + 1) * 128], AF.Copy,
                            scale=cdp[:, g * H + h: g * H + h + 1]))
                        if last:
                            P.on("scalar", lambda e, f=fn: f(e).then_inc(s_sc.h, 1))
                            s_sc.bump()
                        else:
                            P.on("scalar", fn)
                qdt4_ready = s_sc.v

                def w1_q2(h):
                    sl = h % 2
                    i = sc * H + h
                    P.wait("gpsimd", gq_x, rel_ready)
                    P.wait("gpsimd", s_sc, qdt4_ready)
                    if ("cmm", i - 2) in hist:
                        P.wait("gpsimd", t_mm, hist[("cmm", i - 2)])
                    P.on("gpsimd", lambda e, sl=sl, h=h: e.tensor_mul(
                        q2bu[sl][:, :], qr[:, h * AS:(h + 1) * AS],
                        qdt4[:, h * AS:(h + 1) * AS]).then_inc(s_g.h, 1))
                    s_g.bump()
                    hist[("q2", i)] = s_g.v

                def w1_corr(h):
                    sl = h % 2
                    i = sc * H + h
                    P.wait("tensor", s_g, hist[("q2", i)])
                    P.wait("tensor", s_v, b0_ready)
                    P.wait("tensor", gq_x, rel_ready)
                    if ("ogm", i - 2) in hist:
                        P.wait("tensor", s_v, hist[("ogm", i - 2)])
                    mm(psC[sl][:, :], S_bf[:, h * 128:(h + 1) * 128], q2bu[sl][:, :],
                       True, False)
                    mm(psC[sl][:, :], ident[:, :], obr[:, h * AS:(h + 1) * AS],
                       False, True, inc=True)
                    hist[("cmm", i)] = t_mm.v

                def w1_og(h):
                    sl = h % 2
                    i = sc * H + h
                    P.wait("vector", t_mm, hist[("cmm", i)])
                    P.wait("vector", gq_x, rel_ready)
                    if h == 0 and ("ogn", sc * H - 1) in hist:
                        P.wait("vector", s_g, hist[("ogn", sc * H - 1)])
                    if h == 0 and ("wolast", sc - 1) in hist:
                        P.wait("vector", t_mm, hist[("wolast", sc - 1)])
                    def fn(e, sl=sl, h=h):
                        with nc.allow_low_precision(reason="gated o as bf16 operand"):
                            e.tensor_mul(og_all[:, h * AS:(h + 1) * AS], psC[sl][:, :],
                                         sigr[:, h * AS:(h + 1) * AS]).then_inc(s_v.h, 1)
                    P.on("vector", fn)
                    s_v.bump()
                    hist[("ogm", i)] = s_v.v

                def w1_sq(h):
                    sl = h % 2
                    i = sc * H + h
                    P.wait("gpsimd", s_v, hist[("ogm", i)])
                    if ("csum", i - 2) in hist:
                        P.wait("gpsimd", t_mm, hist[("csum", i - 2)])
                    P.on("gpsimd", lambda e, sl=sl, h=h: e.tensor_mul(
                        sqh[sl][:, :], og_all[:, h * AS:(h + 1) * AS],
                        og_all[:, h * AS:(h + 1) * AS]).then_inc(s_g.h, 1))
                    s_g.bump()
                    hist[("sqv", i)] = s_g.v
                    P.wait("tensor", s_g, hist[("sqv", i)])
                    if ("rme", i - 2) in hist:
                        P.wait("tensor", s_v, hist[("rme", i - 2)])
                    mm(psR[sl][0:1, :], ones_bf[:, 0:1], sqh[sl][:, :], True, True, inc=True)
                    hist[("csum", i)] = t_mm.v
                    P.wait("vector", t_mm, hist[("csum", i)])
                    if h == 0 and ("rsq", sc * 4 - 1) in hist:
                        P.wait("vector", s_sc, hist[("rsq", sc * 4 - 1)])
                    def fn(e, sl=sl, h=h):
                        with nc.allow_low_precision(reason="variance as bf16 for rsqrt"):
                            e.tensor_scalar(rrh_all[0:1, h * AS:(h + 1) * AS],
                                            psR[sl][0:1, :], 1.0 / HD, EPS,
                                            ALU.mult, ALU.add).then_inc(s_v.h, 1)
                    P.on("vector", fn)
                    s_v.bump()
                    hist[("rme", i)] = s_v.v
                    if h % 4 == 3:
                        grp = h // 4
                        P.wait("scalar", s_v, hist[("rme", i)])
                        P.on("scalar", lambda e, grp=grp: e.activation(
                            rrh_all[0:1, grp * 4 * AS:(grp + 1) * 4 * AS],
                            rrh_all[0:1, grp * 4 * AS:(grp + 1) * 4 * AS], AF.Ln))
                        P.drain("scalar")
                        P.on("scalar", lambda e, grp=grp: e.activation(
                            rrh_all[0:1, grp * 4 * AS:(grp + 1) * 4 * AS],
                            rrh_all[0:1, grp * 4 * AS:(grp + 1) * 4 * AS], AF.Exp,
                            scale=-0.5).then_inc(s_sc.h, 1))
                        s_sc.bump()
                        hist[("rsq", sc * 4 + grp)] = s_sc.v

                for step in range(H + 3):
                    if step < H:
                        w1_q2(step)
                    if 0 <= step - 1 < H:
                        w1_corr(step - 1)
                    if 0 <= step - 2 < H:
                        w1_og(step - 2)
                    if 0 <= step - 3 < H:
                        w1_sq(step - 3)

                def w2_bcast(h):
                    sl = h % 2
                    i = sc * H + h
                    P.wait("tensor", s_sc, hist[("rsq", sc * 4 + h // 4)])
                    if ("rfc", i - 2) in hist:
                        P.wait("tensor", s_sc, hist[("rfc", i - 2)])
                    mm(psB[sl][:, :], ones_bf[0:1, 0:128],
                       rrh_all[0:1, h * AS:(h + 1) * AS], True, True, inc=True)
                    hist[("bmm", i)] = t_mm.v

                def w2_copy(h):
                    sl = h % 2
                    i = sc * H + h
                    P.wait("scalar", t_mm, hist[("bmm", i)])
                    if ("ogn", i - 2) in hist:
                        P.wait("scalar", s_g, hist[("ogn", i - 2)])
                    def fn(e, sl=sl):
                        with nc.allow_low_precision(reason="1/rms factor in bf16"):
                            e.copy(rrf[sl][:, :], psB[sl][:, :]).then_inc(s_sc.h, 1)
                    P.on("scalar", fn)
                    s_sc.bump()
                    hist[("rfc", i)] = s_sc.v

                def w2_ogn(h):
                    sl = h % 2
                    i = sc * H + h
                    P.wait("gpsimd", s_sc, hist[("rfc", i)])
                    P.on("gpsimd", lambda e, sl=sl, h=h: e.tensor_mul(
                        ogn_fm[:, h * AS:(h + 1) * AS], og_all[:, h * AS:(h + 1) * AS],
                        rrf[sl][:, :]).then_inc(s_g.h, 1))
                    s_g.bump()
                    hist[("ogn", i)] = s_g.v

                for step in range(H + 2):
                    if step < H:
                        w2_bcast(step)
                    if 0 <= step - 1 < H:
                        w2_copy(step - 1)
                    if 0 <= step - 2 < H:
                        w2_ogn(step - 2)

                hist[("rel", sc)] = (hist[("cmm", sc * H + H - 1)],
                                     hist[("ogm", sc * H + H - 1)])
                ognT_ready = hist[("ogn", sc * H + H - 1)]

                for eb in range(NQB):
                    wb_i = eb % 2
                    gwo = sc * NQB + eb
                    P.wait("sync", gq_o, wo_scaled)
                    if ("womm", gwo - 2) in hist:
                        P.wait("sync", t_mm, hist[("womm", gwo - 2)])
                    P.dma_slot("sync", aqw[wb_i], wob2[wb_i][:, :], wo_s[eb])
                    abI = 0
                    prev_cons = hist.get(("b3xcons", abI))
                    if prev_cons:
                        P.wait("sync", s_v, prev_cons)
                    P.dma_slot("sync", xas[abI], xadd[abI][:, :].rearrange("p (a b) -> p a b", a=4),
                               src[sc * AS:(sc + 1) * AS,
                                   eb * QBW:(eb + 1) * QBW].rearrange("(t p) w -> p t w", p=128))
                    P.wait("tensor", aqw[wb_i])
                    P.wait("tensor", s_g, ognT_ready)
                    wob = wob2[wb_i]
                    for c in range(NCHS):
                        nw_ = wctr[0]
                        wctr[0] += 1
                        pa = pb3[nw_ % 2]
                        if ("wo", nw_ - 2) in hist:
                            P.wait("tensor", s_v, hist[("wo", nw_ - 2)])
                        for dc in range(16):
                            mm(pa[:, :], ogn_fm[:, dc * AS + c * 128: dc * AS + (c + 1) * 128],
                               wob[:, dc * QBW:(dc + 1) * QBW], dc == 0, dc == 15, inc=(dc == 15))
                        womm = t_mm.v
                        P.wait("vector", t_mm, womm)
                        P.wait("vector", xas[abI])
                        P.wait("vector", xts[abI])
                        P.on("vector", lambda e, pa=pa, abI=abI, c=c: e.tensor_add(
                            xtmp[abI][:, c * QBW:(c + 1) * QBW], pa[:, :],
                            xadd[abI][:, c * QBW:(c + 1) * QBW]).then_inc(s_v.h, 1))
                        s_v.bump()
                        hist[("wo", nw_)] = s_v.v
                    hist[("womm", gwo)] = t_mm.v
                    hist[("b3xcons", abI)] = s_v.v
                    P.wait("sync", s_v, hist[("b3xcons", abI)])
                    P.dma_slot("sync", xts[abI],
                               dst[sc * AS:(sc + 1) * AS,
                                   eb * QBW:(eb + 1) * QBW].rearrange("(t p) w -> p t w", p=128),
                               xtmp[abI][:, :].rearrange("p (a b) -> p a b", a=4))
                hist[("wolast", sc)] = t_mm.v
            b3.close()
            aes.close()

        prologue()
        have_attn = "attn" in stages
        have_f2 = "ffn2" in stages
        if "ffn1" in stages:
            ffn(x_in, x_cur if (have_attn or have_f2) else y_out, join=False)
        conv_issue(len(conv_jobs))
        if have_attn:
            attn(x_cur if "ffn1" in stages else x_in, x_cur if have_f2 else y_out)
        if have_f2:
            ffn(x_cur if ("ffn1" in stages or have_attn) else x_in, y_out)

        for sem in P.sems:
            P.wait("sync", sem)

        with nc.Block() as block:
            P.emit(block)

    return nc


def kernel(L0_attn_x, L0_attn_log_decay, L0_attn_Wq, L0_attn_Wk, L0_attn_Wv,
           L0_attn_Wo, L0_attn_Wg1, L0_attn_Wg2, L0_attn_norm_w,
           L0_ffn_gate_w, L0_ffn_up_w, L0_ffn_down_w,
           L0_ffn_gate_scale, L0_ffn_up_scale, L0_ffn_down_scale,
           _stages=None):
    if _stages is None:
        _stages = STAGES
    x = np.ascontiguousarray(np.asarray(L0_attn_x, dtype=np.float32).reshape(NCORE * T, D))
    ld = np.asarray(L0_attn_log_decay, dtype=np.float32)
    asc = np.ascontiguousarray

    common = {
        "gw": asc(np.asarray(L0_ffn_gate_w, np.float32)),
        "uw": asc(np.asarray(L0_ffn_up_w, np.float32)),
        "dw": asc(np.asarray(L0_ffn_down_w, np.float32)),
        "wq": asc(np.asarray(L0_attn_Wq, np.float32)),
        "wk": asc(np.asarray(L0_attn_Wk, np.float32)),
        "wv": asc(np.asarray(L0_attn_Wv, np.float32)),
        "wo": asc(np.asarray(L0_attn_Wo, np.float32)),
        "wg1": asc(np.asarray(L0_attn_Wg1, np.float32)),
        "wg2": asc(np.asarray(L0_attn_Wg2, np.float32)),
        "ld_b": asc(np.broadcast_to(ld.reshape(1, H), (128, H))),
        "gs_pp": asc(np.asarray(L0_ffn_gate_scale, np.float32).reshape(NIC, 128).T),
        "us_pp": asc(np.asarray(L0_ffn_up_scale, np.float32).reshape(NIC, 128).T),
        "ds_b": asc(np.broadcast_to(np.asarray(L0_ffn_down_scale, np.float32).reshape(1, D), (128, D))),
        "nw_pp": asc(np.asarray(L0_attn_norm_w, np.float32).reshape(H, 128).T),
    }
    in_maps = []
    for c in range(NCORE):
        pw = np.zeros(NCORE, np.float32)
        mk = np.zeros(NCORE, np.float32)
        for cp in range(c):
            pw[cp] = 2048.0 * (c - 1 - cp)
            mk[cp] = 1.0
        m = dict(common)
        m["x"] = asc(x[c * T:(c + 1) * T, :])
        m["pw_b"] = asc(np.broadcast_to(pw.reshape(1, NCORE), (128, NCORE)))
        m["mk_b"] = asc(np.broadcast_to(mk.reshape(1, NCORE), (128, NCORE)))
        in_maps.append(m)

    nc = build_nc(_stages)
    res = run_bass_kernel_spmd(nc, in_maps, core_ids=list(range(NCORE)))
    out = np.concatenate([res.results[c]["y"] for c in range(NCORE)], axis=0)
    return out.reshape(1, NCORE * T, D).astype(np.float32)


# version 17
# speedup vs baseline: 1.0955x; 1.0319x over previous
import numpy as np
from contextlib import ExitStack

import concourse.bass as bass
import concourse.bass_utils as _bass_utils
import concourse.mybir as mybir
from concourse.bass_utils import run_bass_kernel_spmd

if not getattr(_bass_utils, "_ldw_opt_patched", False):
    _orig_run_command = _bass_utils.run_command

    def _run_command_ldw(cmd, **kw):
        if isinstance(cmd, list):
            cmd = ["--enable-ldw-opt=true" if c == "--enable-ldw-opt=false" else c
                   for c in cmd]
        return _orig_run_command(cmd, **kw)

    _bass_utils.run_command = _run_command_ldw
    _bass_utils._ldw_opt_patched = True

F32 = mybir.dt.float32
BF16 = mybir.dt.bfloat16
F8 = mybir.dt.float8e4
AF = mybir.ActivationFunctionType
ALU = mybir.AluOpType
AX = mybir.AxisListType

NCORE = 8
T = 2048
D = 2048
I = 6144
NIC = I // 128
H = 16
HD = 128
CH = 128
TS = 512
NSC = T // TS
AS = 512
NAS = T // AS
NCHS = AS // CH
IBW = 256
NIB = I // IBW
EBW = 512
NEB = D // EBW
QBW = 512
NQB = D // QBW
EPS = 1e-5

ENGINES = ("sync", "gpsimd", "scalar", "vector", "tensor")
STAGES = ("ffn1", "attn", "ffn2")


class Sem:
    def __init__(self, nc, es, name):
        self.h = es.enter_context(nc.semaphore(name))
        self.v = 0

    def bump(self, n=1):
        self.v += n
        return self.v


class Plan:
    def __init__(self):
        self.ev = {e: [] for e in ENGINES}
        self.last_wait = {}
        self.sems = []

    def on(self, eng, fn):
        self.ev[eng].append(fn)

    def wait(self, eng, sem, val=None):
        v = sem.v if val is None else val
        if v <= 0:
            return
        key = (eng, id(sem))
        if self.last_wait.get(key, -1) >= v:
            return
        self.last_wait[key] = v
        self.on(eng, lambda e, s=sem.h, v=v: e.wait_ge(s, v))

    def join(self, engines=ENGINES):
        for eng in engines:
            for sem in self.sems:
                self.wait(eng, sem)

    def dma(self, eng, sem, out, in_):
        self.wait(eng, sem)
        self.on(eng, lambda e, o=out, i=in_: e.dma_start(out=o, in_=i).then_inc(sem.h, 16))
        return sem.bump(16)

    def dma_slot(self, eng, sem, out, in_):
        self.on(eng, lambda e, o=out, i=in_: e.dma_start(out=o, in_=i).then_inc(sem.h, 16))
        return sem.bump(16)

    def wait_marker(self, eng, marker):
        for sem, val in marker:
            self.wait(eng, sem, val)

    def drain(self, eng):
        self.on(eng, lambda e: e.drain())

    def fence(self, eng, sem):
        self.on(eng, lambda e: e.drain())
        self.on(eng, lambda e, s=sem.h: e.nop().then_inc(s, 1))
        return sem.bump()

    def emit(self, block):
        for eng in ENGINES:
            evs = self.ev[eng]

            def body(e, evs=evs):
                for fn in evs:
                    fn(e)

            getattr(block, eng)(body)


def build_nc(stages=("ffn1", "attn", "ffn2")):
    nc = bass.Bass("TRN2", target_bir_lowering=False)

    x_in = nc.dram_tensor("x", [T, D], F32, kind="ExternalInput").ap()
    gw_in = nc.dram_tensor("gw", [D, I], F32, kind="ExternalInput").ap()
    uw_in = nc.dram_tensor("uw", [D, I], F32, kind="ExternalInput").ap()
    dw_in = nc.dram_tensor("dw", [I, D], F32, kind="ExternalInput").ap()
    wq_in = nc.dram_tensor("wq", [D, D], F32, kind="ExternalInput").ap()
    wk_in = nc.dram_tensor("wk", [D, D], F32, kind="ExternalInput").ap()
    wv_in = nc.dram_tensor("wv", [D, D], F32, kind="ExternalInput").ap()
    wo_in = nc.dram_tensor("wo", [D, D], F32, kind="ExternalInput").ap()
    wg1_in = nc.dram_tensor("wg1", [D, HD], F32, kind="ExternalInput").ap()
    wg2_in = nc.dram_tensor("wg2", [HD, D], F32, kind="ExternalInput").ap()
    ld_b_in = nc.dram_tensor("ld_b", [128, H], F32, kind="ExternalInput").ap()
    gs_in = nc.dram_tensor("gs_pp", [128, NIC], F32, kind="ExternalInput").ap()
    us_in = nc.dram_tensor("us_pp", [128, NIC], F32, kind="ExternalInput").ap()
    ds_in = nc.dram_tensor("ds_b", [128, D], F32, kind="ExternalInput").ap()
    nw_in = nc.dram_tensor("nw_pp", [128, H], F32, kind="ExternalInput").ap()
    pw_in = nc.dram_tensor("pw_b", [128, NCORE], F32, kind="ExternalInput").ap()
    mk_in = nc.dram_tensor("mk_b", [128, NCORE], F32, kind="ExternalInput").ap()
    y_out = nc.dram_tensor("y", [T, D], F32, kind="ExternalOutput").ap()

    x_cur = nc.dram_tensor("x_cur", [T, D], F32).ap()
    gw_s = nc.dram_tensor("gw_s", [NIB, 128, 16 * IBW], F8).ap()
    uw_s = nc.dram_tensor("uw_s", [NIB, 128, 16 * IBW], F8).ap()
    dw_s = nc.dram_tensor("dw_s", [NEB * 6, 128, 8 * EBW], F8).ap()
    wq_s = nc.dram_tensor("wq_s", [NQB, 128, 16 * QBW], BF16).ap()
    wk_s = nc.dram_tensor("wk_s", [NQB, 128, 16 * QBW], BF16).ap()
    wv_s = nc.dram_tensor("wv_s", [NQB, 128, 16 * QBW], BF16).ap()
    wo_s = nc.dram_tensor("wo_s", [NQB, 128, 16 * QBW], BF16).ap()
    sig_sp = nc.dram_tensor("sig_sp", [NAS, 128, H * AS], BF16).ap()
    q_sp = nc.dram_tensor("q_sp", [NAS, 128, H * AS], BF16).ap()
    o_sp = nc.dram_tensor("o_sp", [NAS, 128, H * AS], BF16).ap()
    kv_b_in = nc.dram_tensor("kv_b_in", [128, H * HD], F32)
    kv_b_out = nc.dram_tensor("kv_b_out", [NCORE * 128, H * HD], F32)

    es = ExitStack()
    with es:
        ident = es.enter_context(nc.sbuf_tensor("ident", [128, 128], BF16))

        P = Plan()
        mksem = lambda name: Sem(nc, es, name)
        sq_w = mksem("sq_w")
        gq_x = mksem("gq_x")
        gq_o = mksem("gq_o")
        gq_c = mksem("gq_c")
        s_cc = mksem("s_cc")
        t_mm = mksem("t_mm")
        s_sc = mksem("s_sc")
        s_v = mksem("s_v")
        s_g = mksem("s_g")
        cvs = [mksem(f"cvs{i}") for i in range(3)]
        cvq = mksem("cvq")
        wgs = [mksem(f"wgs{i}") for i in range(2)]
        dws = [mksem(f"dws{i}") for i in range(2)]
        sxl = [mksem(f"sxl{i}") for i in range(2)]
        xas = [mksem(f"xas{i}") for i in range(2)]
        xts = [mksem(f"xts{i}") for i in range(2)]
        aqw = [mksem(f"aqw{i}") for i in range(2)]
        P.sems = [sq_w, gq_x, gq_o, gq_c, s_cc, t_mm, s_sc, s_v, s_g, cvq] + cvs + wgs + dws + sxl + xas + xts + aqw

        conv_jobs = []
        for ib in range(NIB):
            conv_jobs.append((gw_s[ib].rearrange("p (a w) -> p a w", a=16),
                              gw_in[:, ib * IBW:(ib + 1) * IBW].rearrange("(a p) w -> p a w", p=128)))
            conv_jobs.append((uw_s[ib].rearrange("p (a w) -> p a w", a=16),
                              uw_in[:, ib * IBW:(ib + 1) * IBW].rearrange("(a p) w -> p a w", p=128)))
        for eb in range(NEB):
            for ig in range(6):
                conv_jobs.append((dw_s[eb * 6 + ig].rearrange("p (a w) -> p a w", a=8),
                                  dw_in[ig * 1024:(ig + 1) * 1024, eb * EBW:(eb + 1) * EBW]
                                  .rearrange("(a p) w -> p a w", p=128)))
        N_FFN_CONV = len(conv_jobs)
        for w_src, w_dst in ((wq_in, wq_s), (wk_in, wk_s), (wv_in, wv_s), (wo_in, wo_s)):
            for eb in range(NQB):
                conv_jobs.append((w_dst[eb].rearrange("p (a w) -> p a w", a=16),
                                  w_src[:, eb * QBW:(eb + 1) * QBW].rearrange("(a p) w -> p a w", p=128)))
        conv_pos = [0]

        def conv_issue(n):
            for _ in range(n):
                j = conv_pos[0]
                if j >= len(conv_jobs):
                    return
                sem = cvs[j % 3] if j < N_FFN_CONV else cvq
                dst, src = conv_jobs[j]
                P.dma_slot("gpsimd", sem, dst, src)
                conv_pos[0] += 1

        def conv_wait(eng):
            for s in cvs:
                P.wait(eng, s)

        def prologue():
            def gp(e):
                e.memset(ident[:, :], 0.0)
                e.drain()
                e.affine_select(out=ident[:, :], in_=ident[:, :], compare_op=ALU.not_equal,
                                fill=1.0, base=0, pattern=[[-1, 128]], channel_multiplier=1)
                e.drain()
            P.on("gpsimd", gp)
            P.fence("gpsimd", s_g)
            conv_issue(4)

        ln_ctr = [0]
        sv_hist = {}

        ffn_id = [0]

        def ffn(src, dst, join=True):
            if join:
                P.join()
            tag = f"f{ffn_id[0]}"
            ffn_id[0] += 1
            fes = ExitStack()
            sb = lambda n, s, d: fes.enter_context(nc.sbuf_tensor(tag + n, s, d))
            pt = lambda n, s, d: fes.enter_context(nc.psum_tensor(tag + n, s, d))
            gs_pp = sb("gs_pp_t", [128, NIC], F32)
            us_pp = sb("us_pp_t", [128, NIC], F32)
            ds_b = sb("ds_b_t", [128, D], F32)
            xln = [sb(f"xln{i}", [128, D], F32) for i in range(2)]
            ln_tm = [sb(f"ln_tm{i}", [128, D], BF16) for i in range(4)]
            st2 = [{n: sb(f"st{i}_" + n, [128, 1], F32)
                    for n in ("sum", "ssq", "mean", "nmean", "tv", "vpe", "sqv", "inv", "mi")}
                   for i in range(2)]
            ln_fm = sb("ln_fm", [128, 16 * TS], BF16)
            fuse = sb("fuse", [128, NIC * TS], BF16)
            gwb = [sb(f"gwb{i}", [128, 16 * IBW], BF16) for i in range(2)]
            uwb = [sb(f"uwb{i}", [128, 16 * IBW], BF16) for i in range(2)]
            dwb = [sb(f"dwb{i}", [128, 8 * EBW], BF16) for i in range(2)]
            gact = sb("gact", [128, TS], BF16)
            xadd = [sb(f"xadd{i}", [128, 4 * EBW], F32) for i in range(2)]
            xtmp = [sb(f"xtmp{i}", [128, 4 * EBW], F32) for i in range(2)]
            ptr = [pt(f"ptr{i}", [128, 8 * 128], BF16) for i in range(2)]
            pg = pt("pg", [128, 512], F32)
            pu = pt("pu", [128, 512], F32)
            pd = [pt(f"pd{t}", [128, 512], F32) for t in range(4)]

            for dstb, srca in ((gs_pp, gs_in), (us_pp, us_in), (ds_b, ds_in)):
                P.dma("sync", sq_w, dstb[:, :], srca)
            scales_ready = sq_w.v

            ln_fm3 = ln_fm[:, :].rearrange("p (a b) -> p a b", a=16)

            def ln_tile(row0, bi, out_tm, tt_guard=0):
                key = ln_ctr[0]
                ln_ctr[0] += 1
                st = st2[bi]
                xb = xln[bi]
                prev = sv_hist.get(key - 2)
                if prev:
                    P.wait("scalar", s_v, prev)
                P.dma_slot("scalar", sxl[bi], xb[:, :], src[row0:row0 + 128, :])
                P.wait("vector", sxl[bi])
                P.on("vector", lambda e, xb=xb: e.reduce_sum(
                    st["sum"][:, :], xb[:, :], axis=AX.X).then_inc(s_v.h, 1))
                s_v.bump()
                P.wait("scalar", sxl[bi])
                if tt_guard:
                    P.wait("scalar", t_mm, tt_guard)
                P.on("scalar", lambda e, xb=xb, o=out_tm: e.activation(
                    o[:, :], xb[:, :], AF.Square, accum_out=st["ssq"][:, :]))
                P.drain("scalar")
                P.wait("scalar", s_v)
                P.on("scalar", lambda e: e.activation(st["mean"][:, :], st["sum"][:, :], AF.Copy, scale=1.0 / D))
                P.on("scalar", lambda e: e.activation(st["nmean"][:, :], st["sum"][:, :], AF.Copy, scale=-1.0 / D))
                P.on("scalar", lambda e: e.activation(st["tv"][:, :], st["ssq"][:, :], AF.Copy,
                                                      scale=1.0 / D, bias=EPS))
                P.fence("scalar", s_sc)
                P.wait("vector", s_sc)
                P.on("vector", lambda e: e.scalar_tensor_tensor(
                    st["vpe"][:, :], st["nmean"][:, :], st["mean"][:, 0:1], st["tv"][:, :],
                    ALU.mult, ALU.add).then_inc(s_v.h, 1))
                s_v.bump()
                P.wait("scalar", s_v)
                P.on("scalar", lambda e: e.activation(
                    st["sqv"][:, :], st["vpe"][:, :], AF.Sqrt).then_inc(s_sc.h, 1))
                s_sc.bump()
                P.wait("vector", s_sc)
                P.on("vector", lambda e: e.reciprocal(st["inv"][:, :], st["sqv"][:, :]))
                P.drain("vector")
                P.on("vector", lambda e: e.tensor_scalar(
                    st["mi"][:, :], st["nmean"][:, :], st["inv"][:, 0:1], None, ALU.mult))
                P.drain("vector")
                if tt_guard:
                    P.wait("vector", t_mm, tt_guard)
                P.on("vector", lambda e, xb=xb, o=out_tm: e.tensor_scalar(
                    o[:, :], xb[:, :], st["inv"][:, 0:1], st["mi"][:, 0:1],
                    ALU.mult, ALU.add).then_inc(s_v.h, 1))
                s_v.bump()
                sv_hist[key] = s_v.v
                return s_v.v

            def mm(out, lhsT, rhs, start, stop, inc=False):
                if inc:
                    P.on("tensor", lambda e, o=out, l=lhsT, r=rhs, a=start, b=stop:
                         e.matmul(o, l, r, start=a, stop=b).then_inc(t_mm.h, 1))
                    t_mm.bump()
                else:
                    P.on("tensor", lambda e, o=out, l=lhsT, r=rhs, a=start, b=stop:
                         e.matmul(o, l, r, start=a, stop=b))

            first_wload = [True]
            first_dload = [True]
            P.wait("tensor", s_g, 1)
            ln_ready_at = {}

            def ln_compute(sc, tguards):
                vals = []
                for t in range(TS // 128):
                    tg = tguards[t] if tguards else 0
                    vals.append(ln_tile(sc * TS + t * 128, t % 2, ln_tm[t % 4], tg))
                return vals

            def ln_transpose(sc, vals, gate_done_val):
                tgs = []
                for t in range(TS // 128):
                    P.wait("tensor", s_v, vals[t])
                    for half in range(2):
                        ptb = ptr[half]
                        P.wait("tensor", s_sc)
                        for dci in range(8):
                            dc = half * 8 + dci
                            if dci < 7:
                                P.on("tensor", lambda e, ptb=ptb, b=ln_tm[t % 4], dc=dc, dci=dci:
                                     e.transpose(ptb[:, dci * 128:(dci + 1) * 128],
                                                 b[:, dc * 128:(dc + 1) * 128], ident[:, :]))
                            else:
                                P.on("tensor", lambda e, ptb=ptb, b=ln_tm[t % 4], dc=dc, dci=dci:
                                     e.transpose(ptb[:, dci * 128:(dci + 1) * 128],
                                                 b[:, dc * 128:(dc + 1) * 128],
                                                 ident[:, :]).then_inc(t_mm.h, 1))
                                t_mm.bump()
                        P.wait("scalar", t_mm)
                        if gate_done_val:
                            P.wait("scalar", t_mm, gate_done_val)
                        P.on("scalar", lambda e, ptb=ptb, half=half, t=t: e.copy(
                            ln_fm3[:, half * 8:(half + 1) * 8, t * 128:(t + 1) * 128],
                            ptb[:, :].rearrange("p (a b) -> p a b", a=8)).then_inc(s_sc.h, 1))
                        s_sc.bump()
                    tgs.append(t_mm.v)
                ln_ready_at[sc] = s_sc.v
                return tgs

            tg_prev = ln_transpose(0, ln_compute(0, None), 0)
            hist = {}
            for sc in range(NSC):
                ln_ready = ln_ready_at[sc]

                for ib in range(NIB):
                    bi = ib % 2
                    gib = sc * NIB + ib
                    conv_wait("gpsimd")
                    if gib >= 2:
                        P.wait("gpsimd", t_mm, hist[("mm", gib - 2)])
                    P.dma_slot("gpsimd", wgs[bi], gwb[bi][:, :], gw_s[ib])
                    P.dma_slot("gpsimd", wgs[bi], uwb[bi][:, :], uw_s[ib])
                    conv_issue(3)
                    P.wait("tensor", wgs[bi])
                    P.wait("tensor", s_sc, ln_ready)
                    for i128 in range(IBW // 128):
                        ic = ib * (IBW // 128) + i128
                        gic = sc * NIC + ic
                        pga = pg if ic % 2 == 0 else pd[0]
                        pua = pu if ic % 2 == 0 else pd[1]
                        if gic >= 2:
                            P.wait("tensor", s_sc, hist[("g", gic - 2)])
                            P.wait("tensor", s_v, hist[("u", gic - 2)])
                        if ic == 1 and ("xcons", 1) in hist:
                            P.wait("tensor", s_v, hist[("xcons", 1)])
                        for dc in range(16):
                            mm(pga[:, :],
                               gwb[bi][:, dc * IBW + i128 * 128: dc * IBW + (i128 + 1) * 128],
                               ln_fm[:, dc * TS:(dc + 1) * TS], dc == 0, dc == 15)
                        for dc in range(16):
                            mm(pua[:, :],
                               uwb[bi][:, dc * IBW + i128 * 128: dc * IBW + (i128 + 1) * 128],
                               ln_fm[:, dc * TS:(dc + 1) * TS], dc == 0, dc == 15,
                               inc=(dc == 15))
                        hist[("mm", gib)] = t_mm.v
                        P.wait("scalar", t_mm)
                        P.wait("scalar", sq_w, scales_ready)
                        if gic >= 1:
                            P.wait("scalar", s_v, hist[("u", gic - 1)])
                        P.on("scalar", lambda e, ic=ic, pga=pga: e.activation(
                            gact[:, :], pga[:, :], AF.Silu,
                            scale=gs_pp[:, ic:ic + 1]).then_inc(s_sc.h, 1))
                        s_sc.bump()
                        hist[("g", gic)] = s_sc.v
                        P.wait("vector", s_sc)
                        P.on("vector", lambda e, ic=ic, pua=pua: e.scalar_tensor_tensor(
                            fuse[:, ic * TS:(ic + 1) * TS], pua[:, :], us_pp[:, ic:ic + 1],
                            gact[:, :], ALU.mult, ALU.mult).then_inc(s_v.h, 1))
                        s_v.bump()
                        hist[("u", gic)] = s_v.v
                fuse_ready = s_v.v
                gate_done = t_mm.v
                if sc + 1 < NSC:
                    ln_vals = ln_compute(sc + 1, tg_prev)

                for eb in range(NEB):
                    if eb > 0:
                        P.wait("tensor", s_v)
                    abI = eb % 2
                    prev_cons = hist.get(("xcons", abI))
                    if prev_cons:
                        P.wait("sync", s_v, prev_cons)
                    P.dma_slot("sync", xas[abI], xadd[abI][:, :].rearrange("p (a b) -> p a b", a=4),
                               src[sc * TS:(sc + 1) * TS,
                                   eb * EBW:(eb + 1) * EBW].rearrange("(t p) w -> p t w", p=128))
                    for ig in range(6):
                        bi = ig % 2
                        gdc = sc * NEB * 6 + eb * 6 + ig
                        conv_wait("gpsimd")
                        if gdc >= 2:
                            P.wait("gpsimd", t_mm, hist[("dmm", gdc - 2)])
                        P.dma_slot("gpsimd", dws[bi], dwb[bi][:, :], dw_s[eb * 6 + ig])
                        conv_issue(1)
                        P.wait("tensor", dws[bi])
                        P.wait("tensor", s_v, fuse_ready)
                        for i8 in range(8):
                            ic = ig * 8 + i8
                            for t in range(4):
                                mm(pd[t][:, :],
                                   fuse[:, ic * TS + t * 128: ic * TS + (t + 1) * 128],
                                   dwb[bi][:, i8 * EBW:(i8 + 1) * EBW],
                                   ic == 0, ic == NIC - 1,
                                   inc=(i8 == 7 and t == 3))
                        hist[("dmm", gdc)] = t_mm.v
                    P.wait("vector", xts[abI])
                    P.wait("vector", xas[abI])
                    P.wait("vector", t_mm)
                    for t in range(4):
                        P.on("vector", lambda e, abI=abI, t=t, eb=eb: e.scalar_tensor_tensor(
                            xtmp[abI][:, t * EBW:(t + 1) * EBW], pd[t][:, :], 0.5,
                            ds_b[:, eb * EBW:(eb + 1) * EBW], ALU.mult, ALU.mult))
                    P.drain("vector")
                    for t in range(4):
                        P.on("vector", lambda e, abI=abI, t=t: e.tensor_add(
                            xtmp[abI][:, t * EBW:(t + 1) * EBW], xtmp[abI][:, t * EBW:(t + 1) * EBW],
                            xadd[abI][:, t * EBW:(t + 1) * EBW]))
                    P.fence("vector", s_v)
                    hist[("xcons", abI)] = s_v.v
                    P.wait("sync", s_v, hist[("xcons", abI)])
                    P.dma_slot("sync", xts[abI],
                               dst[sc * TS:(sc + 1) * TS,
                                   eb * EBW:(eb + 1) * EBW].rearrange("(t p) w -> p t w", p=128),
                               xtmp[abI][:, :].rearrange("p (a b) -> p a b", a=4))
                if sc + 1 < NSC:
                    tg_prev = ln_transpose(sc + 1, ln_vals, gate_done)
            fes.close()

        def attn(src, dst):
            P.join()
            aes = ExitStack()
            asb = lambda n, s, d: aes.enter_context(nc.sbuf_tensor("a" + n, s, d))
            ldb = asb("ldb", [128, H], F32)
            kdt = asb("kdt", [128, H], F32)
            cdb = asb("cdb", [128, H], F32)
            pw_b = asb("pw_b", [128, NCORE], F32)
            mk_b = asb("mk_b", [128, NCORE], F32)
            Wb = asb("Wb", [128, H * NCORE], F32)
            qdt = asb("qdt", [128, H * 128], F32)
            S = asb("S", [128, H * HD], F32)
            S_bf = asb("S_bf", [128, H * HD], BF16)
            Bst = asb("Bst", [128, H * HD], F32)
            nw_pp = asb("nw_pp", [128, H], F32)
            wg1_bf = asb("wg1_bf", [128, H * 128], BF16)
            wg2_bf = asb("wg2_bf", [128, D], BF16)
            dsc1 = asb("dsc1", [128, 128], F32)
            dsc2 = asb("dsc2", [128, 128], F32)
            cdp = asb("cdp", [128, NAS * NCHS * H], F32)
            ones_bf = asb("ones_bf", [128, 128], BF16)

            def mm(out, lhsT, rhs, start, stop, inc=False):
                if inc:
                    P.on("tensor", lambda e, o=out, l=lhsT, r=rhs, a=start, b=stop:
                         e.matmul(o, l, r, start=a, stop=b).then_inc(t_mm.h, 1))
                    t_mm.bump()
                else:
                    P.on("tensor", lambda e, o=out, l=lhsT, r=rhs, a=start, b=stop:
                         e.matmul(o, l, r, start=a, stop=b))

            for dstb, srca in ((ldb, ld_b_in), (pw_b, pw_in), (mk_b, mk_in), (nw_pp, nw_in)):
                P.dma("sync", sq_w, dstb[:, :], srca)
            ses = ExitStack()
            wg1f = ses.enter_context(nc.sbuf_tensor("awg1f", [128, H * 128], F32))
            wg2f = ses.enter_context(nc.sbuf_tensor("awg2f", [128, D], F32))
            P.dma("sync", sq_w, wg1f[:, :].rearrange("p (a b) -> p a b", a=16),
                  wg1_in.rearrange("(a p) b -> p a b", p=128))
            P.dma("sync", sq_w, wg2f[:, :], wg2_in)

            def gp_masks(e):
                e.iota(dsc1[:, :], [[1, 128]], base=0, channel_multiplier=-1,
                       allow_small_or_imprecise_dtypes=True)
                e.drain()
                e.iota(dsc2[:, :], [[1, 128]], base=1, channel_multiplier=0,
                       allow_small_or_imprecise_dtypes=True)
                e.drain()
                e.memset(S[:, :], 0.0)
                e.memset(S_bf[:, :], 0.0)
                e.memset(Bst[:, :], 0.0)
                e.memset(cdp[:, 0:H], 1.0)
                e.memset(ones_bf[:, :], 1.0)
            P.on("gpsimd", gp_masks)
            P.fence("gpsimd", s_g)
            P.wait("scalar", s_g)
            P.wait("scalar", sq_w)
            P.on("scalar", lambda e: e.copy(wg1_bf[:, :], wg1f[:, :]))
            P.on("scalar", lambda e: e.copy(wg2_bf[:, :], wg2f[:, :]))
            P.on("scalar", lambda e: e.activation(cdb[:, :], ldb[:, :], AF.Exp, scale=128.0))
            P.on("scalar", lambda e: e.activation(dsc1[:, :], dsc1[:, :], AF.Relu))
            P.drain("scalar")
            for h in range(H):
                P.on("scalar", lambda e, h=h: e.activation(
                    qdt[:, h * 128:(h + 1) * 128], dsc2[:, :], AF.Exp, scale=ldb[:, h:h + 1]))
            for h in range(H):
                P.on("scalar", lambda e, h=h: e.activation(
                    Wb[:, h * NCORE:(h + 1) * NCORE], pw_b[:, :], AF.Exp, scale=ldb[:, h:h + 1]))
            P.drain("scalar")
            P.fence("scalar", s_sc)
            P.wait("vector", s_sc)
            for h in range(H):
                P.on("vector", lambda e, h=h: e.tensor_mul(
                    Wb[:, h * NCORE:(h + 1) * NCORE], Wb[:, h * NCORE:(h + 1) * NCORE], mk_b[:, :]))
            P.wait("vector", s_g)
            for g in range(1, NAS * NCHS):
                P.on("vector", lambda e, g=g: e.tensor_mul(
                    cdp[:, g * H:(g + 1) * H], cdp[:, (g - 1) * H:g * H], cdb[:, :]))
                P.drain("vector")
            P.fence("vector", s_v)
            P.wait("gpsimd", s_sc)
            P.on("gpsimd", lambda e: e.iota(dsc2[:, 0:1], [[0, 1]], base=127, channel_multiplier=-1,
                                            allow_small_or_imprecise_dtypes=True))
            P.fence("gpsimd", s_g)
            P.wait("scalar", s_g)
            for h in range(H):
                P.on("scalar", lambda e, h=h: e.activation(
                    kdt[:, h:h + 1], dsc2[:, 0:1], AF.Exp, scale=ldb[:, h:h + 1]))
            P.drain("scalar")
            P.fence("scalar", s_sc)
            ses.close()

            b1 = ExitStack()
            bsb = lambda n, s, d: b1.enter_context(nc.sbuf_tensor("b1" + n, s, d))
            bpt = lambda n, s, d: b1.enter_context(nc.psum_tensor("b1" + n, s, d))
            DT = bsb("DT", [128, H * 128], BF16)
            xln = [bsb(f"xln{i}", [128, D], F32) for i in range(2)]
            ln_tm = [bsb(f"ln_tm{i}", [128, D], BF16) for i in range(2)]
            st2 = [{n: bsb(f"st{i}_" + n, [128, 1], F32)
                    for n in ("sum", "ssq", "mean", "nmean", "tv", "vpe", "sqv", "inv", "mi")}
                   for i in range(2)]
            ln2_fm = bsb("ln2_fm", [128, 16 * AS], BF16)
            q_fm = bsb("q_fm", [128, H * AS], BF16)
            k_fm = bsb("k_fm", [128, H * AS], BF16)
            v_tm = bsb("v_tm", [128, NCHS * D], BF16)
            ktm = bsb("ktm", [128, NCHS * H * 128], BF16)
            o_fm = bsb("o_fm", [128, H * AS], BF16)
            sigh = [bsb(f"sigh{i}", [128, 4 * AS], BF16) for i in range(2)]
            g1b = bsb("g1b", [128, AS], BF16)
            q2b = [bsb(f"q2b{i}", [128, 128], BF16) for i in range(2)]
            scd = [bsb(f"scd{i}", [128, 128], BF16) for i in range(2)]
            wqb2 = [bsb(f"wqb{i}", [128, 16 * QBW], BF16) for i in range(2)]
            pban = [bpt(f"pban{i}", [128, 512], F32) for i in range(6)]
            ptrb = [bpt(f"ptrb{i}", [128, 8 * 128], BF16) for i in range(2)]

            ln2_fm3 = ln2_fm[:, :].rearrange("p (a b) -> p a b", a=16)

            def ln_tile(row0, bi, out_tm):
                key = ln_ctr[0]
                ln_ctr[0] += 1
                st = st2[bi]
                xb = xln[bi]
                prev = sv_hist.get(key - 2)
                if prev:
                    P.wait("scalar", s_v, prev)
                P.dma_slot("scalar", sxl[bi], xb[:, :], src[row0:row0 + 128, :])
                P.wait("vector", sxl[bi])
                P.on("vector", lambda e, xb=xb: e.reduce_sum(
                    st["sum"][:, :], xb[:, :], axis=AX.X).then_inc(s_v.h, 1))
                s_v.bump()
                P.wait("scalar", sxl[bi])
                P.on("scalar", lambda e, xb=xb, o=out_tm: e.activation(
                    o[:, :], xb[:, :], AF.Square, accum_out=st["ssq"][:, :]))
                P.drain("scalar")
                P.wait("scalar", s_v)
                P.on("scalar", lambda e: e.activation(st["mean"][:, :], st["sum"][:, :], AF.Copy, scale=1.0 / D))
                P.on("scalar", lambda e: e.activation(st["nmean"][:, :], st["sum"][:, :], AF.Copy, scale=-1.0 / D))
                P.on("scalar", lambda e: e.activation(st["tv"][:, :], st["ssq"][:, :], AF.Copy,
                                                      scale=1.0 / D, bias=EPS))
                P.drain("scalar")
                P.fence("scalar", s_sc)
                P.wait("vector", s_sc)
                P.on("vector", lambda e: e.scalar_tensor_tensor(
                    st["vpe"][:, :], st["nmean"][:, :], st["mean"][:, 0:1], st["tv"][:, :],
                    ALU.mult, ALU.add).then_inc(s_v.h, 1))
                s_v.bump()
                P.wait("scalar", s_v)
                P.on("scalar", lambda e: e.activation(
                    st["sqv"][:, :], st["vpe"][:, :], AF.Sqrt).then_inc(s_sc.h, 1))
                s_sc.bump()
                P.wait("vector", s_sc)
                P.on("vector", lambda e: e.reciprocal(st["inv"][:, :], st["sqv"][:, :]))
                P.drain("vector")
                P.on("vector", lambda e: e.tensor_scalar(
                    st["mi"][:, :], st["nmean"][:, :], st["inv"][:, 0:1], None, ALU.mult))
                P.drain("vector")
                P.on("vector", lambda e, xb=xb, o=out_tm: e.tensor_scalar(
                    o[:, :], xb[:, :], st["inv"][:, 0:1], st["mi"][:, 0:1],
                    ALU.mult, ALU.add).then_inc(s_v.h, 1))
                s_v.bump()
                sv_hist[key] = s_v.v

            P.wait("scalar", s_sc)
            for h in range(H):
                P.on("scalar", lambda e, h=h: e.activation(
                    DT[:, h * 128:(h + 1) * 128], dsc1[:, :], AF.Exp, scale=ldb[:, h:h + 1]))
            P.drain("scalar")
            P.fence("scalar", s_sc)
            P.wait("gpsimd", s_sc)

            def gp_dtmask(e):
                for h in range(H):
                    e.affine_select(out=DT[:, h * 128:(h + 1) * 128],
                                    in_=DT[:, h * 128:(h + 1) * 128],
                                    compare_op=ALU.is_ge, fill=0.0, base=0,
                                    pattern=[[1, 128]], channel_multiplier=-1)
                e.drain()
            P.on("gpsimd", gp_dtmask)
            P.fence("gpsimd", s_g)

            hist = {}
            pctr = [0]
            vctr = [0]
            wctr2 = [0]
            for sc in range(NAS):
                for t in range(NCHS):
                    ln_tile(sc * AS + t * 128, t % 2, ln_tm[t % 2])
                    P.wait("tensor", s_v)
                    for half in range(2):
                        ptb = ptrb[half]
                        P.wait("tensor", s_sc)
                        for dci in range(8):
                            dc = half * 8 + dci
                            mmfn = (lambda e, ptb=ptb, b=ln_tm[t % 2], dc=dc, dci=dci:
                                    e.transpose(ptb[:, dci * 128:(dci + 1) * 128],
                                                b[:, dc * 128:(dc + 1) * 128], ident[:, :]))
                            if dci < 7:
                                P.on("tensor", mmfn)
                            else:
                                P.on("tensor", lambda e, f=mmfn: f(e).then_inc(t_mm.h, 1))
                                t_mm.bump()
                        P.wait("scalar", t_mm)
                        P.on("scalar", lambda e, ptb=ptb, half=half, t=t: e.copy(
                            ln2_fm3[:, half * 8:(half + 1) * 8, t * 128:(t + 1) * 128],
                            ptb[:, :].rearrange("p (a b) -> p a b", a=8)).then_inc(s_sc.h, 1))
                        s_sc.bump()
                ln2_ready = s_sc.v

                if ("g1r", sc - 1) in hist:
                    P.wait("tensor", s_sc, hist[("g1r", sc - 1)])
                if ("qsp", sc - 1) in hist:
                    P.wait("scalar", gq_o, hist[("qsp", sc - 1)])
                def proj_fm(w_s, out_buf, act, nm):
                    for eb in range(NQB):
                        wb_i = (wctr2[0] % 2)
                        wctr2[0] += 1
                        wqb = wqb2[wb_i]
                        if ("wq", wctr2[0] - 3) in hist:
                            P.wait("sync", t_mm, hist[("wq", wctr2[0] - 3)])
                        else:
                            P.wait("sync", t_mm)
                        P.dma_slot("sync", aqw[wb_i], wqb[:, :], w_s[eb])
                        P.wait("tensor", aqw[wb_i])
                        P.wait("tensor", s_sc, ln2_ready)
                        for e128 in range(4):
                            ec = eb * 4 + e128
                            n = pctr[0]
                            pctr[0] += 1
                            pa = pban[n % 2]
                            if ("proj", n - 2) in hist:
                                P.wait("tensor", s_sc, hist[("proj", n - 2)])
                            for dc in range(16):
                                mm(pa[:, :], wqb[:, dc * QBW + e128 * 128: dc * QBW + (e128 + 1) * 128],
                                   ln2_fm[:, dc * AS:(dc + 1) * AS], dc == 0, dc == 15,
                                   inc=(dc == 15))
                            P.wait("scalar", t_mm)
                            P.on("scalar", lambda e, pa=pa, ec=ec, o=out_buf, act=act: e.activation(
                                o[:, ec * AS:(ec + 1) * AS], pa[:, :], act).then_inc(s_sc.h, 1))
                            s_sc.bump()
                            hist[("proj", n)] = s_sc.v
                        hist[("wq", wctr2[0] - 1)] = t_mm.v
                proj_fm(wq_s, q_fm, AF.Silu, "wq")
                proj_fm(wk_s, k_fm, AF.Silu, "wk")
                q_ready = s_sc.v
                P.wait("gpsimd", s_sc, q_ready)
                P.dma("gpsimd", gq_o, q_sp[sc], q_fm[:, :])
                hist[("qsp", sc)] = gq_o.v

                if ("sgev", sc * H - 1) in hist:
                    P.wait("tensor", s_sc, hist[("sgev", sc * H - 1)])
                for eb in range(NQB):
                    wb_i = (wctr2[0] % 2)
                    wctr2[0] += 1
                    wqb = wqb2[wb_i]
                    if ("wq", wctr2[0] - 3) in hist:
                        P.wait("sync", t_mm, hist[("wq", wctr2[0] - 3)])
                    else:
                        P.wait("sync", t_mm)
                    P.dma_slot("sync", aqw[wb_i], wqb[:, :], wv_s[eb])
                    P.wait("tensor", aqw[wb_i])
                    for t in range(NCHS):
                        nv = vctr[0]
                        vctr[0] += 1
                        pa = pban[2 + (nv % 2)]
                        if ("vp", nv - 2) in hist:
                            P.wait("tensor", s_sc, hist[("vp", nv - 2)])
                        for dc in range(16):
                            mm(pa[:, :], ln2_fm[:, dc * AS + t * 128: dc * AS + (t + 1) * 128],
                               wqb[:, dc * QBW:(dc + 1) * QBW], dc == 0, dc == 15, inc=(dc == 15))
                        P.wait("scalar", t_mm)
                        P.on("scalar", lambda e, pa=pa, t=t, eb=eb: e.activation(
                            v_tm[:, t * D + eb * QBW: t * D + (eb + 1) * QBW], pa[:, :],
                            AF.Silu).then_inc(s_sc.h, 1))
                        s_sc.bump()
                        hist[("vp", nv)] = s_sc.v
                    hist[("wq", wctr2[0] - 1)] = t_mm.v
                v_ready = s_sc.v

                kq_ready = s_sc.v
                for c in range(NCHS):
                    for half in range(2):
                        ptb = ptrb[half]
                        P.wait("tensor", s_sc)
                        for hh in range(8):
                            h = half * 8 + hh
                            f = (lambda e, ptb=ptb, hh=hh, h=h, c=c:
                                 e.transpose(ptb[:, hh * 128:(hh + 1) * 128],
                                             k_fm[:, h * AS + c * 128: h * AS + (c + 1) * 128],
                                             ident[:, :]))
                            if hh < 7:
                                P.on("tensor", f)
                            else:
                                P.on("tensor", lambda e, f=f: f(e).then_inc(t_mm.h, 1))
                                t_mm.bump()
                        P.wait("scalar", t_mm)
                        for hh in range(8):
                            h = half * 8 + hh
                            last = hh == 7
                            fn = (lambda e, ptb=ptb, hh=hh, h=h, c=c: e.activation(
                                ktm[:, (c * H + h) * 128:(c * H + h + 1) * 128],
                                ptb[:, hh * 128:(hh + 1) * 128], AF.Copy,
                                scale=kdt[:, h:h + 1]))
                            if last:
                                P.on("scalar", lambda e, f=fn: f(e).then_inc(s_sc.h, 1))
                                s_sc.bump()
                            else:
                                P.on("scalar", fn)
                ktm_ready = s_sc.v

                P.wait("tensor", s_g)
                P.wait("vector", s_g)
                sitems = [(c, h) for c in range(NCHS) for h in range(H)]
                base = sc * NCHS * H

                def emit_score(idx):
                    c, h = sitems[idx]
                    i = base + idx
                    sl = i % 2
                    if ("scd", i - 2) in hist:
                        P.wait("tensor", s_v, hist[("scd", i - 2)])
                    mm(pban[sl][:, 0:128], k_fm[:, h * AS + c * 128: h * AS + (c + 1) * 128],
                       q_fm[:, h * AS + c * 128: h * AS + (c + 1) * 128], True, True, inc=True)
                    hist[("sc", i)] = t_mm.v

                emit_score(0)
                for idx, (c, h) in enumerate(sitems):
                    i = base + idx
                    sl = i % 2
                    if idx == 0 and ("osp", sc - 1) in hist:
                        P.wait("scalar", gq_o, hist[("osp", sc - 1)])
                    P.wait("vector", t_mm, hist[("sc", i)])
                    if ("omm", i - 2) in hist:
                        P.wait("vector", t_mm, hist[("omm", i - 2)])
                    P.on("vector", lambda e, sl=sl, h=h: e.tensor_mul(
                        scd[sl][:, :], pban[sl][:, 0:128], DT[:, h * 128:(h + 1) * 128]))
                    P.on("vector", lambda e, sl=sl, h=h, c=c: e.tensor_mul(
                        q2b[sl][:, :], q_fm[:, h * AS + c * 128: h * AS + (c + 1) * 128],
                        qdt[:, h * 128:(h + 1) * 128]).then_inc(s_v.h, 1))
                    s_v.bump()
                    hist[("scd", i)] = s_v.v
                    if idx + 1 < len(sitems):
                        emit_score(idx + 1)
                    P.wait("tensor", s_v, hist[("scd", i)])
                    if ("oev", i - 2) in hist:
                        P.wait("tensor", s_sc, hist[("oev", i - 2)])
                    mm(pban[2 + sl][:, 0:128],
                       v_tm[:, c * D + h * 128: c * D + (h + 1) * 128], scd[sl][:, :],
                       True, False)
                    mm(pban[2 + sl][:, 0:128], S_bf[:, h * 128:(h + 1) * 128], q2b[sl][:, :],
                       False, True, inc=True)
                    hist[("omm", i)] = t_mm.v
                    P.wait("scalar", t_mm, hist[("omm", i)])
                    P.on("scalar", lambda e, sl=sl, h=h, c=c: e.activation(
                        o_fm[:, h * AS + c * 128: h * AS + (c + 1) * 128],
                        pban[2 + sl][:, 0:128], AF.Copy).then_inc(s_sc.h, 1))
                    s_sc.bump()
                    hist[("oev", i)] = s_sc.v
                    if ("sup", i - 2) in hist:
                        P.wait("tensor", s_v, hist[("sup", i - 2)])
                    mm(pban[4 + sl][:, 0:128], ktm[:, (c * H + h) * 128:(c * H + h + 1) * 128],
                       v_tm[:, c * D + h * 128: c * D + (h + 1) * 128], True, True, inc=True)
                    hist[("kv", i)] = t_mm.v
                    P.wait("vector", t_mm, hist[("kv", i)])
                    P.on("vector", lambda e, sl=sl, h=h: e.scalar_tensor_tensor(
                        S[:, h * 128:(h + 1) * 128], S[:, h * 128:(h + 1) * 128],
                        cdb[:, h:h + 1], pban[4 + sl][:, 0:128], ALU.mult, ALU.add))
                    P.drain("vector")
                    P.on("vector", lambda e, h=h: e.tensor_copy(
                        S_bf[:, h * 128:(h + 1) * 128],
                        S[:, h * 128:(h + 1) * 128]).then_inc(s_v.h, 1))
                    s_v.bump()
                    hist[("sup", i)] = s_v.v
                last_i = base + len(sitems) - 1
                P.wait("gpsimd", s_sc, hist[("oev", last_i)])
                P.dma("gpsimd", gq_o, o_sp[sc], o_fm[:, :])
                hist[("osp", sc)] = gq_o.v

                P.wait("tensor", s_v, hist[("scd", last_i)])
                for dc in range(16):
                    mm(pban[0][:, :], wg1_bf[:, dc * 128:(dc + 1) * 128],
                       ln2_fm[:, dc * AS:(dc + 1) * AS], dc == 0, dc == 15, inc=(dc == 15))
                g1v = t_mm.v
                P.wait("scalar", t_mm, g1v)
                P.on("scalar", lambda e: e.copy(g1b[:, :], pban[0][:, :]).then_inc(s_sc.h, 1))
                s_sc.bump()
                hist[("g1r", sc)] = s_sc.v
                P.wait("tensor", s_sc, hist[("g1r", sc)])
                P.wait("tensor", s_sc, hist[("oev", last_i)])
                for h in range(H):
                    shi = (h // 4) % 2
                    pa = pban[2 + (h % 2)]
                    if ("sgev", sc * H + h - 2) in hist:
                        P.wait("tensor", s_sc, hist[("sgev", sc * H + h - 2)])
                    mm(pa[:, :], wg2_bf[:, h * 128:(h + 1) * 128], g1b[:, :],
                       True, True, inc=True)
                    P.wait("scalar", t_mm)
                    if h % 4 == 0 and ("sigsp", sc * H + h - 5) in hist:
                        P.wait("scalar", gq_o, hist[("sigsp", sc * H + h - 5)])
                    P.on("scalar", lambda e, pa=pa, shi=shi, h=h: e.activation(
                        sigh[shi][:, (h % 4) * AS:(h % 4 + 1) * AS], pa[:, :],
                        AF.Sigmoid).then_inc(s_sc.h, 1))
                    s_sc.bump()
                    hist[("sgev", sc * H + h)] = s_sc.v
                    if h % 4 == 3:
                        P.wait("gpsimd", s_sc, hist[("sgev", sc * H + h)])
                        P.dma("gpsimd", gq_o, sig_sp[sc][:, (h - 3) * AS:(h + 1) * AS],
                              sigh[shi][:, :])
                        hist[("sigsp", sc * H + h)] = gq_o.v
            b1.close()

            P.wait("gpsimd", s_v)
            P.dma("gpsimd", gq_c, kv_b_in.ap(), S[:, :])
            P.wait("gpsimd", gq_c)
            P.on("gpsimd", lambda e: e.collective_compute(
                "AllGather", ALU.bypass, replica_groups=[list(range(NCORE))],
                ins=[kv_b_in.ap()], outs=[kv_b_out.ap()]).then_inc(s_cc.h, 1))
            s_cc.bump(1)

            def b2_reduce():
                P.wait("gpsimd", s_cc)
                P.wait("gpsimd", t_mm)
                b2 = ExitStack()
                kvg = b2.enter_context(nc.sbuf_tensor("kvg", [128, H * HD], F32))
                for cp in range(NCORE):
                    P.dma("gpsimd", gq_c, kvg[:, :], kv_b_out.ap()[cp * 128:(cp + 1) * 128, :])
                    P.wait("vector", gq_c)
                    for h in range(H):
                        P.on("vector", lambda e, h=h, cp=cp: e.scalar_tensor_tensor(
                            Bst[:, h * 128:(h + 1) * 128], kvg[:, h * 128:(h + 1) * 128],
                            Wb[:, h * NCORE + cp: h * NCORE + cp + 1],
                            Bst[:, h * 128:(h + 1) * 128], ALU.mult, ALU.add))
                    P.drain("vector")
                    P.fence("vector", s_v)
                    P.wait("gpsimd", s_v)
                b2.close()
                P.on("vector", lambda e: e.tensor_copy(S_bf[:, :], Bst[:, :]).then_inc(s_v.h, 1))
                s_v.bump()
                return s_v.v

            b3 = ExitStack()
            csb = lambda n, s, d: b3.enter_context(nc.sbuf_tensor("b3" + n, s, d))
            cpt = lambda n, s, d: b3.enter_context(nc.psum_tensor("b3" + n, s, d))
            qr = csb("qr", [128, H * AS], BF16)
            obr = csb("obr", [128, H * AS], BF16)
            sigr = csb("sigr", [128, H * AS], BF16)
            qdt4 = csb("qdt4", [128, H * AS], BF16)
            og_all = csb("oga", [128, H * AS], BF16)
            ogn_fm = og_all
            sqh = [csb(f"sq{i}", [128, AS], BF16) for i in range(2)]
            q2bu = [csb(f"q2u{i}", [128, AS], BF16) for i in range(2)]
            rrh_all = csb("rra", [1, H * AS], BF16)
            rrf = [csb(f"rf{i}", [128, AS], BF16) for i in range(2)]
            wob2 = [csb(f"wob{i}", [128, 16 * QBW], BF16) for i in range(2)]
            xadd = [csb("xadd0", [128, 4 * QBW], F32)]
            xtmp = [csb("xtmp0", [128, 4 * QBW], F32)]
            psC = [cpt(f"psC{i}", [128, AS], F32) for i in range(2)]
            psR = [cpt(f"psR{i}", [1, AS], F32) for i in range(2)]
            psB = [cpt(f"psB{i}", [128, AS], F32) for i in range(2)]
            pb3 = [cpt(f"pb3{i}", [128, 512], F32) for i in range(2)]

            hist = {}
            wctr = [0]
            P.wait("gpsimd", gq_o)

            for eb in range(NQB):
                P.wait("sync", gq_o)
                P.dma("sync", aqw[0], wob2[0][:, :], wo_s[eb])
                P.wait("scalar", aqw[0])
                for dc in range(16):
                    fn = (lambda e, dc=dc: e.activation(
                        wob2[0][:, dc * QBW:(dc + 1) * QBW],
                        wob2[0][:, dc * QBW:(dc + 1) * QBW], AF.Copy,
                        scale=nw_pp[:, dc:dc + 1]))
                    if dc == 15:
                        P.on("scalar", lambda e, f=fn: f(e).then_inc(s_sc.h, 1))
                        s_sc.bump()
                    else:
                        P.on("scalar", fn)
                P.wait("gpsimd", s_sc)
                P.dma("gpsimd", gq_o, wo_s[eb], wob2[0][:, :])
            wo_scaled = gq_o.v

            P.dma("gpsimd", gq_x, qr[:, :], q_sp[0])
            P.dma("gpsimd", gq_x, obr[:, :], o_sp[0])
            P.dma("gpsimd", gq_x, sigr[:, :], sig_sp[0])
            rel0_ready = gq_x.v

            b0_ready = b2_reduce()
            for sc in range(NAS):
                if sc == 0:
                    rel_ready = rel0_ready
                else:
                    pgd = hist.get(("rel", sc - 1))
                    if pgd:
                        P.wait("gpsimd", t_mm, pgd[0])
                        P.wait("gpsimd", s_v, pgd[1])
                    P.dma("gpsimd", gq_x, qr[:, :], q_sp[sc])
                    P.dma("gpsimd", gq_x, obr[:, :], o_sp[sc])
                    P.dma("gpsimd", gq_x, sigr[:, :], sig_sp[sc])
                    rel_ready = gq_x.v

                if ("q2", sc * H - 1) in hist:
                    P.wait("scalar", s_g, hist[("q2", sc * H - 1)])
                for h in range(H):
                    for c in range(NCHS):
                        g = sc * NCHS + c
                        last = (h == H - 1 and c == NCHS - 1)
                        fn = (lambda e, h=h, c=c, g=g: e.activation(
                            qdt4[:, h * AS + c * 128: h * AS + (c + 1) * 128],
                            qdt[:, h * 128:(h + 1) * 128], AF.Copy,
                            scale=cdp[:, g * H + h: g * H + h + 1]))
                        if last:
                            P.on("scalar", lambda e, f=fn: f(e).then_inc(s_sc.h, 1))
                            s_sc.bump()
                        else:
                            P.on("scalar", fn)
                qdt4_ready = s_sc.v

                def w1_q2(h):
                    sl = h % 2
                    i = sc * H + h
                    P.wait("gpsimd", gq_x, rel_ready)
                    P.wait("gpsimd", s_sc, qdt4_ready)
                    if ("cmm", i - 2) in hist:
                        P.wait("gpsimd", t_mm, hist[("cmm", i - 2)])
                    P.on("gpsimd", lambda e, sl=sl, h=h: e.tensor_mul(
                        q2bu[sl][:, :], qr[:, h * AS:(h + 1) * AS],
                        qdt4[:, h * AS:(h + 1) * AS]).then_inc(s_g.h, 1))
                    s_g.bump()
                    hist[("q2", i)] = s_g.v

                def w1_corr(h):
                    sl = h % 2
                    i = sc * H + h
                    P.wait("tensor", s_g, hist[("q2", i)])
                    P.wait("tensor", s_v, b0_ready)
                    P.wait("tensor", gq_x, rel_ready)
                    if ("ogm", i - 2) in hist:
                        P.wait("tensor", s_v, hist[("ogm", i - 2)])
                    mm(psC[sl][:, :], S_bf[:, h * 128:(h + 1) * 128], q2bu[sl][:, :],
                       True, False)
                    mm(psC[sl][:, :], ident[:, :], obr[:, h * AS:(h + 1) * AS],
                       False, True, inc=True)
                    hist[("cmm", i)] = t_mm.v

                def w1_og(h):
                    sl = h % 2
                    i = sc * H + h
                    P.wait("vector", t_mm, hist[("cmm", i)])
                    P.wait("vector", gq_x, rel_ready)
                    if h == 0 and ("ogn", sc * H - 1) in hist:
                        P.wait("vector", s_g, hist[("ogn", sc * H - 1)])
                    if h == 0 and ("wolast", sc - 1) in hist:
                        P.wait("vector", t_mm, hist[("wolast", sc - 1)])
                    def fn(e, sl=sl, h=h):
                        with nc.allow_low_precision(reason="gated o as bf16 operand"):
                            e.tensor_mul(og_all[:, h * AS:(h + 1) * AS], psC[sl][:, :],
                                         sigr[:, h * AS:(h + 1) * AS]).then_inc(s_v.h, 1)
                    P.on("vector", fn)
                    s_v.bump()
                    hist[("ogm", i)] = s_v.v

                def w1_sq(h):
                    sl = h % 2
                    i = sc * H + h
                    P.wait("gpsimd", s_v, hist[("ogm", i)])
                    if ("csum", i - 2) in hist:
                        P.wait("gpsimd", t_mm, hist[("csum", i - 2)])
                    P.on("gpsimd", lambda e, sl=sl, h=h: e.tensor_mul(
                        sqh[sl][:, :], og_all[:, h * AS:(h + 1) * AS],
                        og_all[:, h * AS:(h + 1) * AS]).then_inc(s_g.h, 1))
                    s_g.bump()
                    hist[("sqv", i)] = s_g.v
                    P.wait("tensor", s_g, hist[("sqv", i)])
                    if ("rme", i - 2) in hist:
                        P.wait("tensor", s_v, hist[("rme", i - 2)])
                    mm(psR[sl][0:1, :], ones_bf[:, 0:1], sqh[sl][:, :], True, True, inc=True)
                    hist[("csum", i)] = t_mm.v
                    P.wait("vector", t_mm, hist[("csum", i)])
                    if h == 0 and ("rsq", sc * 4 - 1) in hist:
                        P.wait("vector", s_sc, hist[("rsq", sc * 4 - 1)])
                    def fn(e, sl=sl, h=h):
                        with nc.allow_low_precision(reason="variance as bf16 for rsqrt"):
                            e.tensor_scalar(rrh_all[0:1, h * AS:(h + 1) * AS],
                                            psR[sl][0:1, :], 1.0 / HD, EPS,
                                            ALU.mult, ALU.add).then_inc(s_v.h, 1)
                    P.on("vector", fn)
                    s_v.bump()
                    hist[("rme", i)] = s_v.v
                    if h % 4 == 3:
                        grp = h // 4
                        P.wait("scalar", s_v, hist[("rme", i)])
                        P.on("scalar", lambda e, grp=grp: e.activation(
                            rrh_all[0:1, grp * 4 * AS:(grp + 1) * 4 * AS],
                            rrh_all[0:1, grp * 4 * AS:(grp + 1) * 4 * AS], AF.Ln))
                        P.drain("scalar")
                        P.on("scalar", lambda e, grp=grp: e.activation(
                            rrh_all[0:1, grp * 4 * AS:(grp + 1) * 4 * AS],
                            rrh_all[0:1, grp * 4 * AS:(grp + 1) * 4 * AS], AF.Exp,
                            scale=-0.5).then_inc(s_sc.h, 1))
                        s_sc.bump()
                        hist[("rsq", sc * 4 + grp)] = s_sc.v

                for step in range(H + 3):
                    if step < H:
                        w1_q2(step)
                    if 0 <= step - 1 < H:
                        w1_corr(step - 1)
                    if 0 <= step - 2 < H:
                        w1_og(step - 2)
                    if 0 <= step - 3 < H:
                        w1_sq(step - 3)

                def w2_bcast(h):
                    sl = h % 2
                    i = sc * H + h
                    P.wait("tensor", s_sc, hist[("rsq", sc * 4 + h // 4)])
                    if ("rfc", i - 2) in hist:
                        P.wait("tensor", s_sc, hist[("rfc", i - 2)])
                    mm(psB[sl][:, :], ones_bf[0:1, 0:128],
                       rrh_all[0:1, h * AS:(h + 1) * AS], True, True, inc=True)
                    hist[("bmm", i)] = t_mm.v

                def w2_copy(h):
                    sl = h % 2
                    i = sc * H + h
                    P.wait("scalar", t_mm, hist[("bmm", i)])
                    if ("ogn", i - 2) in hist:
                        P.wait("scalar", s_g, hist[("ogn", i - 2)])
                    def fn(e, sl=sl):
                        with nc.allow_low_precision(reason="1/rms factor in bf16"):
                            e.copy(rrf[sl][:, :], psB[sl][:, :]).then_inc(s_sc.h, 1)
                    P.on("scalar", fn)
                    s_sc.bump()
                    hist[("rfc", i)] = s_sc.v

                def w2_ogn(h):
                    sl = h % 2
                    i = sc * H + h
                    P.wait("gpsimd", s_sc, hist[("rfc", i)])
                    P.on("gpsimd", lambda e, sl=sl, h=h: e.tensor_mul(
                        ogn_fm[:, h * AS:(h + 1) * AS], og_all[:, h * AS:(h + 1) * AS],
                        rrf[sl][:, :]).then_inc(s_g.h, 1))
                    s_g.bump()
                    hist[("ogn", i)] = s_g.v

                for step in range(H + 2):
                    if step < H:
                        w2_bcast(step)
                    if 0 <= step - 1 < H:
                        w2_copy(step - 1)
                    if 0 <= step - 2 < H:
                        w2_ogn(step - 2)

                hist[("rel", sc)] = (hist[("cmm", sc * H + H - 1)],
                                     hist[("ogm", sc * H + H - 1)])
                ognT_ready = hist[("ogn", sc * H + H - 1)]

                for eb in range(NQB):
                    wb_i = eb % 2
                    gwo = sc * NQB + eb
                    P.wait("sync", gq_o, wo_scaled)
                    if ("womm", gwo - 2) in hist:
                        P.wait("sync", t_mm, hist[("womm", gwo - 2)])
                    P.dma_slot("sync", aqw[wb_i], wob2[wb_i][:, :], wo_s[eb])
                    abI = 0
                    prev_cons = hist.get(("b3xcons", abI))
                    if prev_cons:
                        P.wait("sync", s_v, prev_cons)
                    P.dma_slot("sync", xas[abI], xadd[abI][:, :].rearrange("p (a b) -> p a b", a=4),
                               src[sc * AS:(sc + 1) * AS,
                                   eb * QBW:(eb + 1) * QBW].rearrange("(t p) w -> p t w", p=128))
                    P.wait("tensor", aqw[wb_i])
                    P.wait("tensor", s_g, ognT_ready)
                    wob = wob2[wb_i]
                    for c in range(NCHS):
                        nw_ = wctr[0]
                        wctr[0] += 1
                        pa = pb3[nw_ % 2]
                        if ("wo", nw_ - 2) in hist:
                            P.wait("tensor", s_v, hist[("wo", nw_ - 2)])
                        for dc in range(16):
                            mm(pa[:, :], ogn_fm[:, dc * AS + c * 128: dc * AS + (c + 1) * 128],
                               wob[:, dc * QBW:(dc + 1) * QBW], dc == 0, dc == 15, inc=(dc == 15))
                        womm = t_mm.v
                        P.wait("vector", t_mm, womm)
                        P.wait("vector", xas[abI])
                        P.wait("vector", xts[abI])
                        P.on("vector", lambda e, pa=pa, abI=abI, c=c: e.tensor_add(
                            xtmp[abI][:, c * QBW:(c + 1) * QBW], pa[:, :],
                            xadd[abI][:, c * QBW:(c + 1) * QBW]).then_inc(s_v.h, 1))
                        s_v.bump()
                        hist[("wo", nw_)] = s_v.v
                    hist[("womm", gwo)] = t_mm.v
                    hist[("b3xcons", abI)] = s_v.v
                    P.wait("sync", s_v, hist[("b3xcons", abI)])
                    P.dma_slot("sync", xts[abI],
                               dst[sc * AS:(sc + 1) * AS,
                                   eb * QBW:(eb + 1) * QBW].rearrange("(t p) w -> p t w", p=128),
                               xtmp[abI][:, :].rearrange("p (a b) -> p a b", a=4))
                hist[("wolast", sc)] = t_mm.v
            b3.close()
            aes.close()

        prologue()
        have_attn = "attn" in stages
        have_f2 = "ffn2" in stages
        if "ffn1" in stages:
            ffn(x_in, x_cur if (have_attn or have_f2) else y_out, join=False)
        conv_issue(len(conv_jobs))
        if have_attn:
            attn(x_cur if "ffn1" in stages else x_in, x_cur if have_f2 else y_out)
        if have_f2:
            ffn(x_cur if ("ffn1" in stages or have_attn) else x_in, y_out)

        for sem in P.sems:
            P.wait("sync", sem)

        with nc.Block() as block:
            P.emit(block)

    return nc


def kernel(L0_attn_x, L0_attn_log_decay, L0_attn_Wq, L0_attn_Wk, L0_attn_Wv,
           L0_attn_Wo, L0_attn_Wg1, L0_attn_Wg2, L0_attn_norm_w,
           L0_ffn_gate_w, L0_ffn_up_w, L0_ffn_down_w,
           L0_ffn_gate_scale, L0_ffn_up_scale, L0_ffn_down_scale,
           _stages=None):
    if _stages is None:
        _stages = STAGES
    x = np.ascontiguousarray(np.asarray(L0_attn_x, dtype=np.float32).reshape(NCORE * T, D))
    ld = np.asarray(L0_attn_log_decay, dtype=np.float32)
    asc = np.ascontiguousarray

    common = {
        "gw": asc(np.asarray(L0_ffn_gate_w, np.float32)),
        "uw": asc(np.asarray(L0_ffn_up_w, np.float32)),
        "dw": asc(np.asarray(L0_ffn_down_w, np.float32)),
        "wq": asc(np.asarray(L0_attn_Wq, np.float32)),
        "wk": asc(np.asarray(L0_attn_Wk, np.float32)),
        "wv": asc(np.asarray(L0_attn_Wv, np.float32)),
        "wo": asc(np.asarray(L0_attn_Wo, np.float32)),
        "wg1": asc(np.asarray(L0_attn_Wg1, np.float32)),
        "wg2": asc(np.asarray(L0_attn_Wg2, np.float32)),
        "ld_b": asc(np.broadcast_to(ld.reshape(1, H), (128, H))),
        "gs_pp": asc(np.asarray(L0_ffn_gate_scale, np.float32).reshape(NIC, 128).T),
        "us_pp": asc(np.asarray(L0_ffn_up_scale, np.float32).reshape(NIC, 128).T),
        "ds_b": asc(np.broadcast_to(np.asarray(L0_ffn_down_scale, np.float32).reshape(1, D), (128, D))),
        "nw_pp": asc(np.asarray(L0_attn_norm_w, np.float32).reshape(H, 128).T),
    }
    in_maps = []
    for c in range(NCORE):
        pw = np.zeros(NCORE, np.float32)
        mk = np.zeros(NCORE, np.float32)
        for cp in range(c):
            pw[cp] = 2048.0 * (c - 1 - cp)
            mk[cp] = 1.0
        m = dict(common)
        m["x"] = asc(x[c * T:(c + 1) * T, :])
        m["pw_b"] = asc(np.broadcast_to(pw.reshape(1, NCORE), (128, NCORE)))
        m["mk_b"] = asc(np.broadcast_to(mk.reshape(1, NCORE), (128, NCORE)))
        in_maps.append(m)

    nc = build_nc(_stages)
    res = run_bass_kernel_spmd(nc, in_maps, core_ids=list(range(NCORE)))
    out = np.concatenate([res.results[c]["y"] for c in range(NCORE)], axis=0)
    return out.reshape(1, NCORE * T, D).astype(np.float32)


# version 19
# speedup vs baseline: 1.1004x; 1.0044x over previous
import numpy as np
from contextlib import ExitStack

import concourse.bass as bass
import concourse.bass_utils as _bass_utils
import concourse.mybir as mybir
from concourse.bass_utils import run_bass_kernel_spmd

if not getattr(_bass_utils, "_ldw_opt_patched", False):
    _orig_run_command = _bass_utils.run_command

    def _run_command_ldw(cmd, **kw):
        if isinstance(cmd, list):
            cmd = ["--enable-ldw-opt=true" if c == "--enable-ldw-opt=false" else c
                   for c in cmd]
        return _orig_run_command(cmd, **kw)

    _bass_utils.run_command = _run_command_ldw
    _bass_utils._ldw_opt_patched = True

F32 = mybir.dt.float32
BF16 = mybir.dt.bfloat16
F8 = mybir.dt.float8e4
AF = mybir.ActivationFunctionType
ALU = mybir.AluOpType
AX = mybir.AxisListType

NCORE = 8
T = 2048
D = 2048
I = 6144
NIC = I // 128
H = 16
HD = 128
CH = 128
TS = 512
NSC = T // TS
AS = 512
NAS = T // AS
NCHS = AS // CH
IBW = 256
NIB = I // IBW
EBW = 512
NEB = D // EBW
QBW = 512
NQB = D // QBW
EPS = 1e-5

ENGINES = ("sync", "gpsimd", "scalar", "vector", "tensor")
STAGES = ("ffn1", "attn", "ffn2")


class Sem:
    def __init__(self, nc, es, name):
        self.h = es.enter_context(nc.semaphore(name))
        self.v = 0

    def bump(self, n=1):
        self.v += n
        return self.v


class Plan:
    def __init__(self):
        self.ev = {e: [] for e in ENGINES}
        self.last_wait = {}
        self.sems = []

    def on(self, eng, fn):
        self.ev[eng].append(fn)

    def wait(self, eng, sem, val=None):
        v = sem.v if val is None else val
        if v <= 0:
            return
        key = (eng, id(sem))
        if self.last_wait.get(key, -1) >= v:
            return
        self.last_wait[key] = v
        self.on(eng, lambda e, s=sem.h, v=v: e.wait_ge(s, v))

    def join(self, engines=ENGINES):
        for eng in engines:
            for sem in self.sems:
                self.wait(eng, sem)

    def dma(self, eng, sem, out, in_):
        self.wait(eng, sem)
        self.on(eng, lambda e, o=out, i=in_: e.dma_start(out=o, in_=i).then_inc(sem.h, 16))
        return sem.bump(16)

    def dma_slot(self, eng, sem, out, in_):
        self.on(eng, lambda e, o=out, i=in_: e.dma_start(out=o, in_=i).then_inc(sem.h, 16))
        return sem.bump(16)

    def wait_marker(self, eng, marker):
        for sem, val in marker:
            self.wait(eng, sem, val)

    def drain(self, eng):
        self.on(eng, lambda e: e.drain())

    def fence(self, eng, sem):
        self.on(eng, lambda e: e.drain())
        self.on(eng, lambda e, s=sem.h: e.nop().then_inc(s, 1))
        return sem.bump()

    def emit(self, block):
        for eng in ENGINES:
            evs = self.ev[eng]

            def body(e, evs=evs):
                for fn in evs:
                    fn(e)

            getattr(block, eng)(body)


def build_nc(stages=("ffn1", "attn", "ffn2")):
    nc = bass.Bass("TRN2", target_bir_lowering=False)

    x_in = nc.dram_tensor("x", [T, D], F32, kind="ExternalInput").ap()
    gw_in = nc.dram_tensor("gw", [D, I], F32, kind="ExternalInput").ap()
    uw_in = nc.dram_tensor("uw", [D, I], F32, kind="ExternalInput").ap()
    dw_in = nc.dram_tensor("dw", [I, D], F32, kind="ExternalInput").ap()
    wq_in = nc.dram_tensor("wq", [D, D], F32, kind="ExternalInput").ap()
    wk_in = nc.dram_tensor("wk", [D, D], F32, kind="ExternalInput").ap()
    wv_in = nc.dram_tensor("wv", [D, D], F32, kind="ExternalInput").ap()
    wo_in = nc.dram_tensor("wo", [D, D], F32, kind="ExternalInput").ap()
    wg1_in = nc.dram_tensor("wg1", [D, HD], F32, kind="ExternalInput").ap()
    wg2_in = nc.dram_tensor("wg2", [HD, D], F32, kind="ExternalInput").ap()
    ld_b_in = nc.dram_tensor("ld_b", [128, H], F32, kind="ExternalInput").ap()
    gs_in = nc.dram_tensor("gs_pp", [128, NIC], F32, kind="ExternalInput").ap()
    us_in = nc.dram_tensor("us_pp", [128, NIC], F32, kind="ExternalInput").ap()
    ds_in = nc.dram_tensor("ds_b", [128, D], F32, kind="ExternalInput").ap()
    nw_in = nc.dram_tensor("nw_pp", [128, H], F32, kind="ExternalInput").ap()
    pw_in = nc.dram_tensor("pw_b", [128, NCORE], F32, kind="ExternalInput").ap()
    mk_in = nc.dram_tensor("mk_b", [128, NCORE], F32, kind="ExternalInput").ap()
    y_out = nc.dram_tensor("y", [T, D], F32, kind="ExternalOutput").ap()

    x_cur = nc.dram_tensor("x_cur", [T, D], F32).ap()
    gw_s = nc.dram_tensor("gw_s", [NIB, 128, 16 * IBW], F8).ap()
    uw_s = nc.dram_tensor("uw_s", [NIB, 128, 16 * IBW], F8).ap()
    dw_s = nc.dram_tensor("dw_s", [NEB * 6, 128, 8 * EBW], F8).ap()
    wq_s = nc.dram_tensor("wq_s", [NQB, 128, 16 * QBW], BF16).ap()
    wk_s = nc.dram_tensor("wk_s", [NQB, 128, 16 * QBW], BF16).ap()
    wv_s = nc.dram_tensor("wv_s", [NQB, 128, 16 * QBW], BF16).ap()
    wo_s = nc.dram_tensor("wo_s", [NQB, 128, 16 * QBW], BF16).ap()
    sig_sp = nc.dram_tensor("sig_sp", [NAS, 128, H * AS], BF16).ap()
    q_sp = nc.dram_tensor("q_sp", [NAS, 128, H * AS], BF16).ap()
    o_sp = nc.dram_tensor("o_sp", [NAS, 128, H * AS], BF16).ap()
    kv_b_in = nc.dram_tensor("kv_b_in", [128, H * HD], F32)
    kv_b_out = nc.dram_tensor("kv_b_out", [NCORE * 128, H * HD], F32)

    es = ExitStack()
    with es:
        ident = es.enter_context(nc.sbuf_tensor("ident", [128, 128], BF16))

        P = Plan()
        mksem = lambda name: Sem(nc, es, name)
        sq_w = mksem("sq_w")
        gq_x = mksem("gq_x")
        gq_o = mksem("gq_o")
        gq_c = mksem("gq_c")
        s_cc = mksem("s_cc")
        t_mm = mksem("t_mm")
        s_sc = mksem("s_sc")
        s_v = mksem("s_v")
        s_g = mksem("s_g")
        cvs = [mksem(f"cvs{i}") for i in range(3)]
        cvq = mksem("cvq")
        wgs = [mksem(f"wgs{i}") for i in range(2)]
        dws = [mksem(f"dws{i}") for i in range(2)]
        sxl = [mksem(f"sxl{i}") for i in range(2)]
        xas = [mksem(f"xas{i}") for i in range(2)]
        xts = [mksem(f"xts{i}") for i in range(2)]
        aqw = [mksem(f"aqw{i}") for i in range(2)]
        P.sems = [sq_w, gq_x, gq_o, gq_c, s_cc, t_mm, s_sc, s_v, s_g, cvq] + cvs + wgs + dws + sxl + xas + xts + aqw

        conv_jobs = []
        for ib in range(NIB):
            conv_jobs.append((gw_s[ib].rearrange("p (a w) -> p a w", a=16),
                              gw_in[:, ib * IBW:(ib + 1) * IBW].rearrange("(a p) w -> p a w", p=128)))
            conv_jobs.append((uw_s[ib].rearrange("p (a w) -> p a w", a=16),
                              uw_in[:, ib * IBW:(ib + 1) * IBW].rearrange("(a p) w -> p a w", p=128)))
        for eb in range(NEB):
            for ig in range(6):
                conv_jobs.append((dw_s[eb * 6 + ig].rearrange("p (a w) -> p a w", a=8),
                                  dw_in[ig * 1024:(ig + 1) * 1024, eb * EBW:(eb + 1) * EBW]
                                  .rearrange("(a p) w -> p a w", p=128)))
        N_FFN_CONV = len(conv_jobs)
        for w_src, w_dst in ((wq_in, wq_s), (wk_in, wk_s), (wv_in, wv_s), (wo_in, wo_s)):
            for eb in range(NQB):
                conv_jobs.append((w_dst[eb].rearrange("p (a w) -> p a w", a=16),
                                  w_src[:, eb * QBW:(eb + 1) * QBW].rearrange("(a p) w -> p a w", p=128)))
        conv_pos = [0]

        def conv_issue(n):
            for _ in range(n):
                j = conv_pos[0]
                if j >= len(conv_jobs):
                    return
                sem = cvs[j % 3] if j < N_FFN_CONV else cvq
                dst, src = conv_jobs[j]
                P.dma_slot("gpsimd", sem, dst, src)
                conv_pos[0] += 1

        def conv_wait(eng):
            for s in cvs:
                P.wait(eng, s)

        def prologue():
            def gp(e):
                e.memset(ident[:, :], 0.0)
                e.drain()
                e.affine_select(out=ident[:, :], in_=ident[:, :], compare_op=ALU.not_equal,
                                fill=1.0, base=0, pattern=[[-1, 128]], channel_multiplier=1)
                e.drain()
            P.on("gpsimd", gp)
            P.fence("gpsimd", s_g)
            conv_issue(4)

        ln_ctr = [0]
        sv_hist = {}

        ffn_id = [0]

        def ffn(src, dst, join=True):
            if join:
                P.join()
            tag = f"f{ffn_id[0]}"
            ffn_id[0] += 1
            fes = ExitStack()
            sb = lambda n, s, d: fes.enter_context(nc.sbuf_tensor(tag + n, s, d))
            pt = lambda n, s, d: fes.enter_context(nc.psum_tensor(tag + n, s, d))
            gs_pp = sb("gs_pp_t", [128, NIC], F32)
            us_pp = sb("us_pp_t", [128, NIC], F32)
            ds_b = sb("ds_b_t", [128, D], F32)
            xln = [sb(f"xln{i}", [128, D], F32) for i in range(2)]
            ln_tm = [sb(f"ln_tm{i}", [128, D], BF16) for i in range(4)]
            st2 = [{n: sb(f"st{i}_" + n, [128, 1], F32)
                    for n in ("sum", "ssq", "mean", "nmean", "tv", "vpe", "sqv", "inv", "mi")}
                   for i in range(2)]
            ln_fm = sb("ln_fm", [128, 16 * TS], BF16)
            fuse = sb("fuse", [128, NIC * TS], BF16)
            gwb = [sb(f"gwb{i}", [128, 16 * IBW], BF16) for i in range(2)]
            uwb = [sb(f"uwb{i}", [128, 16 * IBW], BF16) for i in range(2)]
            dwb = [sb(f"dwb{i}", [128, 8 * EBW], BF16) for i in range(2)]
            gact = sb("gact", [128, TS], BF16)
            xadd = [sb(f"xadd{i}", [128, 4 * EBW], F32) for i in range(2)]
            xtmp = [sb(f"xtmp{i}", [128, 4 * EBW], F32) for i in range(2)]
            ptr = [pt(f"ptr{i}", [128, 8 * 128], BF16) for i in range(2)]
            pg = pt("pg", [128, 512], F32)
            pu = pt("pu", [128, 512], F32)
            pd = [pt(f"pd{t}", [128, 512], F32) for t in range(4)]

            for dstb, srca in ((gs_pp, gs_in), (us_pp, us_in), (ds_b, ds_in)):
                P.dma("sync", sq_w, dstb[:, :], srca)
            scales_ready = sq_w.v

            ln_fm3 = ln_fm[:, :].rearrange("p (a b) -> p a b", a=16)

            def ln_tile(row0, bi, out_tm, tt_guard=0):
                key = ln_ctr[0]
                ln_ctr[0] += 1
                st = st2[bi]
                xb = xln[bi]
                prev = sv_hist.get(key - 2)
                if prev:
                    P.wait("scalar", s_v, prev)
                P.dma_slot("scalar", sxl[bi], xb[:, :], src[row0:row0 + 128, :])
                P.wait("vector", sxl[bi])
                P.on("vector", lambda e, xb=xb: e.reduce_sum(
                    st["sum"][:, :], xb[:, :], axis=AX.X).then_inc(s_v.h, 1))
                s_v.bump()
                P.wait("scalar", sxl[bi])
                if tt_guard:
                    P.wait("scalar", t_mm, tt_guard)
                P.on("scalar", lambda e, xb=xb, o=out_tm: e.activation(
                    o[:, :], xb[:, :], AF.Square, accum_out=st["ssq"][:, :]))
                P.drain("scalar")
                P.wait("scalar", s_v)
                P.on("scalar", lambda e: e.activation(st["mean"][:, :], st["sum"][:, :], AF.Copy, scale=1.0 / D))
                P.on("scalar", lambda e: e.activation(st["nmean"][:, :], st["sum"][:, :], AF.Copy, scale=-1.0 / D))
                P.on("scalar", lambda e: e.activation(st["tv"][:, :], st["ssq"][:, :], AF.Copy,
                                                      scale=1.0 / D, bias=EPS))
                P.fence("scalar", s_sc)
                P.wait("vector", s_sc)
                P.on("vector", lambda e: e.scalar_tensor_tensor(
                    st["vpe"][:, :], st["nmean"][:, :], st["mean"][:, 0:1], st["tv"][:, :],
                    ALU.mult, ALU.add).then_inc(s_v.h, 1))
                s_v.bump()
                P.wait("scalar", s_v)
                P.on("scalar", lambda e: e.activation(
                    st["sqv"][:, :], st["vpe"][:, :], AF.Sqrt).then_inc(s_sc.h, 1))
                s_sc.bump()
                P.wait("vector", s_sc)
                P.on("vector", lambda e: e.reciprocal(st["inv"][:, :], st["sqv"][:, :]))
                P.drain("vector")
                P.on("vector", lambda e: e.tensor_scalar(
                    st["mi"][:, :], st["nmean"][:, :], st["inv"][:, 0:1], None, ALU.mult))
                P.drain("vector")
                if tt_guard:
                    P.wait("vector", t_mm, tt_guard)
                P.on("vector", lambda e, xb=xb, o=out_tm: e.tensor_scalar(
                    o[:, :], xb[:, :], st["inv"][:, 0:1], st["mi"][:, 0:1],
                    ALU.mult, ALU.add).then_inc(s_v.h, 1))
                s_v.bump()
                sv_hist[key] = s_v.v
                return s_v.v

            def mm(out, lhsT, rhs, start, stop, inc=False):
                if inc:
                    P.on("tensor", lambda e, o=out, l=lhsT, r=rhs, a=start, b=stop:
                         e.matmul(o, l, r, start=a, stop=b).then_inc(t_mm.h, 1))
                    t_mm.bump()
                else:
                    P.on("tensor", lambda e, o=out, l=lhsT, r=rhs, a=start, b=stop:
                         e.matmul(o, l, r, start=a, stop=b))

            first_wload = [True]
            first_dload = [True]
            P.wait("tensor", s_g, 1)
            ln_ready_at = {}

            def ln_compute(sc, tguards):
                vals = []
                for t in range(TS // 128):
                    tg = tguards[t] if tguards else 0
                    vals.append(ln_tile(sc * TS + t * 128, t % 2, ln_tm[t % 4], tg))
                return vals

            def ln_transpose(sc, vals, gate_done_val):
                tgs = []
                for t in range(TS // 128):
                    P.wait("tensor", s_v, vals[t])
                    for half in range(2):
                        ptb = ptr[half]
                        P.wait("tensor", s_sc)
                        for dci in range(8):
                            dc = half * 8 + dci
                            if dci < 7:
                                P.on("tensor", lambda e, ptb=ptb, b=ln_tm[t % 4], dc=dc, dci=dci:
                                     e.transpose(ptb[:, dci * 128:(dci + 1) * 128],
                                                 b[:, dc * 128:(dc + 1) * 128], ident[:, :]))
                            else:
                                P.on("tensor", lambda e, ptb=ptb, b=ln_tm[t % 4], dc=dc, dci=dci:
                                     e.transpose(ptb[:, dci * 128:(dci + 1) * 128],
                                                 b[:, dc * 128:(dc + 1) * 128],
                                                 ident[:, :]).then_inc(t_mm.h, 1))
                                t_mm.bump()
                        P.wait("scalar", t_mm)
                        if gate_done_val:
                            P.wait("scalar", t_mm, gate_done_val)
                        P.on("scalar", lambda e, ptb=ptb, half=half, t=t: e.copy(
                            ln_fm3[:, half * 8:(half + 1) * 8, t * 128:(t + 1) * 128],
                            ptb[:, :].rearrange("p (a b) -> p a b", a=8)).then_inc(s_sc.h, 1))
                        s_sc.bump()
                    tgs.append(t_mm.v)
                ln_ready_at[sc] = s_sc.v
                return tgs

            tg_prev = ln_transpose(0, ln_compute(0, None), 0)
            hist = {}
            for sc in range(NSC):
                ln_ready = ln_ready_at[sc]

                for ib in range(NIB):
                    bi = ib % 2
                    gib = sc * NIB + ib
                    conv_wait("gpsimd")
                    if gib >= 2:
                        P.wait("gpsimd", t_mm, hist[("mm", gib - 2)])
                    P.dma_slot("gpsimd", wgs[bi], gwb[bi][:, :], gw_s[ib])
                    P.dma_slot("gpsimd", wgs[bi], uwb[bi][:, :], uw_s[ib])
                    conv_issue(3)
                    P.wait("tensor", wgs[bi])
                    P.wait("tensor", s_sc, ln_ready)
                    for i128 in range(IBW // 128):
                        ic = ib * (IBW // 128) + i128
                        gic = sc * NIC + ic
                        pga = pg if ic % 2 == 0 else pd[0]
                        pua = pu if ic % 2 == 0 else pd[1]
                        if gic >= 2:
                            P.wait("tensor", s_sc, hist[("g", gic - 2)])
                            P.wait("tensor", s_v, hist[("u", gic - 2)])
                        if ic == 1 and ("xcons", 1) in hist:
                            P.wait("tensor", s_v, hist[("xcons", 1)])
                        for dc in range(16):
                            mm(pga[:, :],
                               gwb[bi][:, dc * IBW + i128 * 128: dc * IBW + (i128 + 1) * 128],
                               ln_fm[:, dc * TS:(dc + 1) * TS], dc == 0, dc == 15)
                        for dc in range(16):
                            mm(pua[:, :],
                               uwb[bi][:, dc * IBW + i128 * 128: dc * IBW + (i128 + 1) * 128],
                               ln_fm[:, dc * TS:(dc + 1) * TS], dc == 0, dc == 15,
                               inc=(dc == 15))
                        hist[("mm", gib)] = t_mm.v
                        P.wait("scalar", t_mm)
                        P.wait("scalar", sq_w, scales_ready)
                        if gic >= 1:
                            P.wait("scalar", s_v, hist[("u", gic - 1)])
                        P.on("scalar", lambda e, ic=ic, pga=pga: e.activation(
                            gact[:, :], pga[:, :], AF.Silu,
                            scale=gs_pp[:, ic:ic + 1]).then_inc(s_sc.h, 1))
                        s_sc.bump()
                        hist[("g", gic)] = s_sc.v
                        P.wait("vector", s_sc)
                        P.on("vector", lambda e, ic=ic, pua=pua: e.scalar_tensor_tensor(
                            fuse[:, ic * TS:(ic + 1) * TS], pua[:, :], us_pp[:, ic:ic + 1],
                            gact[:, :], ALU.mult, ALU.mult).then_inc(s_v.h, 1))
                        s_v.bump()
                        hist[("u", gic)] = s_v.v
                fuse_ready = s_v.v
                gate_done = t_mm.v
                if sc + 1 < NSC:
                    ln_vals = ln_compute(sc + 1, tg_prev)

                for eb in range(NEB):
                    if eb > 0:
                        P.wait("tensor", s_v)
                    abI = eb % 2
                    prev_cons = hist.get(("xcons", abI))
                    if prev_cons:
                        P.wait("sync", s_v, prev_cons)
                    P.dma_slot("sync", xas[abI], xadd[abI][:, :].rearrange("p (a b) -> p a b", a=4),
                               src[sc * TS:(sc + 1) * TS,
                                   eb * EBW:(eb + 1) * EBW].rearrange("(t p) w -> p t w", p=128))
                    for ig in range(6):
                        bi = ig % 2
                        gdc = sc * NEB * 6 + eb * 6 + ig
                        conv_wait("gpsimd")
                        if gdc >= 2:
                            P.wait("gpsimd", t_mm, hist[("dmm", gdc - 2)])
                        P.dma_slot("gpsimd", dws[bi], dwb[bi][:, :], dw_s[eb * 6 + ig])
                        conv_issue(1)
                        P.wait("tensor", dws[bi])
                        P.wait("tensor", s_v, fuse_ready)
                        for i8 in range(8):
                            ic = ig * 8 + i8
                            for t in range(4):
                                mm(pd[t][:, :],
                                   fuse[:, ic * TS + t * 128: ic * TS + (t + 1) * 128],
                                   dwb[bi][:, i8 * EBW:(i8 + 1) * EBW],
                                   ic == 0, ic == NIC - 1,
                                   inc=(i8 == 7 and t == 3))
                        hist[("dmm", gdc)] = t_mm.v
                    P.wait("vector", xts[abI])
                    P.wait("vector", xas[abI])
                    P.wait("vector", t_mm)
                    for t in range(4):
                        P.on("vector", lambda e, abI=abI, t=t, eb=eb: e.scalar_tensor_tensor(
                            xtmp[abI][:, t * EBW:(t + 1) * EBW], pd[t][:, :], 0.5,
                            ds_b[:, eb * EBW:(eb + 1) * EBW], ALU.mult, ALU.mult))
                    P.drain("vector")
                    for t in range(4):
                        P.on("vector", lambda e, abI=abI, t=t: e.tensor_add(
                            xtmp[abI][:, t * EBW:(t + 1) * EBW], xtmp[abI][:, t * EBW:(t + 1) * EBW],
                            xadd[abI][:, t * EBW:(t + 1) * EBW]))
                    P.fence("vector", s_v)
                    hist[("xcons", abI)] = s_v.v
                    P.wait("sync", s_v, hist[("xcons", abI)])
                    P.dma_slot("sync", xts[abI],
                               dst[sc * TS:(sc + 1) * TS,
                                   eb * EBW:(eb + 1) * EBW].rearrange("(t p) w -> p t w", p=128),
                               xtmp[abI][:, :].rearrange("p (a b) -> p a b", a=4))
                if sc + 1 < NSC:
                    tg_prev = ln_transpose(sc + 1, ln_vals, gate_done)
            fes.close()

        def attn(src, dst):
            P.join()
            aes = ExitStack()
            asb = lambda n, s, d: aes.enter_context(nc.sbuf_tensor("a" + n, s, d))
            ldb = asb("ldb", [128, H], F32)
            kdt = asb("kdt", [128, H], F32)
            cdb = asb("cdb", [128, H], F32)
            pw_b = asb("pw_b", [128, NCORE], F32)
            mk_b = asb("mk_b", [128, NCORE], F32)
            Wb = asb("Wb", [128, H * NCORE], F32)
            qdt = asb("qdt", [128, H * 128], F32)
            S = asb("S", [128, H * HD], F32)
            S_bf = asb("S_bf", [128, H * HD], BF16)
            Bst = asb("Bst", [128, H * HD], F32)
            nw_pp = asb("nw_pp", [128, H], F32)
            wg1_bf = asb("wg1_bf", [128, H * 128], BF16)
            wg2_bf = asb("wg2_bf", [128, D], BF16)
            dsc1 = asb("dsc1", [128, 128], F32)
            dsc2 = asb("dsc2", [128, 128], F32)
            cdp = asb("cdp", [128, NAS * NCHS * H], F32)
            ones_bf = asb("ones_bf", [128, 128], BF16)

            def mm(out, lhsT, rhs, start, stop, inc=False):
                if inc:
                    P.on("tensor", lambda e, o=out, l=lhsT, r=rhs, a=start, b=stop:
                         e.matmul(o, l, r, start=a, stop=b).then_inc(t_mm.h, 1))
                    t_mm.bump()
                else:
                    P.on("tensor", lambda e, o=out, l=lhsT, r=rhs, a=start, b=stop:
                         e.matmul(o, l, r, start=a, stop=b))

            for dstb, srca in ((ldb, ld_b_in), (pw_b, pw_in), (mk_b, mk_in), (nw_pp, nw_in)):
                P.dma("sync", sq_w, dstb[:, :], srca)
            ses = ExitStack()
            wg1f = ses.enter_context(nc.sbuf_tensor("awg1f", [128, H * 128], F32))
            wg2f = ses.enter_context(nc.sbuf_tensor("awg2f", [128, D], F32))
            P.dma("sync", sq_w, wg1f[:, :].rearrange("p (a b) -> p a b", a=16),
                  wg1_in.rearrange("(a p) b -> p a b", p=128))
            P.dma("sync", sq_w, wg2f[:, :], wg2_in)

            def gp_masks(e):
                e.iota(dsc1[:, :], [[1, 128]], base=0, channel_multiplier=-1,
                       allow_small_or_imprecise_dtypes=True)
                e.drain()
                e.iota(dsc2[:, :], [[1, 128]], base=1, channel_multiplier=0,
                       allow_small_or_imprecise_dtypes=True)
                e.drain()
                e.memset(S[:, :], 0.0)
                e.memset(S_bf[:, :], 0.0)
                e.memset(Bst[:, :], 0.0)
                e.memset(cdp[:, 0:H], 1.0)
                e.memset(ones_bf[:, :], 1.0)
            P.on("gpsimd", gp_masks)
            P.fence("gpsimd", s_g)
            P.wait("scalar", s_g)
            P.wait("scalar", sq_w)
            P.on("scalar", lambda e: e.copy(wg1_bf[:, :], wg1f[:, :]))
            P.on("scalar", lambda e: e.copy(wg2_bf[:, :], wg2f[:, :]))
            P.on("scalar", lambda e: e.activation(cdb[:, :], ldb[:, :], AF.Exp, scale=128.0))
            P.on("scalar", lambda e: e.activation(dsc1[:, :], dsc1[:, :], AF.Relu))
            P.drain("scalar")
            for h in range(H):
                P.on("scalar", lambda e, h=h: e.activation(
                    qdt[:, h * 128:(h + 1) * 128], dsc2[:, :], AF.Exp, scale=ldb[:, h:h + 1]))
            for h in range(H):
                P.on("scalar", lambda e, h=h: e.activation(
                    Wb[:, h * NCORE:(h + 1) * NCORE], pw_b[:, :], AF.Exp, scale=ldb[:, h:h + 1]))
            P.drain("scalar")
            P.fence("scalar", s_sc)
            P.wait("vector", s_sc)
            for h in range(H):
                P.on("vector", lambda e, h=h: e.tensor_mul(
                    Wb[:, h * NCORE:(h + 1) * NCORE], Wb[:, h * NCORE:(h + 1) * NCORE], mk_b[:, :]))
            P.wait("vector", s_g)
            for g in range(1, NAS * NCHS):
                P.on("vector", lambda e, g=g: e.tensor_mul(
                    cdp[:, g * H:(g + 1) * H], cdp[:, (g - 1) * H:g * H], cdb[:, :]))
                P.drain("vector")
            P.fence("vector", s_v)
            P.wait("gpsimd", s_sc)
            P.on("gpsimd", lambda e: e.iota(dsc2[:, 0:1], [[0, 1]], base=127, channel_multiplier=-1,
                                            allow_small_or_imprecise_dtypes=True))
            P.fence("gpsimd", s_g)
            P.wait("scalar", s_g)
            for h in range(H):
                P.on("scalar", lambda e, h=h: e.activation(
                    kdt[:, h:h + 1], dsc2[:, 0:1], AF.Exp, scale=ldb[:, h:h + 1]))
            P.drain("scalar")
            P.fence("scalar", s_sc)
            ses.close()

            b1 = ExitStack()
            bsb = lambda n, s, d: b1.enter_context(nc.sbuf_tensor("b1" + n, s, d))
            bpt = lambda n, s, d: b1.enter_context(nc.psum_tensor("b1" + n, s, d))
            DT = bsb("DT", [128, H * 128], BF16)
            xln = [bsb(f"xln{i}", [128, D], F32) for i in range(2)]
            ln_tm = [bsb(f"ln_tm{i}", [128, D], BF16) for i in range(2)]
            st2 = [{n: bsb(f"st{i}_" + n, [128, 1], F32)
                    for n in ("sum", "ssq", "mean", "nmean", "tv", "vpe", "sqv", "inv", "mi")}
                   for i in range(2)]
            ln2_fm = bsb("ln2_fm", [128, 16 * AS], BF16)
            q_fm = bsb("q_fm", [128, H * AS], BF16)
            k_fm = bsb("k_fm", [128, H * AS], BF16)
            v_tm = bsb("v_tm", [128, NCHS * D], BF16)
            ktm = bsb("ktm", [128, NCHS * H * 128], BF16)
            o_fm = bsb("o_fm", [128, H * AS], BF16)
            sigh = [bsb(f"sigh{i}", [128, 4 * AS], BF16) for i in range(2)]
            g1b = bsb("g1b", [128, AS], BF16)
            q2b = [bsb(f"q2b{i}", [128, 128], BF16) for i in range(2)]
            scd = [bsb(f"scd{i}", [128, 128], BF16) for i in range(2)]
            wqb2 = [bsb(f"wqb{i}", [128, 16 * QBW], BF16) for i in range(2)]
            pban = [bpt(f"pban{i}", [128, 512], F32) for i in range(6)]
            ptrb = [bpt(f"ptrb{i}", [128, 8 * 128], BF16) for i in range(2)]

            ln2_fm3 = ln2_fm[:, :].rearrange("p (a b) -> p a b", a=16)

            def ln_tile(row0, bi, out_tm):
                key = ln_ctr[0]
                ln_ctr[0] += 1
                st = st2[bi]
                xb = xln[bi]
                prev = sv_hist.get(key - 2)
                if prev:
                    P.wait("scalar", s_v, prev)
                P.dma_slot("scalar", sxl[bi], xb[:, :], src[row0:row0 + 128, :])
                P.wait("vector", sxl[bi])
                P.on("vector", lambda e, xb=xb: e.reduce_sum(
                    st["sum"][:, :], xb[:, :], axis=AX.X).then_inc(s_v.h, 1))
                s_v.bump()
                P.wait("scalar", sxl[bi])
                P.on("scalar", lambda e, xb=xb, o=out_tm: e.activation(
                    o[:, :], xb[:, :], AF.Square, accum_out=st["ssq"][:, :]))
                P.drain("scalar")
                P.wait("scalar", s_v)
                P.on("scalar", lambda e: e.activation(st["mean"][:, :], st["sum"][:, :], AF.Copy, scale=1.0 / D))
                P.on("scalar", lambda e: e.activation(st["nmean"][:, :], st["sum"][:, :], AF.Copy, scale=-1.0 / D))
                P.on("scalar", lambda e: e.activation(st["tv"][:, :], st["ssq"][:, :], AF.Copy,
                                                      scale=1.0 / D, bias=EPS))
                P.drain("scalar")
                P.fence("scalar", s_sc)
                P.wait("vector", s_sc)
                P.on("vector", lambda e: e.scalar_tensor_tensor(
                    st["vpe"][:, :], st["nmean"][:, :], st["mean"][:, 0:1], st["tv"][:, :],
                    ALU.mult, ALU.add).then_inc(s_v.h, 1))
                s_v.bump()
                P.wait("scalar", s_v)
                P.on("scalar", lambda e: e.activation(
                    st["sqv"][:, :], st["vpe"][:, :], AF.Sqrt).then_inc(s_sc.h, 1))
                s_sc.bump()
                P.wait("vector", s_sc)
                P.on("vector", lambda e: e.reciprocal(st["inv"][:, :], st["sqv"][:, :]))
                P.drain("vector")
                P.on("vector", lambda e: e.tensor_scalar(
                    st["mi"][:, :], st["nmean"][:, :], st["inv"][:, 0:1], None, ALU.mult))
                P.drain("vector")
                P.on("vector", lambda e, xb=xb, o=out_tm: e.tensor_scalar(
                    o[:, :], xb[:, :], st["inv"][:, 0:1], st["mi"][:, 0:1],
                    ALU.mult, ALU.add).then_inc(s_v.h, 1))
                s_v.bump()
                sv_hist[key] = s_v.v

            P.wait("scalar", s_sc)
            for h in range(H):
                P.on("scalar", lambda e, h=h: e.activation(
                    DT[:, h * 128:(h + 1) * 128], dsc1[:, :], AF.Exp, scale=ldb[:, h:h + 1]))
            P.drain("scalar")
            P.fence("scalar", s_sc)
            P.wait("gpsimd", s_sc)

            def gp_dtmask(e):
                for h in range(H):
                    e.affine_select(out=DT[:, h * 128:(h + 1) * 128],
                                    in_=DT[:, h * 128:(h + 1) * 128],
                                    compare_op=ALU.is_ge, fill=0.0, base=0,
                                    pattern=[[1, 128]], channel_multiplier=-1)
                e.drain()
            P.on("gpsimd", gp_dtmask)
            P.fence("gpsimd", s_g)

            hist = {}
            pctr = [0]
            vctr = [0]
            wctr2 = [0]
            for sc in range(NAS):
                for t in range(NCHS):
                    ln_tile(sc * AS + t * 128, t % 2, ln_tm[t % 2])
                    P.wait("tensor", s_v)
                    for half in range(2):
                        ptb = ptrb[half]
                        P.wait("tensor", s_sc)
                        for dci in range(8):
                            dc = half * 8 + dci
                            mmfn = (lambda e, ptb=ptb, b=ln_tm[t % 2], dc=dc, dci=dci:
                                    e.transpose(ptb[:, dci * 128:(dci + 1) * 128],
                                                b[:, dc * 128:(dc + 1) * 128], ident[:, :]))
                            if dci < 7:
                                P.on("tensor", mmfn)
                            else:
                                P.on("tensor", lambda e, f=mmfn: f(e).then_inc(t_mm.h, 1))
                                t_mm.bump()
                        P.wait("scalar", t_mm)
                        P.on("scalar", lambda e, ptb=ptb, half=half, t=t: e.copy(
                            ln2_fm3[:, half * 8:(half + 1) * 8, t * 128:(t + 1) * 128],
                            ptb[:, :].rearrange("p (a b) -> p a b", a=8)).then_inc(s_sc.h, 1))
                        s_sc.bump()
                ln2_ready = s_sc.v

                if ("g1r", sc - 1) in hist:
                    P.wait("tensor", s_sc, hist[("g1r", sc - 1)])
                if ("qsp", sc - 1) in hist:
                    P.wait("scalar", gq_o, hist[("qsp", sc - 1)])
                def proj_fm(w_s, out_buf, act, nm):
                    for eb in range(NQB):
                        wb_i = (wctr2[0] % 2)
                        wctr2[0] += 1
                        wqb = wqb2[wb_i]
                        if ("wq", wctr2[0] - 3) in hist:
                            P.wait("sync", t_mm, hist[("wq", wctr2[0] - 3)])
                        else:
                            P.wait("sync", t_mm)
                        P.dma_slot("sync", aqw[wb_i], wqb[:, :], w_s[eb])
                        P.wait("tensor", aqw[wb_i])
                        P.wait("tensor", s_sc, ln2_ready)
                        for e128 in range(4):
                            ec = eb * 4 + e128
                            n = pctr[0]
                            pctr[0] += 1
                            pa = pban[n % 2]
                            if ("proj", n - 2) in hist:
                                P.wait("tensor", s_sc, hist[("proj", n - 2)])
                            for dc in range(16):
                                mm(pa[:, :], wqb[:, dc * QBW + e128 * 128: dc * QBW + (e128 + 1) * 128],
                                   ln2_fm[:, dc * AS:(dc + 1) * AS], dc == 0, dc == 15,
                                   inc=(dc == 15))
                            P.wait("scalar", t_mm)
                            P.on("scalar", lambda e, pa=pa, ec=ec, o=out_buf, act=act: e.activation(
                                o[:, ec * AS:(ec + 1) * AS], pa[:, :], act).then_inc(s_sc.h, 1))
                            s_sc.bump()
                            hist[("proj", n)] = s_sc.v
                        hist[("wq", wctr2[0] - 1)] = t_mm.v
                proj_fm(wq_s, q_fm, AF.Silu, "wq")
                proj_fm(wk_s, k_fm, AF.Silu, "wk")
                q_ready = s_sc.v
                P.wait("gpsimd", s_sc, q_ready)
                P.dma("gpsimd", gq_o, q_sp[sc], q_fm[:, :])
                hist[("qsp", sc)] = gq_o.v

                if ("sgev", sc * H - 1) in hist:
                    P.wait("tensor", s_sc, hist[("sgev", sc * H - 1)])
                for eb in range(NQB):
                    wb_i = (wctr2[0] % 2)
                    wctr2[0] += 1
                    wqb = wqb2[wb_i]
                    if ("wq", wctr2[0] - 3) in hist:
                        P.wait("sync", t_mm, hist[("wq", wctr2[0] - 3)])
                    else:
                        P.wait("sync", t_mm)
                    P.dma_slot("sync", aqw[wb_i], wqb[:, :], wv_s[eb])
                    P.wait("tensor", aqw[wb_i])
                    for t in range(NCHS):
                        nv = vctr[0]
                        vctr[0] += 1
                        pa = pban[2 + (nv % 2)]
                        if ("vp", nv - 2) in hist:
                            P.wait("tensor", s_sc, hist[("vp", nv - 2)])
                        for dc in range(16):
                            mm(pa[:, :], ln2_fm[:, dc * AS + t * 128: dc * AS + (t + 1) * 128],
                               wqb[:, dc * QBW:(dc + 1) * QBW], dc == 0, dc == 15, inc=(dc == 15))
                        P.wait("scalar", t_mm)
                        P.on("scalar", lambda e, pa=pa, t=t, eb=eb: e.activation(
                            v_tm[:, t * D + eb * QBW: t * D + (eb + 1) * QBW], pa[:, :],
                            AF.Silu).then_inc(s_sc.h, 1))
                        s_sc.bump()
                        hist[("vp", nv)] = s_sc.v
                    hist[("wq", wctr2[0] - 1)] = t_mm.v
                v_ready = s_sc.v

                kq_ready = s_sc.v
                for c in range(NCHS):
                    for half in range(2):
                        ptb = ptrb[half]
                        P.wait("tensor", s_sc)
                        for hh in range(8):
                            h = half * 8 + hh
                            f = (lambda e, ptb=ptb, hh=hh, h=h, c=c:
                                 e.transpose(ptb[:, hh * 128:(hh + 1) * 128],
                                             k_fm[:, h * AS + c * 128: h * AS + (c + 1) * 128],
                                             ident[:, :]))
                            if hh < 7:
                                P.on("tensor", f)
                            else:
                                P.on("tensor", lambda e, f=f: f(e).then_inc(t_mm.h, 1))
                                t_mm.bump()
                        P.wait("scalar", t_mm)
                        for hh in range(8):
                            h = half * 8 + hh
                            last = hh == 7
                            fn = (lambda e, ptb=ptb, hh=hh, h=h, c=c: e.activation(
                                ktm[:, (c * H + h) * 128:(c * H + h + 1) * 128],
                                ptb[:, hh * 128:(hh + 1) * 128], AF.Copy,
                                scale=kdt[:, h:h + 1]))
                            if last:
                                P.on("scalar", lambda e, f=fn: f(e).then_inc(s_sc.h, 1))
                                s_sc.bump()
                            else:
                                P.on("scalar", fn)
                ktm_ready = s_sc.v

                P.wait("tensor", s_g)
                P.wait("vector", s_g)
                sitems = [(c, h) for c in range(NCHS) for h in range(H)]
                base = sc * NCHS * H

                def emit_score(idx):
                    c, h = sitems[idx]
                    i = base + idx
                    sl = i % 2
                    if ("scd", i - 2) in hist:
                        P.wait("tensor", s_v, hist[("scd", i - 2)])
                    mm(pban[sl][:, 0:128], k_fm[:, h * AS + c * 128: h * AS + (c + 1) * 128],
                       q_fm[:, h * AS + c * 128: h * AS + (c + 1) * 128], True, True, inc=True)
                    hist[("sc", i)] = t_mm.v

                emit_score(0)
                for idx, (c, h) in enumerate(sitems):
                    i = base + idx
                    sl = i % 2
                    if idx == 0 and ("osp", sc - 1) in hist:
                        P.wait("scalar", gq_o, hist[("osp", sc - 1)])
                    P.wait("vector", t_mm, hist[("sc", i)])
                    if ("omm", i - 2) in hist:
                        P.wait("vector", t_mm, hist[("omm", i - 2)])
                    P.on("vector", lambda e, sl=sl, h=h: e.tensor_mul(
                        scd[sl][:, :], pban[sl][:, 0:128], DT[:, h * 128:(h + 1) * 128]))
                    P.on("vector", lambda e, sl=sl, h=h, c=c: e.tensor_mul(
                        q2b[sl][:, :], q_fm[:, h * AS + c * 128: h * AS + (c + 1) * 128],
                        qdt[:, h * 128:(h + 1) * 128]).then_inc(s_v.h, 1))
                    s_v.bump()
                    hist[("scd", i)] = s_v.v
                    if idx + 1 < len(sitems):
                        emit_score(idx + 1)
                    P.wait("tensor", s_v, hist[("scd", i)])
                    if ("oev", i - 2) in hist:
                        P.wait("tensor", s_sc, hist[("oev", i - 2)])
                    mm(pban[2 + sl][:, 0:128],
                       v_tm[:, c * D + h * 128: c * D + (h + 1) * 128], scd[sl][:, :],
                       True, False)
                    mm(pban[2 + sl][:, 0:128], S_bf[:, h * 128:(h + 1) * 128], q2b[sl][:, :],
                       False, True, inc=True)
                    hist[("omm", i)] = t_mm.v
                    P.wait("scalar", t_mm, hist[("omm", i)])
                    P.on("scalar", lambda e, sl=sl, h=h, c=c: e.activation(
                        o_fm[:, h * AS + c * 128: h * AS + (c + 1) * 128],
                        pban[2 + sl][:, 0:128], AF.Copy).then_inc(s_sc.h, 1))
                    s_sc.bump()
                    hist[("oev", i)] = s_sc.v
                    if ("sup", i - 2) in hist:
                        P.wait("tensor", s_v, hist[("sup", i - 2)])
                    mm(pban[4 + sl][:, 0:128], ktm[:, (c * H + h) * 128:(c * H + h + 1) * 128],
                       v_tm[:, c * D + h * 128: c * D + (h + 1) * 128], True, True, inc=True)
                    hist[("kv", i)] = t_mm.v
                    P.wait("vector", t_mm, hist[("kv", i)])
                    P.on("vector", lambda e, sl=sl, h=h: e.scalar_tensor_tensor(
                        S[:, h * 128:(h + 1) * 128], S[:, h * 128:(h + 1) * 128],
                        cdb[:, h:h + 1], pban[4 + sl][:, 0:128], ALU.mult, ALU.add))
                    P.drain("vector")
                    P.on("vector", lambda e, h=h: e.tensor_copy(
                        S_bf[:, h * 128:(h + 1) * 128],
                        S[:, h * 128:(h + 1) * 128]).then_inc(s_v.h, 1))
                    s_v.bump()
                    hist[("sup", i)] = s_v.v
                if sc == NAS - 1:
                    P.wait("gpsimd", s_v)
                    P.dma("gpsimd", gq_c, kv_b_in.ap(), S[:, :])
                    P.wait("gpsimd", gq_c)
                    P.on("gpsimd", lambda e: e.collective_compute(
                        "AllGather", ALU.bypass, replica_groups=[list(range(NCORE))],
                        ins=[kv_b_in.ap()], outs=[kv_b_out.ap()]).then_inc(s_cc.h, 1))
                    s_cc.bump(1)
                last_i = base + len(sitems) - 1
                P.wait("gpsimd", s_sc, hist[("oev", last_i)])
                P.dma("gpsimd", gq_o, o_sp[sc], o_fm[:, :])
                hist[("osp", sc)] = gq_o.v

                P.wait("tensor", s_v, hist[("scd", last_i)])
                for dc in range(16):
                    mm(pban[0][:, :], wg1_bf[:, dc * 128:(dc + 1) * 128],
                       ln2_fm[:, dc * AS:(dc + 1) * AS], dc == 0, dc == 15, inc=(dc == 15))
                g1v = t_mm.v
                P.wait("scalar", t_mm, g1v)
                P.on("scalar", lambda e: e.copy(g1b[:, :], pban[0][:, :]).then_inc(s_sc.h, 1))
                s_sc.bump()
                hist[("g1r", sc)] = s_sc.v
                P.wait("tensor", s_sc, hist[("g1r", sc)])
                P.wait("tensor", s_sc, hist[("oev", last_i)])
                for h in range(H):
                    shi = (h // 4) % 2
                    pa = pban[2 + (h % 2)]
                    if ("sgev", sc * H + h - 2) in hist:
                        P.wait("tensor", s_sc, hist[("sgev", sc * H + h - 2)])
                    mm(pa[:, :], wg2_bf[:, h * 128:(h + 1) * 128], g1b[:, :],
                       True, True, inc=True)
                    P.wait("scalar", t_mm)
                    if h % 4 == 0 and ("sigsp", sc * H + h - 5) in hist:
                        P.wait("scalar", gq_o, hist[("sigsp", sc * H + h - 5)])
                    P.on("scalar", lambda e, pa=pa, shi=shi, h=h: e.activation(
                        sigh[shi][:, (h % 4) * AS:(h % 4 + 1) * AS], pa[:, :],
                        AF.Sigmoid).then_inc(s_sc.h, 1))
                    s_sc.bump()
                    hist[("sgev", sc * H + h)] = s_sc.v
                    if h % 4 == 3:
                        P.wait("gpsimd", s_sc, hist[("sgev", sc * H + h)])
                        P.dma("gpsimd", gq_o, sig_sp[sc][:, (h - 3) * AS:(h + 1) * AS],
                              sigh[shi][:, :])
                        hist[("sigsp", sc * H + h)] = gq_o.v
            b1.close()

            def b2_reduce():
                P.wait("gpsimd", s_cc)
                P.wait("gpsimd", t_mm)
                b2 = ExitStack()
                kvg = [b2.enter_context(nc.sbuf_tensor(f"kvg{i}", [128, H * HD], F32))
                       for i in range(2)]
                P.dma("gpsimd", gq_c, kvg[0][:, :], kv_b_out.ap()[0:128, :])
                lds = {0: gq_c.v}
                done = {}
                for cp in range(NCORE):
                    if cp + 1 < NCORE:
                        if cp - 1 in done:
                            P.wait("gpsimd", s_v, done[cp - 1])
                        P.dma("gpsimd", gq_c, kvg[(cp + 1) % 2][:, :],
                              kv_b_out.ap()[(cp + 1) * 128:(cp + 2) * 128, :])
                        lds[cp + 1] = gq_c.v
                    P.wait("vector", gq_c, lds[cp])
                    for h in range(H):
                        P.on("vector", lambda e, h=h, cp=cp: e.scalar_tensor_tensor(
                            Bst[:, h * 128:(h + 1) * 128], kvg[cp % 2][:, h * 128:(h + 1) * 128],
                            Wb[:, h * NCORE + cp: h * NCORE + cp + 1],
                            Bst[:, h * 128:(h + 1) * 128], ALU.mult, ALU.add))
                    P.drain("vector")
                    done[cp] = P.fence("vector", s_v)
                b2.close()
                P.on("vector", lambda e: e.tensor_copy(S_bf[:, :], Bst[:, :]).then_inc(s_v.h, 1))
                s_v.bump()
                return s_v.v

            b3 = ExitStack()
            csb = lambda n, s, d: b3.enter_context(nc.sbuf_tensor("b3" + n, s, d))
            cpt = lambda n, s, d: b3.enter_context(nc.psum_tensor("b3" + n, s, d))
            qr = csb("qr", [128, H * AS], BF16)
            obr = csb("obr", [128, H * AS], BF16)
            sigr = csb("sigr", [128, H * AS], BF16)
            qdt4 = csb("qdt4", [128, H * AS], BF16)
            og_all = csb("oga", [128, H * AS], BF16)
            ogn_fm = og_all
            sqh = [csb(f"sq{i}", [128, AS], BF16) for i in range(2)]
            q2bu = [csb(f"q2u{i}", [128, AS], BF16) for i in range(2)]
            rrh_all = csb("rra", [1, H * AS], BF16)
            rrf = [csb(f"rf{i}", [128, AS], BF16) for i in range(2)]
            wob2 = [csb(f"wob{i}", [128, 16 * QBW], BF16) for i in range(2)]
            xadd = [csb("xadd0", [128, 4 * QBW], F32)]
            xtmp = [csb("xtmp0", [128, 4 * QBW], F32)]
            psC = [cpt(f"psC{i}", [128, AS], F32) for i in range(2)]
            psR = [cpt(f"psR{i}", [1, AS], F32) for i in range(2)]
            psB = [cpt(f"psB{i}", [128, AS], F32) for i in range(2)]
            pb3 = [cpt(f"pb3{i}", [128, 512], F32) for i in range(2)]

            hist = {}
            wctr = [0]
            P.wait("gpsimd", gq_o)

            for eb in range(NQB):
                P.wait("sync", gq_o)
                P.dma("sync", aqw[0], wob2[0][:, :], wo_s[eb])
                P.wait("scalar", aqw[0])
                for dc in range(16):
                    fn = (lambda e, dc=dc: e.activation(
                        wob2[0][:, dc * QBW:(dc + 1) * QBW],
                        wob2[0][:, dc * QBW:(dc + 1) * QBW], AF.Copy,
                        scale=nw_pp[:, dc:dc + 1]))
                    if dc == 15:
                        P.on("scalar", lambda e, f=fn: f(e).then_inc(s_sc.h, 1))
                        s_sc.bump()
                    else:
                        P.on("scalar", fn)
                P.wait("gpsimd", s_sc)
                P.dma("gpsimd", gq_o, wo_s[eb], wob2[0][:, :])
            wo_scaled = gq_o.v

            P.dma("gpsimd", gq_x, qr[:, :], q_sp[0])
            P.dma("gpsimd", gq_x, obr[:, :], o_sp[0])
            P.dma("gpsimd", gq_x, sigr[:, :], sig_sp[0])
            rel0_ready = gq_x.v

            b0_ready = b2_reduce()
            for sc in range(NAS):
                if sc == 0:
                    rel_ready = rel0_ready
                else:
                    pgd = hist.get(("rel", sc - 1))
                    if pgd:
                        P.wait("gpsimd", t_mm, pgd[0])
                        P.wait("gpsimd", s_v, pgd[1])
                    P.dma("gpsimd", gq_x, qr[:, :], q_sp[sc])
                    P.dma("gpsimd", gq_x, obr[:, :], o_sp[sc])
                    P.dma("gpsimd", gq_x, sigr[:, :], sig_sp[sc])
                    rel_ready = gq_x.v

                if ("q2", sc * H - 1) in hist:
                    P.wait("scalar", s_g, hist[("q2", sc * H - 1)])
                for h in range(H):
                    for c in range(NCHS):
                        g = sc * NCHS + c
                        last = (h == H - 1 and c == NCHS - 1)
                        fn = (lambda e, h=h, c=c, g=g: e.activation(
                            qdt4[:, h * AS + c * 128: h * AS + (c + 1) * 128],
                            qdt[:, h * 128:(h + 1) * 128], AF.Copy,
                            scale=cdp[:, g * H + h: g * H + h + 1]))
                        if last:
                            P.on("scalar", lambda e, f=fn: f(e).then_inc(s_sc.h, 1))
                            s_sc.bump()
                        else:
                            P.on("scalar", fn)
                qdt4_ready = s_sc.v

                def w1_q2(h):
                    sl = h % 2
                    i = sc * H + h
                    P.wait("gpsimd", gq_x, rel_ready)
                    P.wait("gpsimd", s_sc, qdt4_ready)
                    if ("cmm", i - 2) in hist:
                        P.wait("gpsimd", t_mm, hist[("cmm", i - 2)])
                    P.on("gpsimd", lambda e, sl=sl, h=h: e.tensor_mul(
                        q2bu[sl][:, :], qr[:, h * AS:(h + 1) * AS],
                        qdt4[:, h * AS:(h + 1) * AS]).then_inc(s_g.h, 1))
                    s_g.bump()
                    hist[("q2", i)] = s_g.v

                def w1_corr(h):
                    sl = h % 2
                    i = sc * H + h
                    P.wait("tensor", s_g, hist[("q2", i)])
                    P.wait("tensor", s_v, b0_ready)
                    P.wait("tensor", gq_x, rel_ready)
                    if ("ogm", i - 2) in hist:
                        P.wait("tensor", s_v, hist[("ogm", i - 2)])
                    mm(psC[sl][:, :], S_bf[:, h * 128:(h + 1) * 128], q2bu[sl][:, :],
                       True, False)
                    mm(psC[sl][:, :], ident[:, :], obr[:, h * AS:(h + 1) * AS],
                       False, True, inc=True)
                    hist[("cmm", i)] = t_mm.v

                def w1_og(h):
                    sl = h % 2
                    i = sc * H + h
                    P.wait("vector", t_mm, hist[("cmm", i)])
                    P.wait("vector", gq_x, rel_ready)
                    if h == 0 and ("ogn", sc * H - 1) in hist:
                        P.wait("vector", s_g, hist[("ogn", sc * H - 1)])
                    if h == 0 and ("wolast", sc - 1) in hist:
                        P.wait("vector", t_mm, hist[("wolast", sc - 1)])
                    def fn(e, sl=sl, h=h):
                        with nc.allow_low_precision(reason="gated o as bf16 operand"):
                            e.tensor_mul(og_all[:, h * AS:(h + 1) * AS], psC[sl][:, :],
                                         sigr[:, h * AS:(h + 1) * AS]).then_inc(s_v.h, 1)
                    P.on("vector", fn)
                    s_v.bump()
                    hist[("ogm", i)] = s_v.v

                def w1_sq(h):
                    sl = h % 2
                    i = sc * H + h
                    P.wait("gpsimd", s_v, hist[("ogm", i)])
                    if ("csum", i - 2) in hist:
                        P.wait("gpsimd", t_mm, hist[("csum", i - 2)])
                    P.on("gpsimd", lambda e, sl=sl, h=h: e.tensor_mul(
                        sqh[sl][:, :], og_all[:, h * AS:(h + 1) * AS],
                        og_all[:, h * AS:(h + 1) * AS]).then_inc(s_g.h, 1))
                    s_g.bump()
                    hist[("sqv", i)] = s_g.v
                    P.wait("tensor", s_g, hist[("sqv", i)])
                    if ("rme", i - 2) in hist:
                        P.wait("tensor", s_v, hist[("rme", i - 2)])
                    mm(psR[sl][0:1, :], ones_bf[:, 0:1], sqh[sl][:, :], True, True, inc=True)
                    hist[("csum", i)] = t_mm.v
                    P.wait("vector", t_mm, hist[("csum", i)])
                    if h == 0 and ("rsq", sc * 4 - 1) in hist:
                        P.wait("vector", s_sc, hist[("rsq", sc * 4 - 1)])
                    def fn(e, sl=sl, h=h):
                        with nc.allow_low_precision(reason="variance as bf16 for rsqrt"):
                            e.tensor_scalar(rrh_all[0:1, h * AS:(h + 1) * AS],
                                            psR[sl][0:1, :], 1.0 / HD, EPS,
                                            ALU.mult, ALU.add).then_inc(s_v.h, 1)
                    P.on("vector", fn)
                    s_v.bump()
                    hist[("rme", i)] = s_v.v
                    if h % 4 == 3:
                        grp = h // 4
                        P.wait("scalar", s_v, hist[("rme", i)])
                        P.on("scalar", lambda e, grp=grp: e.activation(
                            rrh_all[0:1, grp * 4 * AS:(grp + 1) * 4 * AS],
                            rrh_all[0:1, grp * 4 * AS:(grp + 1) * 4 * AS], AF.Ln))
                        P.drain("scalar")
                        P.on("scalar", lambda e, grp=grp: e.activation(
                            rrh_all[0:1, grp * 4 * AS:(grp + 1) * 4 * AS],
                            rrh_all[0:1, grp * 4 * AS:(grp + 1) * 4 * AS], AF.Exp,
                            scale=-0.5).then_inc(s_sc.h, 1))
                        s_sc.bump()
                        hist[("rsq", sc * 4 + grp)] = s_sc.v

                for step in range(H + 3):
                    if step < H:
                        w1_q2(step)
                    if 0 <= step - 1 < H:
                        w1_corr(step - 1)
                    if 0 <= step - 2 < H:
                        w1_og(step - 2)
                    if 0 <= step - 3 < H:
                        w1_sq(step - 3)

                def w2_bcast(h):
                    sl = h % 2
                    i = sc * H + h
                    P.wait("tensor", s_sc, hist[("rsq", sc * 4 + h // 4)])
                    if ("rfc", i - 2) in hist:
                        P.wait("tensor", s_sc, hist[("rfc", i - 2)])
                    mm(psB[sl][:, :], ones_bf[0:1, 0:128],
                       rrh_all[0:1, h * AS:(h + 1) * AS], True, True, inc=True)
                    hist[("bmm", i)] = t_mm.v

                def w2_copy(h):
                    sl = h % 2
                    i = sc * H + h
                    P.wait("scalar", t_mm, hist[("bmm", i)])
                    if ("ogn", i - 2) in hist:
                        P.wait("scalar", s_g, hist[("ogn", i - 2)])
                    def fn(e, sl=sl):
                        with nc.allow_low_precision(reason="1/rms factor in bf16"):
                            e.copy(rrf[sl][:, :], psB[sl][:, :]).then_inc(s_sc.h, 1)
                    P.on("scalar", fn)
                    s_sc.bump()
                    hist[("rfc", i)] = s_sc.v

                def w2_ogn(h):
                    sl = h % 2
                    i = sc * H + h
                    P.wait("gpsimd", s_sc, hist[("rfc", i)])
                    P.on("gpsimd", lambda e, sl=sl, h=h: e.tensor_mul(
                        ogn_fm[:, h * AS:(h + 1) * AS], og_all[:, h * AS:(h + 1) * AS],
                        rrf[sl][:, :]).then_inc(s_g.h, 1))
                    s_g.bump()
                    hist[("ogn", i)] = s_g.v

                for step in range(H + 2):
                    if step < H:
                        w2_bcast(step)
                    if 0 <= step - 1 < H:
                        w2_copy(step - 1)
                    if 0 <= step - 2 < H:
                        w2_ogn(step - 2)

                hist[("rel", sc)] = (hist[("cmm", sc * H + H - 1)],
                                     hist[("ogm", sc * H + H - 1)])
                ognT_ready = hist[("ogn", sc * H + H - 1)]

                for eb in range(NQB):
                    wb_i = eb % 2
                    gwo = sc * NQB + eb
                    P.wait("sync", gq_o, wo_scaled)
                    if ("womm", gwo - 2) in hist:
                        P.wait("sync", t_mm, hist[("womm", gwo - 2)])
                    P.dma_slot("sync", aqw[wb_i], wob2[wb_i][:, :], wo_s[eb])
                    abI = 0
                    prev_cons = hist.get(("b3xcons", abI))
                    if prev_cons:
                        P.wait("sync", s_v, prev_cons)
                    P.dma_slot("sync", xas[abI], xadd[abI][:, :].rearrange("p (a b) -> p a b", a=4),
                               src[sc * AS:(sc + 1) * AS,
                                   eb * QBW:(eb + 1) * QBW].rearrange("(t p) w -> p t w", p=128))
                    P.wait("tensor", aqw[wb_i])
                    P.wait("tensor", s_g, ognT_ready)
                    wob = wob2[wb_i]
                    for c in range(NCHS):
                        nw_ = wctr[0]
                        wctr[0] += 1
                        pa = pb3[nw_ % 2]
                        if ("wo", nw_ - 2) in hist:
                            P.wait("tensor", s_v, hist[("wo", nw_ - 2)])
                        for dc in range(16):
                            mm(pa[:, :], ogn_fm[:, dc * AS + c * 128: dc * AS + (c + 1) * 128],
                               wob[:, dc * QBW:(dc + 1) * QBW], dc == 0, dc == 15, inc=(dc == 15))
                        womm = t_mm.v
                        P.wait("vector", t_mm, womm)
                        P.wait("vector", xas[abI])
                        P.wait("vector", xts[abI])
                        P.on("vector", lambda e, pa=pa, abI=abI, c=c: e.tensor_add(
                            xtmp[abI][:, c * QBW:(c + 1) * QBW], pa[:, :],
                            xadd[abI][:, c * QBW:(c + 1) * QBW]).then_inc(s_v.h, 1))
                        s_v.bump()
                        hist[("wo", nw_)] = s_v.v
                    hist[("womm", gwo)] = t_mm.v
                    hist[("b3xcons", abI)] = s_v.v
                    P.wait("sync", s_v, hist[("b3xcons", abI)])
                    P.dma_slot("sync", xts[abI],
                               dst[sc * AS:(sc + 1) * AS,
                                   eb * QBW:(eb + 1) * QBW].rearrange("(t p) w -> p t w", p=128),
                               xtmp[abI][:, :].rearrange("p (a b) -> p a b", a=4))
                hist[("wolast", sc)] = t_mm.v
            b3.close()
            aes.close()

        prologue()
        have_attn = "attn" in stages
        have_f2 = "ffn2" in stages
        if "ffn1" in stages:
            ffn(x_in, x_cur if (have_attn or have_f2) else y_out, join=False)
        conv_issue(len(conv_jobs))
        if have_attn:
            attn(x_cur if "ffn1" in stages else x_in, x_cur if have_f2 else y_out)
        if have_f2:
            ffn(x_cur if ("ffn1" in stages or have_attn) else x_in, y_out)

        for sem in P.sems:
            P.wait("sync", sem)

        with nc.Block() as block:
            P.emit(block)

    return nc


def kernel(L0_attn_x, L0_attn_log_decay, L0_attn_Wq, L0_attn_Wk, L0_attn_Wv,
           L0_attn_Wo, L0_attn_Wg1, L0_attn_Wg2, L0_attn_norm_w,
           L0_ffn_gate_w, L0_ffn_up_w, L0_ffn_down_w,
           L0_ffn_gate_scale, L0_ffn_up_scale, L0_ffn_down_scale,
           _stages=None):
    if _stages is None:
        _stages = STAGES
    x = np.ascontiguousarray(np.asarray(L0_attn_x, dtype=np.float32).reshape(NCORE * T, D))
    ld = np.asarray(L0_attn_log_decay, dtype=np.float32)
    asc = np.ascontiguousarray

    common = {
        "gw": asc(np.asarray(L0_ffn_gate_w, np.float32)),
        "uw": asc(np.asarray(L0_ffn_up_w, np.float32)),
        "dw": asc(np.asarray(L0_ffn_down_w, np.float32)),
        "wq": asc(np.asarray(L0_attn_Wq, np.float32)),
        "wk": asc(np.asarray(L0_attn_Wk, np.float32)),
        "wv": asc(np.asarray(L0_attn_Wv, np.float32)),
        "wo": asc(np.asarray(L0_attn_Wo, np.float32)),
        "wg1": asc(np.asarray(L0_attn_Wg1, np.float32)),
        "wg2": asc(np.asarray(L0_attn_Wg2, np.float32)),
        "ld_b": asc(np.broadcast_to(ld.reshape(1, H), (128, H))),
        "gs_pp": asc(np.asarray(L0_ffn_gate_scale, np.float32).reshape(NIC, 128).T),
        "us_pp": asc(np.asarray(L0_ffn_up_scale, np.float32).reshape(NIC, 128).T),
        "ds_b": asc(np.broadcast_to(np.asarray(L0_ffn_down_scale, np.float32).reshape(1, D), (128, D))),
        "nw_pp": asc(np.asarray(L0_attn_norm_w, np.float32).reshape(H, 128).T),
    }
    in_maps = []
    for c in range(NCORE):
        pw = np.zeros(NCORE, np.float32)
        mk = np.zeros(NCORE, np.float32)
        for cp in range(c):
            pw[cp] = 2048.0 * (c - 1 - cp)
            mk[cp] = 1.0
        m = dict(common)
        m["x"] = asc(x[c * T:(c + 1) * T, :])
        m["pw_b"] = asc(np.broadcast_to(pw.reshape(1, NCORE), (128, NCORE)))
        m["mk_b"] = asc(np.broadcast_to(mk.reshape(1, NCORE), (128, NCORE)))
        in_maps.append(m)

    nc = build_nc(_stages)
    res = run_bass_kernel_spmd(nc, in_maps, core_ids=list(range(NCORE)))
    out = np.concatenate([res.results[c]["y"] for c in range(NCORE)], axis=0)
    return out.reshape(1, NCORE * T, D).astype(np.float32)


# version 20 (final)
# speedup vs baseline: 1.1251x; 1.0225x over previous
import numpy as np
from contextlib import ExitStack

import concourse.bass as bass
import concourse.bass_utils as _bass_utils
import concourse.mybir as mybir
from concourse.bass_utils import run_bass_kernel_spmd

if not getattr(_bass_utils, "_ldw_opt_patched", False):
    _orig_run_command = _bass_utils.run_command

    def _run_command_ldw(cmd, **kw):
        if isinstance(cmd, list):
            cmd = ["--enable-ldw-opt=true" if c == "--enable-ldw-opt=false" else c
                   for c in cmd]
        return _orig_run_command(cmd, **kw)

    _bass_utils.run_command = _run_command_ldw
    _bass_utils._ldw_opt_patched = True

F32 = mybir.dt.float32
BF16 = mybir.dt.bfloat16
F8 = mybir.dt.float8e4
AF = mybir.ActivationFunctionType
ALU = mybir.AluOpType
AX = mybir.AxisListType

NCORE = 8
T = 2048
D = 2048
I = 6144
NIC = I // 128
H = 16
HD = 128
CH = 128
TS = 512
NSC = T // TS
AS = 512
NAS = T // AS
NCHS = AS // CH
IBW = 256
NIB = I // IBW
EBW = 512
NEB = D // EBW
QBW = 512
NQB = D // QBW
EPS = 1e-5

ENGINES = ("sync", "gpsimd", "scalar", "vector", "tensor")
STAGES = ("ffn1", "attn", "ffn2")


class Sem:
    def __init__(self, nc, es, name):
        self.h = es.enter_context(nc.semaphore(name))
        self.v = 0

    def bump(self, n=1):
        self.v += n
        return self.v


class Plan:
    def __init__(self):
        self.ev = {e: [] for e in ENGINES}
        self.last_wait = {}
        self.sems = []

    def on(self, eng, fn):
        self.ev[eng].append(fn)

    def wait(self, eng, sem, val=None):
        v = sem.v if val is None else val
        if v <= 0:
            return
        key = (eng, id(sem))
        if self.last_wait.get(key, -1) >= v:
            return
        self.last_wait[key] = v
        self.on(eng, lambda e, s=sem.h, v=v: e.wait_ge(s, v))

    def join(self, engines=ENGINES):
        for eng in engines:
            for sem in self.sems:
                self.wait(eng, sem)

    def dma(self, eng, sem, out, in_):
        self.wait(eng, sem)
        self.on(eng, lambda e, o=out, i=in_: e.dma_start(out=o, in_=i).then_inc(sem.h, 16))
        return sem.bump(16)

    def dma_slot(self, eng, sem, out, in_):
        self.on(eng, lambda e, o=out, i=in_: e.dma_start(out=o, in_=i).then_inc(sem.h, 16))
        return sem.bump(16)

    def wait_marker(self, eng, marker):
        for sem, val in marker:
            self.wait(eng, sem, val)

    def drain(self, eng):
        self.on(eng, lambda e: e.drain())

    def fence(self, eng, sem):
        self.on(eng, lambda e: e.drain())
        self.on(eng, lambda e, s=sem.h: e.nop().then_inc(s, 1))
        return sem.bump()

    def emit(self, block):
        for eng in ENGINES:
            evs = self.ev[eng]

            def body(e, evs=evs):
                for fn in evs:
                    fn(e)

            getattr(block, eng)(body)


def build_nc(stages=("ffn1", "attn", "ffn2")):
    nc = bass.Bass("TRN2", target_bir_lowering=False)

    x_in = nc.dram_tensor("x", [T, D], F32, kind="ExternalInput").ap()
    gw_in = nc.dram_tensor("gw", [D, I], F32, kind="ExternalInput").ap()
    uw_in = nc.dram_tensor("uw", [D, I], F32, kind="ExternalInput").ap()
    dw_in = nc.dram_tensor("dw", [I, D], F32, kind="ExternalInput").ap()
    wq_in = nc.dram_tensor("wq", [D, D], F32, kind="ExternalInput").ap()
    wk_in = nc.dram_tensor("wk", [D, D], F32, kind="ExternalInput").ap()
    wv_in = nc.dram_tensor("wv", [D, D], F32, kind="ExternalInput").ap()
    wo_in = nc.dram_tensor("wo", [D, D], F32, kind="ExternalInput").ap()
    wg1_in = nc.dram_tensor("wg1", [D, HD], F32, kind="ExternalInput").ap()
    wg2_in = nc.dram_tensor("wg2", [HD, D], F32, kind="ExternalInput").ap()
    ld_b_in = nc.dram_tensor("ld_b", [128, H], F32, kind="ExternalInput").ap()
    gs_in = nc.dram_tensor("gs_pp", [128, NIC], F32, kind="ExternalInput").ap()
    us_in = nc.dram_tensor("us_pp", [128, NIC], F32, kind="ExternalInput").ap()
    ds_in = nc.dram_tensor("ds_b", [128, D], F32, kind="ExternalInput").ap()
    nw_in = nc.dram_tensor("nw_pp", [128, H], F32, kind="ExternalInput").ap()
    pw_in = nc.dram_tensor("pw_b", [128, NCORE], F32, kind="ExternalInput").ap()
    mk_in = nc.dram_tensor("mk_b", [128, NCORE], F32, kind="ExternalInput").ap()
    y_out = nc.dram_tensor("y", [T, D], F32, kind="ExternalOutput").ap()

    x_cur = nc.dram_tensor("x_cur", [T, D], F32).ap()
    gw_s = nc.dram_tensor("gw_s", [NIB, 128, 16 * IBW], F8).ap()
    uw_s = nc.dram_tensor("uw_s", [NIB, 128, 16 * IBW], F8).ap()
    dw_s = nc.dram_tensor("dw_s", [NEB * 6, 128, 8 * EBW], F8).ap()
    wq_s = nc.dram_tensor("wq_s", [NQB, 128, 16 * QBW], BF16).ap()
    wk_s = nc.dram_tensor("wk_s", [NQB, 128, 16 * QBW], BF16).ap()
    wv_s = nc.dram_tensor("wv_s", [NQB, 128, 16 * QBW], BF16).ap()
    wo_s = nc.dram_tensor("wo_s", [NQB, 128, 16 * QBW], BF16).ap()
    sig_sp = nc.dram_tensor("sig_sp", [NAS, 128, H * AS], BF16).ap()
    q_sp = nc.dram_tensor("q_sp", [NAS, 128, H * AS], BF16).ap()
    o_sp = nc.dram_tensor("o_sp", [NAS, 128, H * AS], BF16).ap()
    kv_b_in = nc.dram_tensor("kv_b_in", [128, H * HD], F32)
    kv_b_out = nc.dram_tensor("kv_b_out", [NCORE * 128, H * HD], F32)

    es = ExitStack()
    with es:
        ident = es.enter_context(nc.sbuf_tensor("ident", [128, 128], BF16))

        P = Plan()
        mksem = lambda name: Sem(nc, es, name)
        sq_w = mksem("sq_w")
        gq_x = mksem("gq_x")
        gq_o = mksem("gq_o")
        gq_c = mksem("gq_c")
        s_cc = mksem("s_cc")
        t_mm = mksem("t_mm")
        s_sc = mksem("s_sc")
        s_v = mksem("s_v")
        s_g = mksem("s_g")
        cvs = [mksem(f"cvs{i}") for i in range(3)]
        cvq = mksem("cvq")
        wgs = [mksem(f"wgs{i}") for i in range(2)]
        dws = [mksem(f"dws{i}") for i in range(2)]
        sxl = [mksem(f"sxl{i}") for i in range(2)]
        xas = [mksem(f"xas{i}") for i in range(2)]
        xts = [mksem(f"xts{i}") for i in range(2)]
        aqw = [mksem(f"aqw{i}") for i in range(2)]
        P.sems = [sq_w, gq_x, gq_o, gq_c, s_cc, t_mm, s_sc, s_v, s_g, cvq] + cvs + wgs + dws + sxl + xas + xts + aqw

        conv_jobs = []
        for ib in range(NIB):
            conv_jobs.append((gw_s[ib].rearrange("p (a w) -> p a w", a=16),
                              gw_in[:, ib * IBW:(ib + 1) * IBW].rearrange("(a p) w -> p a w", p=128)))
            conv_jobs.append((uw_s[ib].rearrange("p (a w) -> p a w", a=16),
                              uw_in[:, ib * IBW:(ib + 1) * IBW].rearrange("(a p) w -> p a w", p=128)))
        for eb in range(NEB):
            for ig in range(6):
                conv_jobs.append((dw_s[eb * 6 + ig].rearrange("p (a w) -> p a w", a=8),
                                  dw_in[ig * 1024:(ig + 1) * 1024, eb * EBW:(eb + 1) * EBW]
                                  .rearrange("(a p) w -> p a w", p=128)))
        N_FFN_CONV = len(conv_jobs)
        for w_src, w_dst in ((wq_in, wq_s), (wk_in, wk_s), (wv_in, wv_s), (wo_in, wo_s)):
            for eb in range(NQB):
                conv_jobs.append((w_dst[eb].rearrange("p (a w) -> p a w", a=16),
                                  w_src[:, eb * QBW:(eb + 1) * QBW].rearrange("(a p) w -> p a w", p=128)))
        conv_pos = [0]

        def conv_issue(n, qmax=0):
            nq = 0
            for _ in range(n):
                j = conv_pos[0]
                if j >= len(conv_jobs):
                    return
                if j >= N_FFN_CONV:
                    if nq >= qmax:
                        return
                    nq += 1
                sem = cvs[j % 3] if j < N_FFN_CONV else cvq
                dst, src = conv_jobs[j]
                P.dma_slot("gpsimd", sem, dst, src)
                conv_pos[0] += 1

        def conv_wait(eng):
            for s in cvs:
                P.wait(eng, s)

        def prologue():
            def gp(e):
                e.memset(ident[:, :], 0.0)
                e.drain()
                e.affine_select(out=ident[:, :], in_=ident[:, :], compare_op=ALU.not_equal,
                                fill=1.0, base=0, pattern=[[-1, 128]], channel_multiplier=1)
                e.drain()
            P.on("gpsimd", gp)
            P.fence("gpsimd", s_g)
            conv_issue(4)

        ln_ctr = [0]
        sv_hist = {}

        ffn_id = [0]

        def ffn(src, dst, join=True):
            if join:
                P.join()
            tag = f"f{ffn_id[0]}"
            ffn_id[0] += 1
            fes = ExitStack()
            sb = lambda n, s, d: fes.enter_context(nc.sbuf_tensor(tag + n, s, d))
            pt = lambda n, s, d: fes.enter_context(nc.psum_tensor(tag + n, s, d))
            gs_pp = sb("gs_pp_t", [128, NIC], F32)
            us_pp = sb("us_pp_t", [128, NIC], F32)
            ds_b = sb("ds_b_t", [128, D], F32)
            xln = [sb(f"xln{i}", [128, D], F32) for i in range(2)]
            ln_tm = [sb(f"ln_tm{i}", [128, D], BF16) for i in range(4)]
            st2 = [{n: sb(f"st{i}_" + n, [128, 1], F32)
                    for n in ("sum", "ssq", "mean", "nmean", "tv", "vpe", "sqv", "inv", "mi")}
                   for i in range(2)]
            ln_fm = sb("ln_fm", [128, 16 * TS], BF16)
            fuse = sb("fuse", [128, NIC * TS], BF16)
            gwb = [sb(f"gwb{i}", [128, 16 * IBW], BF16) for i in range(2)]
            uwb = [sb(f"uwb{i}", [128, 16 * IBW], BF16) for i in range(2)]
            dwb = [sb(f"dwb{i}", [128, 8 * EBW], BF16) for i in range(2)]
            gact = sb("gact", [128, TS], BF16)
            xadd = [sb(f"xadd{i}", [128, 4 * EBW], F32) for i in range(2)]
            xtmp = [sb(f"xtmp{i}", [128, 4 * EBW], F32) for i in range(2)]
            ptr = [pt(f"ptr{i}", [128, 8 * 128], BF16) for i in range(2)]
            pg = pt("pg", [128, 512], F32)
            pu = pt("pu", [128, 512], F32)
            pd = [pt(f"pd{t}", [128, 512], F32) for t in range(4)]

            for dstb, srca in ((gs_pp, gs_in), (us_pp, us_in), (ds_b, ds_in)):
                P.dma("sync", sq_w, dstb[:, :], srca)
            scales_ready = sq_w.v

            ln_fm3 = ln_fm[:, :].rearrange("p (a b) -> p a b", a=16)

            def ln_tile(row0, bi, out_tm, tt_guard=0):
                key = ln_ctr[0]
                ln_ctr[0] += 1
                st = st2[bi]
                xb = xln[bi]
                prev = sv_hist.get(key - 2)
                if prev:
                    P.wait("scalar", s_v, prev)
                P.dma_slot("scalar", sxl[bi], xb[:, :], src[row0:row0 + 128, :])
                P.wait("vector", sxl[bi])
                P.on("vector", lambda e, xb=xb: e.reduce_sum(
                    st["sum"][:, :], xb[:, :], axis=AX.X).then_inc(s_v.h, 1))
                s_v.bump()
                P.wait("scalar", sxl[bi])
                if tt_guard:
                    P.wait("scalar", t_mm, tt_guard)
                P.on("scalar", lambda e, xb=xb, o=out_tm: e.activation(
                    o[:, :], xb[:, :], AF.Square, accum_out=st["ssq"][:, :]))
                P.drain("scalar")
                P.wait("scalar", s_v)
                P.on("scalar", lambda e: e.activation(st["mean"][:, :], st["sum"][:, :], AF.Copy, scale=1.0 / D))
                P.on("scalar", lambda e: e.activation(st["nmean"][:, :], st["sum"][:, :], AF.Copy, scale=-1.0 / D))
                P.on("scalar", lambda e: e.activation(st["tv"][:, :], st["ssq"][:, :], AF.Copy,
                                                      scale=1.0 / D, bias=EPS))
                P.fence("scalar", s_sc)
                P.wait("vector", s_sc)
                P.on("vector", lambda e: e.scalar_tensor_tensor(
                    st["vpe"][:, :], st["nmean"][:, :], st["mean"][:, 0:1], st["tv"][:, :],
                    ALU.mult, ALU.add).then_inc(s_v.h, 1))
                s_v.bump()
                P.wait("scalar", s_v)
                P.on("scalar", lambda e: e.activation(
                    st["sqv"][:, :], st["vpe"][:, :], AF.Sqrt).then_inc(s_sc.h, 1))
                s_sc.bump()
                P.wait("vector", s_sc)
                P.on("vector", lambda e: e.reciprocal(st["inv"][:, :], st["sqv"][:, :]))
                P.drain("vector")
                P.on("vector", lambda e: e.tensor_scalar(
                    st["mi"][:, :], st["nmean"][:, :], st["inv"][:, 0:1], None, ALU.mult))
                P.drain("vector")
                if tt_guard:
                    P.wait("vector", t_mm, tt_guard)
                P.on("vector", lambda e, xb=xb, o=out_tm: e.tensor_scalar(
                    o[:, :], xb[:, :], st["inv"][:, 0:1], st["mi"][:, 0:1],
                    ALU.mult, ALU.add).then_inc(s_v.h, 1))
                s_v.bump()
                sv_hist[key] = s_v.v
                return s_v.v

            def mm(out, lhsT, rhs, start, stop, inc=False):
                if inc:
                    P.on("tensor", lambda e, o=out, l=lhsT, r=rhs, a=start, b=stop:
                         e.matmul(o, l, r, start=a, stop=b).then_inc(t_mm.h, 1))
                    t_mm.bump()
                else:
                    P.on("tensor", lambda e, o=out, l=lhsT, r=rhs, a=start, b=stop:
                         e.matmul(o, l, r, start=a, stop=b))

            first_wload = [True]
            first_dload = [True]
            P.wait("tensor", s_g, 1)
            ln_ready_at = {}

            def ln_compute(sc, tguards):
                vals = []
                for t in range(TS // 128):
                    tg = tguards[t] if tguards else 0
                    vals.append(ln_tile(sc * TS + t * 128, t % 2, ln_tm[t % 4], tg))
                return vals

            def ln_transpose(sc, vals, gate_done_val):
                tgs = []
                for t in range(TS // 128):
                    P.wait("tensor", s_v, vals[t])
                    for half in range(2):
                        ptb = ptr[half]
                        P.wait("tensor", s_sc)
                        for dci in range(8):
                            dc = half * 8 + dci
                            if dci < 7:
                                P.on("tensor", lambda e, ptb=ptb, b=ln_tm[t % 4], dc=dc, dci=dci:
                                     e.transpose(ptb[:, dci * 128:(dci + 1) * 128],
                                                 b[:, dc * 128:(dc + 1) * 128], ident[:, :]))
                            else:
                                P.on("tensor", lambda e, ptb=ptb, b=ln_tm[t % 4], dc=dc, dci=dci:
                                     e.transpose(ptb[:, dci * 128:(dci + 1) * 128],
                                                 b[:, dc * 128:(dc + 1) * 128],
                                                 ident[:, :]).then_inc(t_mm.h, 1))
                                t_mm.bump()
                        P.wait("scalar", t_mm)
                        if gate_done_val:
                            P.wait("scalar", t_mm, gate_done_val)
                        P.on("scalar", lambda e, ptb=ptb, half=half, t=t: e.copy(
                            ln_fm3[:, half * 8:(half + 1) * 8, t * 128:(t + 1) * 128],
                            ptb[:, :].rearrange("p (a b) -> p a b", a=8)).then_inc(s_sc.h, 1))
                        s_sc.bump()
                    tgs.append(t_mm.v)
                ln_ready_at[sc] = s_sc.v
                return tgs

            tg_prev = ln_transpose(0, ln_compute(0, None), 0)
            hist = {}
            for sc in range(NSC):
                ln_ready = ln_ready_at[sc]

                for ib in range(NIB):
                    bi = ib % 2
                    gib = sc * NIB + ib
                    conv_wait("gpsimd")
                    if gib >= 2:
                        P.wait("gpsimd", t_mm, hist[("mm", gib - 2)])
                    P.dma_slot("gpsimd", wgs[bi], gwb[bi][:, :], gw_s[ib])
                    P.dma_slot("gpsimd", wgs[bi], uwb[bi][:, :], uw_s[ib])
                    conv_issue(2, qmax=(1 if ib % 2 == 0 else 0))
                    P.wait("tensor", wgs[bi])
                    P.wait("tensor", s_sc, ln_ready)
                    for i128 in range(IBW // 128):
                        ic = ib * (IBW // 128) + i128
                        gic = sc * NIC + ic
                        pga = pg if ic % 2 == 0 else pd[0]
                        pua = pu if ic % 2 == 0 else pd[1]
                        if gic >= 2:
                            P.wait("tensor", s_sc, hist[("g", gic - 2)])
                            P.wait("tensor", s_v, hist[("u", gic - 2)])
                        if ic == 1 and ("xcons", 1) in hist:
                            P.wait("tensor", s_v, hist[("xcons", 1)])
                        for dc in range(16):
                            mm(pga[:, :],
                               gwb[bi][:, dc * IBW + i128 * 128: dc * IBW + (i128 + 1) * 128],
                               ln_fm[:, dc * TS:(dc + 1) * TS], dc == 0, dc == 15)
                        for dc in range(16):
                            mm(pua[:, :],
                               uwb[bi][:, dc * IBW + i128 * 128: dc * IBW + (i128 + 1) * 128],
                               ln_fm[:, dc * TS:(dc + 1) * TS], dc == 0, dc == 15,
                               inc=(dc == 15))
                        hist[("mm", gib)] = t_mm.v
                        P.wait("scalar", t_mm)
                        P.wait("scalar", sq_w, scales_ready)
                        if gic >= 1:
                            P.wait("scalar", s_v, hist[("u", gic - 1)])
                        P.on("scalar", lambda e, ic=ic, pga=pga: e.activation(
                            gact[:, :], pga[:, :], AF.Silu,
                            scale=gs_pp[:, ic:ic + 1]).then_inc(s_sc.h, 1))
                        s_sc.bump()
                        hist[("g", gic)] = s_sc.v
                        P.wait("vector", s_sc)
                        P.on("vector", lambda e, ic=ic, pua=pua: e.scalar_tensor_tensor(
                            fuse[:, ic * TS:(ic + 1) * TS], pua[:, :], us_pp[:, ic:ic + 1],
                            gact[:, :], ALU.mult, ALU.mult).then_inc(s_v.h, 1))
                        s_v.bump()
                        hist[("u", gic)] = s_v.v
                fuse_ready = s_v.v
                gate_done = t_mm.v
                if sc + 1 < NSC:
                    ln_vals = ln_compute(sc + 1, tg_prev)

                for eb in range(NEB):
                    if eb > 0:
                        P.wait("tensor", s_v)
                    abI = eb % 2
                    prev_cons = hist.get(("xcons", abI))
                    if prev_cons:
                        P.wait("sync", s_v, prev_cons)
                    P.dma_slot("sync", xas[abI], xadd[abI][:, :].rearrange("p (a b) -> p a b", a=4),
                               src[sc * TS:(sc + 1) * TS,
                                   eb * EBW:(eb + 1) * EBW].rearrange("(t p) w -> p t w", p=128))
                    for ig in range(6):
                        bi = ig % 2
                        gdc = sc * NEB * 6 + eb * 6 + ig
                        conv_wait("gpsimd")
                        if gdc >= 2:
                            P.wait("gpsimd", t_mm, hist[("dmm", gdc - 2)])
                        P.dma_slot("gpsimd", dws[bi], dwb[bi][:, :], dw_s[eb * 6 + ig])
                        conv_issue(1, qmax=(1 if ig % 2 == 0 else 0))
                        P.wait("tensor", dws[bi])
                        P.wait("tensor", s_v, fuse_ready)
                        for i8 in range(8):
                            ic = ig * 8 + i8
                            for t in range(4):
                                mm(pd[t][:, :],
                                   fuse[:, ic * TS + t * 128: ic * TS + (t + 1) * 128],
                                   dwb[bi][:, i8 * EBW:(i8 + 1) * EBW],
                                   ic == 0, ic == NIC - 1,
                                   inc=(i8 == 7 and t == 3))
                        hist[("dmm", gdc)] = t_mm.v
                    P.wait("vector", xts[abI])
                    P.wait("vector", xas[abI])
                    P.wait("vector", t_mm)
                    for t in range(4):
                        P.on("vector", lambda e, abI=abI, t=t, eb=eb: e.scalar_tensor_tensor(
                            xtmp[abI][:, t * EBW:(t + 1) * EBW], pd[t][:, :], 0.5,
                            ds_b[:, eb * EBW:(eb + 1) * EBW], ALU.mult, ALU.mult))
                    P.drain("vector")
                    for t in range(4):
                        P.on("vector", lambda e, abI=abI, t=t: e.tensor_add(
                            xtmp[abI][:, t * EBW:(t + 1) * EBW], xtmp[abI][:, t * EBW:(t + 1) * EBW],
                            xadd[abI][:, t * EBW:(t + 1) * EBW]))
                    P.fence("vector", s_v)
                    hist[("xcons", abI)] = s_v.v
                    P.wait("sync", s_v, hist[("xcons", abI)])
                    P.dma_slot("sync", xts[abI],
                               dst[sc * TS:(sc + 1) * TS,
                                   eb * EBW:(eb + 1) * EBW].rearrange("(t p) w -> p t w", p=128),
                               xtmp[abI][:, :].rearrange("p (a b) -> p a b", a=4))
                if sc + 1 < NSC:
                    tg_prev = ln_transpose(sc + 1, ln_vals, gate_done)
            fes.close()

        def attn(src, dst):
            P.join()
            aes = ExitStack()
            asb = lambda n, s, d: aes.enter_context(nc.sbuf_tensor("a" + n, s, d))
            ldb = asb("ldb", [128, H], F32)
            kdt = asb("kdt", [128, H], F32)
            cdb = asb("cdb", [128, H], F32)
            pw_b = asb("pw_b", [128, NCORE], F32)
            mk_b = asb("mk_b", [128, NCORE], F32)
            Wb = asb("Wb", [128, H * NCORE], F32)
            qdt = asb("qdt", [128, H * 128], F32)
            S = asb("S", [128, H * HD], F32)
            S_bf = asb("S_bf", [128, H * HD], BF16)
            Bst = asb("Bst", [128, H * HD], F32)
            nw_pp = asb("nw_pp", [128, H], F32)
            wg1_bf = asb("wg1_bf", [128, H * 128], BF16)
            wg2_bf = asb("wg2_bf", [128, D], BF16)
            dsc1 = asb("dsc1", [128, 128], F32)
            dsc2 = asb("dsc2", [128, 128], F32)
            cdp = asb("cdp", [128, NAS * NCHS * H], F32)
            ones_bf = asb("ones_bf", [128, 128], BF16)

            def mm(out, lhsT, rhs, start, stop, inc=False):
                if inc:
                    P.on("tensor", lambda e, o=out, l=lhsT, r=rhs, a=start, b=stop:
                         e.matmul(o, l, r, start=a, stop=b).then_inc(t_mm.h, 1))
                    t_mm.bump()
                else:
                    P.on("tensor", lambda e, o=out, l=lhsT, r=rhs, a=start, b=stop:
                         e.matmul(o, l, r, start=a, stop=b))

            for dstb, srca in ((ldb, ld_b_in), (pw_b, pw_in), (mk_b, mk_in), (nw_pp, nw_in)):
                P.dma("sync", sq_w, dstb[:, :], srca)
            ses = ExitStack()
            wg1f = ses.enter_context(nc.sbuf_tensor("awg1f", [128, H * 128], F32))
            wg2f = ses.enter_context(nc.sbuf_tensor("awg2f", [128, D], F32))
            P.dma("sync", sq_w, wg1f[:, :].rearrange("p (a b) -> p a b", a=16),
                  wg1_in.rearrange("(a p) b -> p a b", p=128))
            P.dma("sync", sq_w, wg2f[:, :], wg2_in)

            def gp_masks(e):
                e.iota(dsc1[:, :], [[1, 128]], base=0, channel_multiplier=-1,
                       allow_small_or_imprecise_dtypes=True)
                e.drain()
                e.iota(dsc2[:, :], [[1, 128]], base=1, channel_multiplier=0,
                       allow_small_or_imprecise_dtypes=True)
                e.drain()
                e.memset(S[:, :], 0.0)
                e.memset(S_bf[:, :], 0.0)
                e.memset(Bst[:, :], 0.0)
                e.memset(cdp[:, 0:H], 1.0)
                e.memset(ones_bf[:, :], 1.0)
            P.on("gpsimd", gp_masks)
            P.fence("gpsimd", s_g)
            P.wait("scalar", s_g)
            P.wait("scalar", sq_w)
            P.on("scalar", lambda e: e.copy(wg1_bf[:, :], wg1f[:, :]))
            P.on("scalar", lambda e: e.copy(wg2_bf[:, :], wg2f[:, :]))
            P.on("scalar", lambda e: e.activation(cdb[:, :], ldb[:, :], AF.Exp, scale=128.0))
            P.on("scalar", lambda e: e.activation(dsc1[:, :], dsc1[:, :], AF.Relu))
            P.drain("scalar")
            for h in range(H):
                P.on("scalar", lambda e, h=h: e.activation(
                    qdt[:, h * 128:(h + 1) * 128], dsc2[:, :], AF.Exp, scale=ldb[:, h:h + 1]))
            for h in range(H):
                P.on("scalar", lambda e, h=h: e.activation(
                    Wb[:, h * NCORE:(h + 1) * NCORE], pw_b[:, :], AF.Exp, scale=ldb[:, h:h + 1]))
            P.drain("scalar")
            P.fence("scalar", s_sc)
            P.wait("vector", s_sc)
            for h in range(H):
                P.on("vector", lambda e, h=h: e.tensor_mul(
                    Wb[:, h * NCORE:(h + 1) * NCORE], Wb[:, h * NCORE:(h + 1) * NCORE], mk_b[:, :]))
            P.wait("vector", s_g)
            for g in range(1, NAS * NCHS):
                P.on("vector", lambda e, g=g: e.tensor_mul(
                    cdp[:, g * H:(g + 1) * H], cdp[:, (g - 1) * H:g * H], cdb[:, :]))
                P.drain("vector")
            P.fence("vector", s_v)
            P.wait("gpsimd", s_sc)
            P.on("gpsimd", lambda e: e.iota(dsc2[:, 0:1], [[0, 1]], base=127, channel_multiplier=-1,
                                            allow_small_or_imprecise_dtypes=True))
            P.fence("gpsimd", s_g)
            P.wait("scalar", s_g)
            for h in range(H):
                P.on("scalar", lambda e, h=h: e.activation(
                    kdt[:, h:h + 1], dsc2[:, 0:1], AF.Exp, scale=ldb[:, h:h + 1]))
            P.drain("scalar")
            P.fence("scalar", s_sc)
            ses.close()

            b1 = ExitStack()
            bsb = lambda n, s, d: b1.enter_context(nc.sbuf_tensor("b1" + n, s, d))
            bpt = lambda n, s, d: b1.enter_context(nc.psum_tensor("b1" + n, s, d))
            DT = bsb("DT", [128, H * 128], BF16)
            xln = [bsb(f"xln{i}", [128, D], F32) for i in range(2)]
            ln_tm = [bsb(f"ln_tm{i}", [128, D], BF16) for i in range(2)]
            st2 = [{n: bsb(f"st{i}_" + n, [128, 1], F32)
                    for n in ("sum", "ssq", "mean", "nmean", "tv", "vpe", "sqv", "inv", "mi")}
                   for i in range(2)]
            ln2_fm = bsb("ln2_fm", [128, 16 * AS], BF16)
            q_fm = bsb("q_fm", [128, H * AS], BF16)
            k_fm = bsb("k_fm", [128, H * AS], BF16)
            v_tm = bsb("v_tm", [128, NCHS * D], BF16)
            ktm = bsb("ktm", [128, NCHS * H * 128], BF16)
            o_fm = bsb("o_fm", [128, H * AS], BF16)
            sigh = [bsb(f"sigh{i}", [128, 4 * AS], BF16) for i in range(2)]
            g1b = bsb("g1b", [128, AS], BF16)
            q2b = [bsb(f"q2b{i}", [128, 128], BF16) for i in range(2)]
            scd = [bsb(f"scd{i}", [128, 128], BF16) for i in range(2)]
            wqb2 = [bsb(f"wqb{i}", [128, 16 * QBW], BF16) for i in range(2)]
            pban = [bpt(f"pban{i}", [128, 512], F32) for i in range(6)]
            ptrb = [bpt(f"ptrb{i}", [128, 8 * 128], BF16) for i in range(2)]

            ln2_fm3 = ln2_fm[:, :].rearrange("p (a b) -> p a b", a=16)

            def ln_tile(row0, bi, out_tm):
                key = ln_ctr[0]
                ln_ctr[0] += 1
                st = st2[bi]
                xb = xln[bi]
                prev = sv_hist.get(key - 2)
                if prev:
                    P.wait("scalar", s_v, prev)
                P.dma_slot("scalar", sxl[bi], xb[:, :], src[row0:row0 + 128, :])
                P.wait("vector", sxl[bi])
                P.on("vector", lambda e, xb=xb: e.reduce_sum(
                    st["sum"][:, :], xb[:, :], axis=AX.X).then_inc(s_v.h, 1))
                s_v.bump()
                P.wait("scalar", sxl[bi])
                P.on("scalar", lambda e, xb=xb, o=out_tm: e.activation(
                    o[:, :], xb[:, :], AF.Square, accum_out=st["ssq"][:, :]))
                P.drain("scalar")
                P.wait("scalar", s_v)
                P.on("scalar", lambda e: e.activation(st["mean"][:, :], st["sum"][:, :], AF.Copy, scale=1.0 / D))
                P.on("scalar", lambda e: e.activation(st["nmean"][:, :], st["sum"][:, :], AF.Copy, scale=-1.0 / D))
                P.on("scalar", lambda e: e.activation(st["tv"][:, :], st["ssq"][:, :], AF.Copy,
                                                      scale=1.0 / D, bias=EPS))
                P.drain("scalar")
                P.fence("scalar", s_sc)
                P.wait("vector", s_sc)
                P.on("vector", lambda e: e.scalar_tensor_tensor(
                    st["vpe"][:, :], st["nmean"][:, :], st["mean"][:, 0:1], st["tv"][:, :],
                    ALU.mult, ALU.add).then_inc(s_v.h, 1))
                s_v.bump()
                P.wait("scalar", s_v)
                P.on("scalar", lambda e: e.activation(
                    st["sqv"][:, :], st["vpe"][:, :], AF.Sqrt).then_inc(s_sc.h, 1))
                s_sc.bump()
                P.wait("vector", s_sc)
                P.on("vector", lambda e: e.reciprocal(st["inv"][:, :], st["sqv"][:, :]))
                P.drain("vector")
                P.on("vector", lambda e: e.tensor_scalar(
                    st["mi"][:, :], st["nmean"][:, :], st["inv"][:, 0:1], None, ALU.mult))
                P.drain("vector")
                P.on("vector", lambda e, xb=xb, o=out_tm: e.tensor_scalar(
                    o[:, :], xb[:, :], st["inv"][:, 0:1], st["mi"][:, 0:1],
                    ALU.mult, ALU.add).then_inc(s_v.h, 1))
                s_v.bump()
                sv_hist[key] = s_v.v

            P.wait("scalar", s_sc)
            for h in range(H):
                P.on("scalar", lambda e, h=h: e.activation(
                    DT[:, h * 128:(h + 1) * 128], dsc1[:, :], AF.Exp, scale=ldb[:, h:h + 1]))
            P.drain("scalar")
            P.fence("scalar", s_sc)
            P.wait("gpsimd", s_sc)

            def gp_dtmask(e):
                for h in range(H):
                    e.affine_select(out=DT[:, h * 128:(h + 1) * 128],
                                    in_=DT[:, h * 128:(h + 1) * 128],
                                    compare_op=ALU.is_ge, fill=0.0, base=0,
                                    pattern=[[1, 128]], channel_multiplier=-1)
                e.drain()
            P.on("gpsimd", gp_dtmask)
            P.fence("gpsimd", s_g)

            hist = {}
            pctr = [0]
            vctr = [0]
            wctr2 = [0]
            for sc in range(NAS):
                for t in range(NCHS):
                    ln_tile(sc * AS + t * 128, t % 2, ln_tm[t % 2])
                    P.wait("tensor", s_v)
                    for half in range(2):
                        ptb = ptrb[half]
                        P.wait("tensor", s_sc)
                        for dci in range(8):
                            dc = half * 8 + dci
                            mmfn = (lambda e, ptb=ptb, b=ln_tm[t % 2], dc=dc, dci=dci:
                                    e.transpose(ptb[:, dci * 128:(dci + 1) * 128],
                                                b[:, dc * 128:(dc + 1) * 128], ident[:, :]))
                            if dci < 7:
                                P.on("tensor", mmfn)
                            else:
                                P.on("tensor", lambda e, f=mmfn: f(e).then_inc(t_mm.h, 1))
                                t_mm.bump()
                        P.wait("scalar", t_mm)
                        P.on("scalar", lambda e, ptb=ptb, half=half, t=t: e.copy(
                            ln2_fm3[:, half * 8:(half + 1) * 8, t * 128:(t + 1) * 128],
                            ptb[:, :].rearrange("p (a b) -> p a b", a=8)).then_inc(s_sc.h, 1))
                        s_sc.bump()
                ln2_ready = s_sc.v

                if ("g1r", sc - 1) in hist:
                    P.wait("tensor", s_sc, hist[("g1r", sc - 1)])
                if ("qsp", sc - 1) in hist:
                    P.wait("scalar", gq_o, hist[("qsp", sc - 1)])
                def proj_fm(w_s, out_buf, act, nm):
                    for eb in range(NQB):
                        wb_i = (wctr2[0] % 2)
                        wctr2[0] += 1
                        wqb = wqb2[wb_i]
                        if ("wq", wctr2[0] - 3) in hist:
                            P.wait("sync", t_mm, hist[("wq", wctr2[0] - 3)])
                        else:
                            P.wait("sync", t_mm)
                        P.dma_slot("sync", aqw[wb_i], wqb[:, :], w_s[eb])
                        P.wait("tensor", aqw[wb_i])
                        P.wait("tensor", s_sc, ln2_ready)
                        for e128 in range(4):
                            ec = eb * 4 + e128
                            n = pctr[0]
                            pctr[0] += 1
                            pa = pban[n % 2]
                            if ("proj", n - 2) in hist:
                                P.wait("tensor", s_sc, hist[("proj", n - 2)])
                            for dc in range(16):
                                mm(pa[:, :], wqb[:, dc * QBW + e128 * 128: dc * QBW + (e128 + 1) * 128],
                                   ln2_fm[:, dc * AS:(dc + 1) * AS], dc == 0, dc == 15,
                                   inc=(dc == 15))
                            P.wait("scalar", t_mm)
                            P.on("scalar", lambda e, pa=pa, ec=ec, o=out_buf, act=act: e.activation(
                                o[:, ec * AS:(ec + 1) * AS], pa[:, :], act).then_inc(s_sc.h, 1))
                            s_sc.bump()
                            hist[("proj", n)] = s_sc.v
                        hist[("wq", wctr2[0] - 1)] = t_mm.v
                proj_fm(wq_s, q_fm, AF.Silu, "wq")
                proj_fm(wk_s, k_fm, AF.Silu, "wk")
                q_ready = s_sc.v
                P.wait("gpsimd", s_sc, q_ready)
                P.dma("gpsimd", gq_o, q_sp[sc], q_fm[:, :])
                hist[("qsp", sc)] = gq_o.v

                if ("sgev", sc * H - 1) in hist:
                    P.wait("tensor", s_sc, hist[("sgev", sc * H - 1)])
                for eb in range(NQB):
                    wb_i = (wctr2[0] % 2)
                    wctr2[0] += 1
                    wqb = wqb2[wb_i]
                    if ("wq", wctr2[0] - 3) in hist:
                        P.wait("sync", t_mm, hist[("wq", wctr2[0] - 3)])
                    else:
                        P.wait("sync", t_mm)
                    P.dma_slot("sync", aqw[wb_i], wqb[:, :], wv_s[eb])
                    P.wait("tensor", aqw[wb_i])
                    for t in range(NCHS):
                        nv = vctr[0]
                        vctr[0] += 1
                        pa = pban[2 + (nv % 2)]
                        if ("vp", nv - 2) in hist:
                            P.wait("tensor", s_sc, hist[("vp", nv - 2)])
                        for dc in range(16):
                            mm(pa[:, :], ln2_fm[:, dc * AS + t * 128: dc * AS + (t + 1) * 128],
                               wqb[:, dc * QBW:(dc + 1) * QBW], dc == 0, dc == 15, inc=(dc == 15))
                        P.wait("scalar", t_mm)
                        P.on("scalar", lambda e, pa=pa, t=t, eb=eb: e.activation(
                            v_tm[:, t * D + eb * QBW: t * D + (eb + 1) * QBW], pa[:, :],
                            AF.Silu).then_inc(s_sc.h, 1))
                        s_sc.bump()
                        hist[("vp", nv)] = s_sc.v
                    hist[("wq", wctr2[0] - 1)] = t_mm.v
                v_ready = s_sc.v

                kq_ready = s_sc.v
                for c in range(NCHS):
                    for half in range(2):
                        ptb = ptrb[half]
                        P.wait("tensor", s_sc)
                        for hh in range(8):
                            h = half * 8 + hh
                            f = (lambda e, ptb=ptb, hh=hh, h=h, c=c:
                                 e.transpose(ptb[:, hh * 128:(hh + 1) * 128],
                                             k_fm[:, h * AS + c * 128: h * AS + (c + 1) * 128],
                                             ident[:, :]))
                            if hh < 7:
                                P.on("tensor", f)
                            else:
                                P.on("tensor", lambda e, f=f: f(e).then_inc(t_mm.h, 1))
                                t_mm.bump()
                        P.wait("scalar", t_mm)
                        for hh in range(8):
                            h = half * 8 + hh
                            last = hh == 7
                            fn = (lambda e, ptb=ptb, hh=hh, h=h, c=c: e.activation(
                                ktm[:, (c * H + h) * 128:(c * H + h + 1) * 128],
                                ptb[:, hh * 128:(hh + 1) * 128], AF.Copy,
                                scale=kdt[:, h:h + 1]))
                            if last:
                                P.on("scalar", lambda e, f=fn: f(e).then_inc(s_sc.h, 1))
                                s_sc.bump()
                            else:
                                P.on("scalar", fn)
                ktm_ready = s_sc.v

                P.wait("tensor", s_g)
                P.wait("vector", s_g)
                sitems = [(c, h) for c in range(NCHS) for h in range(H)]
                base = sc * NCHS * H

                def emit_score(idx):
                    c, h = sitems[idx]
                    i = base + idx
                    sl = i % 2
                    if ("scd", i - 2) in hist:
                        P.wait("tensor", s_v, hist[("scd", i - 2)])
                    mm(pban[sl][:, 0:128], k_fm[:, h * AS + c * 128: h * AS + (c + 1) * 128],
                       q_fm[:, h * AS + c * 128: h * AS + (c + 1) * 128], True, True, inc=True)
                    hist[("sc", i)] = t_mm.v

                emit_score(0)
                for idx, (c, h) in enumerate(sitems):
                    i = base + idx
                    sl = i % 2
                    if idx == 0 and ("osp", sc - 1) in hist:
                        P.wait("scalar", gq_o, hist[("osp", sc - 1)])
                    P.wait("vector", t_mm, hist[("sc", i)])
                    if ("omm", i - 2) in hist:
                        P.wait("vector", t_mm, hist[("omm", i - 2)])
                    P.on("vector", lambda e, sl=sl, h=h: e.tensor_mul(
                        scd[sl][:, :], pban[sl][:, 0:128], DT[:, h * 128:(h + 1) * 128]))
                    P.on("vector", lambda e, sl=sl, h=h, c=c: e.tensor_mul(
                        q2b[sl][:, :], q_fm[:, h * AS + c * 128: h * AS + (c + 1) * 128],
                        qdt[:, h * 128:(h + 1) * 128]).then_inc(s_v.h, 1))
                    s_v.bump()
                    hist[("scd", i)] = s_v.v
                    if idx + 1 < len(sitems):
                        emit_score(idx + 1)
                    P.wait("tensor", s_v, hist[("scd", i)])
                    if ("oev", i - 2) in hist:
                        P.wait("tensor", s_sc, hist[("oev", i - 2)])
                    mm(pban[2 + sl][:, 0:128],
                       v_tm[:, c * D + h * 128: c * D + (h + 1) * 128], scd[sl][:, :],
                       True, False)
                    mm(pban[2 + sl][:, 0:128], S_bf[:, h * 128:(h + 1) * 128], q2b[sl][:, :],
                       False, True, inc=True)
                    hist[("omm", i)] = t_mm.v
                    P.wait("scalar", t_mm, hist[("omm", i)])
                    P.on("scalar", lambda e, sl=sl, h=h, c=c: e.activation(
                        o_fm[:, h * AS + c * 128: h * AS + (c + 1) * 128],
                        pban[2 + sl][:, 0:128], AF.Copy).then_inc(s_sc.h, 1))
                    s_sc.bump()
                    hist[("oev", i)] = s_sc.v
                    if ("sup", i - 2) in hist:
                        P.wait("tensor", s_v, hist[("sup", i - 2)])
                    mm(pban[4 + sl][:, 0:128], ktm[:, (c * H + h) * 128:(c * H + h + 1) * 128],
                       v_tm[:, c * D + h * 128: c * D + (h + 1) * 128], True, True, inc=True)
                    hist[("kv", i)] = t_mm.v
                    P.wait("vector", t_mm, hist[("kv", i)])
                    P.on("vector", lambda e, sl=sl, h=h: e.scalar_tensor_tensor(
                        S[:, h * 128:(h + 1) * 128], S[:, h * 128:(h + 1) * 128],
                        cdb[:, h:h + 1], pban[4 + sl][:, 0:128], ALU.mult, ALU.add))
                    P.drain("vector")
                    P.on("vector", lambda e, h=h: e.tensor_copy(
                        S_bf[:, h * 128:(h + 1) * 128],
                        S[:, h * 128:(h + 1) * 128]).then_inc(s_v.h, 1))
                    s_v.bump()
                    hist[("sup", i)] = s_v.v
                if sc == NAS - 1:
                    P.wait("gpsimd", s_v)
                    P.dma("gpsimd", gq_c, kv_b_in.ap(), S[:, :])
                    P.wait("gpsimd", gq_c)
                    P.on("gpsimd", lambda e: e.collective_compute(
                        "AllGather", ALU.bypass, replica_groups=[list(range(NCORE))],
                        ins=[kv_b_in.ap()], outs=[kv_b_out.ap()]).then_inc(s_cc.h, 1))
                    s_cc.bump(1)
                last_i = base + len(sitems) - 1
                P.wait("gpsimd", s_sc, hist[("oev", last_i)])
                P.dma("gpsimd", gq_o, o_sp[sc], o_fm[:, :])
                hist[("osp", sc)] = gq_o.v

                P.wait("tensor", s_v, hist[("scd", last_i)])
                for dc in range(16):
                    mm(pban[0][:, :], wg1_bf[:, dc * 128:(dc + 1) * 128],
                       ln2_fm[:, dc * AS:(dc + 1) * AS], dc == 0, dc == 15, inc=(dc == 15))
                g1v = t_mm.v
                P.wait("scalar", t_mm, g1v)
                P.on("scalar", lambda e: e.copy(g1b[:, :], pban[0][:, :]).then_inc(s_sc.h, 1))
                s_sc.bump()
                hist[("g1r", sc)] = s_sc.v
                P.wait("tensor", s_sc, hist[("g1r", sc)])
                P.wait("tensor", s_sc, hist[("oev", last_i)])
                for h in range(H):
                    shi = (h // 4) % 2
                    pa = pban[2 + (h % 2)]
                    if ("sgev", sc * H + h - 2) in hist:
                        P.wait("tensor", s_sc, hist[("sgev", sc * H + h - 2)])
                    mm(pa[:, :], wg2_bf[:, h * 128:(h + 1) * 128], g1b[:, :],
                       True, True, inc=True)
                    P.wait("scalar", t_mm)
                    if h % 4 == 0 and ("sigsp", sc * H + h - 5) in hist:
                        P.wait("scalar", gq_o, hist[("sigsp", sc * H + h - 5)])
                    P.on("scalar", lambda e, pa=pa, shi=shi, h=h: e.activation(
                        sigh[shi][:, (h % 4) * AS:(h % 4 + 1) * AS], pa[:, :],
                        AF.Sigmoid).then_inc(s_sc.h, 1))
                    s_sc.bump()
                    hist[("sgev", sc * H + h)] = s_sc.v
                    if h % 4 == 3:
                        P.wait("gpsimd", s_sc, hist[("sgev", sc * H + h)])
                        P.dma("gpsimd", gq_o, sig_sp[sc][:, (h - 3) * AS:(h + 1) * AS],
                              sigh[shi][:, :])
                        hist[("sigsp", sc * H + h)] = gq_o.v
            b1.close()

            def b2_reduce():
                P.wait("gpsimd", s_cc)
                P.wait("gpsimd", t_mm)
                b2 = ExitStack()
                kvg = [b2.enter_context(nc.sbuf_tensor(f"kvg{i}", [128, H * HD], F32))
                       for i in range(2)]
                P.dma("gpsimd", gq_c, kvg[0][:, :], kv_b_out.ap()[0:128, :])
                lds = {0: gq_c.v}
                done = {}
                for cp in range(NCORE):
                    if cp + 1 < NCORE:
                        if cp - 1 in done:
                            P.wait("gpsimd", s_v, done[cp - 1])
                        P.dma("gpsimd", gq_c, kvg[(cp + 1) % 2][:, :],
                              kv_b_out.ap()[(cp + 1) * 128:(cp + 2) * 128, :])
                        lds[cp + 1] = gq_c.v
                    P.wait("vector", gq_c, lds[cp])
                    for h in range(H):
                        P.on("vector", lambda e, h=h, cp=cp: e.scalar_tensor_tensor(
                            Bst[:, h * 128:(h + 1) * 128], kvg[cp % 2][:, h * 128:(h + 1) * 128],
                            Wb[:, h * NCORE + cp: h * NCORE + cp + 1],
                            Bst[:, h * 128:(h + 1) * 128], ALU.mult, ALU.add))
                    P.drain("vector")
                    done[cp] = P.fence("vector", s_v)
                b2.close()
                P.on("vector", lambda e: e.tensor_copy(S_bf[:, :], Bst[:, :]).then_inc(s_v.h, 1))
                s_v.bump()
                return s_v.v

            b3 = ExitStack()
            csb = lambda n, s, d: b3.enter_context(nc.sbuf_tensor("b3" + n, s, d))
            cpt = lambda n, s, d: b3.enter_context(nc.psum_tensor("b3" + n, s, d))
            qr = csb("qr", [128, H * AS], BF16)
            obr = csb("obr", [128, H * AS], BF16)
            sigr = csb("sigr", [128, H * AS], BF16)
            qdt4 = csb("qdt4", [128, H * AS], BF16)
            og_all = csb("oga", [128, H * AS], BF16)
            ogn_fm = og_all
            sqh = [csb(f"sq{i}", [128, AS], BF16) for i in range(2)]
            q2bu = [csb(f"q2u{i}", [128, AS], BF16) for i in range(2)]
            rrh_all = csb("rra", [1, H * AS], BF16)
            rrf = [csb(f"rf{i}", [128, AS], BF16) for i in range(2)]
            wob2 = [csb(f"wob{i}", [128, 16 * QBW], BF16) for i in range(2)]
            xadd = [csb("xadd0", [128, 4 * QBW], F32)]
            xtmp = [csb("xtmp0", [128, 4 * QBW], F32)]
            psC = [cpt(f"psC{i}", [128, AS], F32) for i in range(2)]
            psR = [cpt(f"psR{i}", [1, AS], F32) for i in range(2)]
            psB = [cpt(f"psB{i}", [128, AS], F32) for i in range(2)]
            pb3 = [cpt(f"pb3{i}", [128, 512], F32) for i in range(2)]

            hist = {}
            wctr = [0]
            P.wait("gpsimd", gq_o)

            for eb in range(NQB):
                P.wait("sync", gq_o)
                P.dma("sync", aqw[0], wob2[0][:, :], wo_s[eb])
                P.wait("scalar", aqw[0])
                for dc in range(16):
                    fn = (lambda e, dc=dc: e.activation(
                        wob2[0][:, dc * QBW:(dc + 1) * QBW],
                        wob2[0][:, dc * QBW:(dc + 1) * QBW], AF.Copy,
                        scale=nw_pp[:, dc:dc + 1]))
                    if dc == 15:
                        P.on("scalar", lambda e, f=fn: f(e).then_inc(s_sc.h, 1))
                        s_sc.bump()
                    else:
                        P.on("scalar", fn)
                P.wait("gpsimd", s_sc)
                P.dma("gpsimd", gq_o, wo_s[eb], wob2[0][:, :])
            wo_scaled = gq_o.v

            P.dma("gpsimd", gq_x, qr[:, :], q_sp[0])
            P.dma("gpsimd", gq_x, obr[:, :], o_sp[0])
            P.dma("gpsimd", gq_x, sigr[:, :], sig_sp[0])
            rel0_ready = gq_x.v

            b0_ready = b2_reduce()
            for sc in range(NAS):
                if sc == 0:
                    rel_ready = rel0_ready
                else:
                    pgd = hist.get(("rel", sc - 1))
                    if pgd:
                        P.wait("gpsimd", t_mm, pgd[0])
                        P.wait("gpsimd", s_v, pgd[1])
                    P.dma("gpsimd", gq_x, qr[:, :], q_sp[sc])
                    P.dma("gpsimd", gq_x, obr[:, :], o_sp[sc])
                    P.dma("gpsimd", gq_x, sigr[:, :], sig_sp[sc])
                    rel_ready = gq_x.v

                if ("q2", sc * H - 1) in hist:
                    P.wait("scalar", s_g, hist[("q2", sc * H - 1)])
                for h in range(H):
                    for c in range(NCHS):
                        g = sc * NCHS + c
                        last = (h == H - 1 and c == NCHS - 1)
                        fn = (lambda e, h=h, c=c, g=g: e.activation(
                            qdt4[:, h * AS + c * 128: h * AS + (c + 1) * 128],
                            qdt[:, h * 128:(h + 1) * 128], AF.Copy,
                            scale=cdp[:, g * H + h: g * H + h + 1]))
                        if last:
                            P.on("scalar", lambda e, f=fn: f(e).then_inc(s_sc.h, 1))
                            s_sc.bump()
                        else:
                            P.on("scalar", fn)
                qdt4_ready = s_sc.v

                def w1_q2(h):
                    sl = h % 2
                    i = sc * H + h
                    P.wait("gpsimd", gq_x, rel_ready)
                    P.wait("gpsimd", s_sc, qdt4_ready)
                    if ("cmm", i - 2) in hist:
                        P.wait("gpsimd", t_mm, hist[("cmm", i - 2)])
                    P.on("gpsimd", lambda e, sl=sl, h=h: e.tensor_mul(
                        q2bu[sl][:, :], qr[:, h * AS:(h + 1) * AS],
                        qdt4[:, h * AS:(h + 1) * AS]).then_inc(s_g.h, 1))
                    s_g.bump()
                    hist[("q2", i)] = s_g.v

                def w1_corr(h):
                    sl = h % 2
                    i = sc * H + h
                    P.wait("tensor", s_g, hist[("q2", i)])
                    P.wait("tensor", s_v, b0_ready)
                    P.wait("tensor", gq_x, rel_ready)
                    if ("ogm", i - 2) in hist:
                        P.wait("tensor", s_v, hist[("ogm", i - 2)])
                    mm(psC[sl][:, :], S_bf[:, h * 128:(h + 1) * 128], q2bu[sl][:, :],
                       True, False)
                    mm(psC[sl][:, :], ident[:, :], obr[:, h * AS:(h + 1) * AS],
                       False, True, inc=True)
                    hist[("cmm", i)] = t_mm.v

                def w1_og(h):
                    sl = h % 2
                    i = sc * H + h
                    P.wait("vector", t_mm, hist[("cmm", i)])
                    P.wait("vector", gq_x, rel_ready)
                    if h == 0 and ("ogn", sc * H - 1) in hist:
                        P.wait("vector", s_g, hist[("ogn", sc * H - 1)])
                    if h == 0 and ("wolast", sc - 1) in hist:
                        P.wait("vector", t_mm, hist[("wolast", sc - 1)])
                    def fn(e, sl=sl, h=h):
                        with nc.allow_low_precision(reason="gated o as bf16 operand"):
                            e.tensor_mul(og_all[:, h * AS:(h + 1) * AS], psC[sl][:, :],
                                         sigr[:, h * AS:(h + 1) * AS]).then_inc(s_v.h, 1)
                    P.on("vector", fn)
                    s_v.bump()
                    hist[("ogm", i)] = s_v.v

                def w1_sq(h):
                    sl = h % 2
                    i = sc * H + h
                    P.wait("gpsimd", s_v, hist[("ogm", i)])
                    if ("csum", i - 2) in hist:
                        P.wait("gpsimd", t_mm, hist[("csum", i - 2)])
                    P.on("gpsimd", lambda e, sl=sl, h=h: e.tensor_mul(
                        sqh[sl][:, :], og_all[:, h * AS:(h + 1) * AS],
                        og_all[:, h * AS:(h + 1) * AS]).then_inc(s_g.h, 1))
                    s_g.bump()
                    hist[("sqv", i)] = s_g.v
                    P.wait("tensor", s_g, hist[("sqv", i)])
                    if ("rme", i - 2) in hist:
                        P.wait("tensor", s_v, hist[("rme", i - 2)])
                    mm(psR[sl][0:1, :], ones_bf[:, 0:1], sqh[sl][:, :], True, True, inc=True)
                    hist[("csum", i)] = t_mm.v
                    P.wait("vector", t_mm, hist[("csum", i)])
                    if h == 0 and ("rsq", sc * 4 - 1) in hist:
                        P.wait("vector", s_sc, hist[("rsq", sc * 4 - 1)])
                    def fn(e, sl=sl, h=h):
                        with nc.allow_low_precision(reason="variance as bf16 for rsqrt"):
                            e.tensor_scalar(rrh_all[0:1, h * AS:(h + 1) * AS],
                                            psR[sl][0:1, :], 1.0 / HD, EPS,
                                            ALU.mult, ALU.add).then_inc(s_v.h, 1)
                    P.on("vector", fn)
                    s_v.bump()
                    hist[("rme", i)] = s_v.v
                    if h % 4 == 3:
                        grp = h // 4
                        P.wait("scalar", s_v, hist[("rme", i)])
                        P.on("scalar", lambda e, grp=grp: e.activation(
                            rrh_all[0:1, grp * 4 * AS:(grp + 1) * 4 * AS],
                            rrh_all[0:1, grp * 4 * AS:(grp + 1) * 4 * AS], AF.Ln))
                        P.drain("scalar")
                        P.on("scalar", lambda e, grp=grp: e.activation(
                            rrh_all[0:1, grp * 4 * AS:(grp + 1) * 4 * AS],
                            rrh_all[0:1, grp * 4 * AS:(grp + 1) * 4 * AS], AF.Exp,
                            scale=-0.5).then_inc(s_sc.h, 1))
                        s_sc.bump()
                        hist[("rsq", sc * 4 + grp)] = s_sc.v

                for step in range(H + 3):
                    if step < H:
                        w1_q2(step)
                    if 0 <= step - 1 < H:
                        w1_corr(step - 1)
                    if 0 <= step - 2 < H:
                        w1_og(step - 2)
                    if 0 <= step - 3 < H:
                        w1_sq(step - 3)

                def w2_bcast(h):
                    sl = h % 2
                    i = sc * H + h
                    P.wait("tensor", s_sc, hist[("rsq", sc * 4 + h // 4)])
                    if ("rfc", i - 2) in hist:
                        P.wait("tensor", s_sc, hist[("rfc", i - 2)])
                    mm(psB[sl][:, :], ones_bf[0:1, 0:128],
                       rrh_all[0:1, h * AS:(h + 1) * AS], True, True, inc=True)
                    hist[("bmm", i)] = t_mm.v

                def w2_copy(h):
                    sl = h % 2
                    i = sc * H + h
                    P.wait("scalar", t_mm, hist[("bmm", i)])
                    if ("ogn", i - 2) in hist:
                        P.wait("scalar", s_g, hist[("ogn", i - 2)])
                    def fn(e, sl=sl):
                        with nc.allow_low_precision(reason="1/rms factor in bf16"):
                            e.copy(rrf[sl][:, :], psB[sl][:, :]).then_inc(s_sc.h, 1)
                    P.on("scalar", fn)
                    s_sc.bump()
                    hist[("rfc", i)] = s_sc.v

                def w2_ogn(h):
                    sl = h % 2
                    i = sc * H + h
                    P.wait("gpsimd", s_sc, hist[("rfc", i)])
                    P.on("gpsimd", lambda e, sl=sl, h=h: e.tensor_mul(
                        ogn_fm[:, h * AS:(h + 1) * AS], og_all[:, h * AS:(h + 1) * AS],
                        rrf[sl][:, :]).then_inc(s_g.h, 1))
                    s_g.bump()
                    hist[("ogn", i)] = s_g.v

                for step in range(H + 2):
                    if step < H:
                        w2_bcast(step)
                    if 0 <= step - 1 < H:
                        w2_copy(step - 1)
                    if 0 <= step - 2 < H:
                        w2_ogn(step - 2)

                hist[("rel", sc)] = (hist[("cmm", sc * H + H - 1)],
                                     hist[("ogm", sc * H + H - 1)])
                ognT_ready = hist[("ogn", sc * H + H - 1)]

                for eb in range(NQB):
                    wb_i = eb % 2
                    gwo = sc * NQB + eb
                    P.wait("sync", gq_o, wo_scaled)
                    if ("womm", gwo - 2) in hist:
                        P.wait("sync", t_mm, hist[("womm", gwo - 2)])
                    P.dma_slot("sync", aqw[wb_i], wob2[wb_i][:, :], wo_s[eb])
                    abI = 0
                    prev_cons = hist.get(("b3xcons", abI))
                    if prev_cons:
                        P.wait("sync", s_v, prev_cons)
                    P.dma_slot("sync", xas[abI], xadd[abI][:, :].rearrange("p (a b) -> p a b", a=4),
                               src[sc * AS:(sc + 1) * AS,
                                   eb * QBW:(eb + 1) * QBW].rearrange("(t p) w -> p t w", p=128))
                    P.wait("tensor", aqw[wb_i])
                    P.wait("tensor", s_g, ognT_ready)
                    wob = wob2[wb_i]
                    for c in range(NCHS):
                        nw_ = wctr[0]
                        wctr[0] += 1
                        pa = pb3[nw_ % 2]
                        if ("wo", nw_ - 2) in hist:
                            P.wait("tensor", s_v, hist[("wo", nw_ - 2)])
                        for dc in range(16):
                            mm(pa[:, :], ogn_fm[:, dc * AS + c * 128: dc * AS + (c + 1) * 128],
                               wob[:, dc * QBW:(dc + 1) * QBW], dc == 0, dc == 15, inc=(dc == 15))
                        womm = t_mm.v
                        P.wait("vector", t_mm, womm)
                        P.wait("vector", xas[abI])
                        P.wait("vector", xts[abI])
                        P.on("vector", lambda e, pa=pa, abI=abI, c=c: e.tensor_add(
                            xtmp[abI][:, c * QBW:(c + 1) * QBW], pa[:, :],
                            xadd[abI][:, c * QBW:(c + 1) * QBW]).then_inc(s_v.h, 1))
                        s_v.bump()
                        hist[("wo", nw_)] = s_v.v
                    hist[("womm", gwo)] = t_mm.v
                    hist[("b3xcons", abI)] = s_v.v
                    P.wait("sync", s_v, hist[("b3xcons", abI)])
                    P.dma_slot("sync", xts[abI],
                               dst[sc * AS:(sc + 1) * AS,
                                   eb * QBW:(eb + 1) * QBW].rearrange("(t p) w -> p t w", p=128),
                               xtmp[abI][:, :].rearrange("p (a b) -> p a b", a=4))
                hist[("wolast", sc)] = t_mm.v
            b3.close()
            aes.close()

        prologue()
        have_attn = "attn" in stages
        have_f2 = "ffn2" in stages
        if "ffn1" in stages:
            ffn(x_in, x_cur if (have_attn or have_f2) else y_out, join=False)
        conv_issue(len(conv_jobs), qmax=len(conv_jobs))
        if have_attn:
            attn(x_cur if "ffn1" in stages else x_in, x_cur if have_f2 else y_out)
        if have_f2:
            ffn(x_cur if ("ffn1" in stages or have_attn) else x_in, y_out)

        for sem in P.sems:
            P.wait("sync", sem)

        with nc.Block() as block:
            P.emit(block)

    return nc


def kernel(L0_attn_x, L0_attn_log_decay, L0_attn_Wq, L0_attn_Wk, L0_attn_Wv,
           L0_attn_Wo, L0_attn_Wg1, L0_attn_Wg2, L0_attn_norm_w,
           L0_ffn_gate_w, L0_ffn_up_w, L0_ffn_down_w,
           L0_ffn_gate_scale, L0_ffn_up_scale, L0_ffn_down_scale,
           _stages=None):
    if _stages is None:
        _stages = STAGES
    x = np.ascontiguousarray(np.asarray(L0_attn_x, dtype=np.float32).reshape(NCORE * T, D))
    ld = np.asarray(L0_attn_log_decay, dtype=np.float32)
    asc = np.ascontiguousarray

    common = {
        "gw": asc(np.asarray(L0_ffn_gate_w, np.float32)),
        "uw": asc(np.asarray(L0_ffn_up_w, np.float32)),
        "dw": asc(np.asarray(L0_ffn_down_w, np.float32)),
        "wq": asc(np.asarray(L0_attn_Wq, np.float32)),
        "wk": asc(np.asarray(L0_attn_Wk, np.float32)),
        "wv": asc(np.asarray(L0_attn_Wv, np.float32)),
        "wo": asc(np.asarray(L0_attn_Wo, np.float32)),
        "wg1": asc(np.asarray(L0_attn_Wg1, np.float32)),
        "wg2": asc(np.asarray(L0_attn_Wg2, np.float32)),
        "ld_b": asc(np.broadcast_to(ld.reshape(1, H), (128, H))),
        "gs_pp": asc(np.asarray(L0_ffn_gate_scale, np.float32).reshape(NIC, 128).T),
        "us_pp": asc(np.asarray(L0_ffn_up_scale, np.float32).reshape(NIC, 128).T),
        "ds_b": asc(np.broadcast_to(np.asarray(L0_ffn_down_scale, np.float32).reshape(1, D), (128, D))),
        "nw_pp": asc(np.asarray(L0_attn_norm_w, np.float32).reshape(H, 128).T),
    }
    in_maps = []
    for c in range(NCORE):
        pw = np.zeros(NCORE, np.float32)
        mk = np.zeros(NCORE, np.float32)
        for cp in range(c):
            pw[cp] = 2048.0 * (c - 1 - cp)
            mk[cp] = 1.0
        m = dict(common)
        m["x"] = asc(x[c * T:(c + 1) * T, :])
        m["pw_b"] = asc(np.broadcast_to(pw.reshape(1, NCORE), (128, NCORE)))
        m["mk_b"] = asc(np.broadcast_to(mk.reshape(1, NCORE), (128, NCORE)))
        in_maps.append(m)

    nc = build_nc(_stages)
    res = run_bass_kernel_spmd(nc, in_maps, core_ids=list(range(NCORE)))
    out = np.concatenate([res.results[c]["y"] for c in range(NCORE)], axis=0)
    return out.reshape(1, NCORE * T, D).astype(np.float32)
